# version 17
# baseline (speedup 1.0000x reference)
import numpy as np
import ml_dtypes
from contextlib import ExitStack

import concourse.bass as bass
import concourse.mybir as mybir
from concourse.bass_utils import run_bass_kernel_spmd

F32 = mybir.dt.float32
BF16 = mybir.dt.bfloat16
I32 = mybir.dt.int32
AF = mybir.ActivationFunctionType
ALU = mybir.AluOpType
AX = mybir.AxisListType

D = 2048
NCH = 16
P = 128
L = 16384
NCORES = 8
T = 2048
HALO_L = 2048
HALO_R = 1024
E = HALO_L + T + HALO_R
NT = 512
NTE = E // NT
NTO = T // NT
I_FFN = 8192
ICH = 16
SCALE = 128.0 ** -0.5
CH_SHIFTS = [1, 2, 4, 8]
SEQ_SHIFTS = [0, 1, -1, 3, -3, 9, -9, 26, -26, 78, -78, 232, -232, 689, -689, 2048]
NS = 16
LN_EPS = 1e-5
RMS_EPS = 1e-6
TH2 = T // 2

_V_FIELDS = [("bq", NCH), ("bk", NCH), ("bv", NCH), ("bo", NCH), ("qnw", NCH),
             ("knw", NCH), ("dsc", NCH), ("gsc", I_FFN // P), ("usc", I_FFN // P),
             ("mixw", 5)]
VOFF = {}
_o = 0
for _n, _w in _V_FIELDS:
    VOFF[_n] = (_o, _w)
    _o += _w
NV_TOT = _o

ENGINES = ("pe", "act", "dve", "pool", "sp", "gq")
DMA_ENGINES = ("sp", "gq")
PHYS = {"pe": "pe", "act": "act", "dve": "dve", "pool": "gp", "sp": "sp", "gq": "gp"}


class Sched:
    def __init__(self, nc, es):
        self.nc = nc
        self.phys_streams = {"pe": [], "act": [], "dve": [], "sp": [], "gp": []}
        self.counts = {e: 0 for e in ENGINES}
        self.sems = {e: es.enter_context(nc.semaphore(f"sem_{e}")) for e in ENGINES}

    def op(self, eng, fn, deps=(), inc=False):
        tok = None
        if inc:
            self.counts[eng] += 16 if eng in DMA_ENGINES else 1
            tok = (eng, self.counts[eng])
        self.phys_streams[PHYS[eng]].append(
            (eng, fn, tuple(d for d in deps if d is not None), tok))
        return tok

    def dma(self, eng, fn, deps=()):
        return self.op(eng, fn, deps=deps, inc=True)

    def all_tokens(self):
        return [(e, self.counts[e]) for e in ENGINES if self.counts[e] > 0]

    def emit(self, block):
        sems = self.sems

        def make(ops):
            def body(e):
                hwm = {}
                for (src_eng, fn, deps, tok) in ops:
                    for (de, dv) in deps:
                        if hwm.get(de, 0) >= dv:
                            continue
                        e.wait_ge(sems[de], dv)
                        hwm[de] = dv
                    inst = fn(e)
                    if tok is not None:
                        amt = 16 if src_eng in DMA_ENGINES else 1
                        inst.then_inc(sems[src_eng], amt)
            return body

        decos = {"pe": block.tensor, "act": block.scalar, "dve": block.vector,
                 "sp": block.sync, "gp": block.gpsimd}
        for ph in ("sp", "gp", "pe", "act", "dve"):
            if self.phys_streams[ph]:
                decos[ph](make(self.phys_streams[ph]))


def build_program(dbg=False, stop_after=None):
    import os as _os
    CPART = int(_os.environ.get("CPART", "4"))
    nc = bass.Bass("TRN2", target_bir_lowering=False, debug=False)

    xT = nc.dram_tensor("xT", [D, E], F32, kind="ExternalInput").ap()
    wq = nc.dram_tensor("wq", [D, D], F32, kind="ExternalInput").ap()
    wk = nc.dram_tensor("wk", [D, D], F32, kind="ExternalInput").ap()
    wv = nc.dram_tensor("wv", [D, D], F32, kind="ExternalInput").ap()
    wo = nc.dram_tensor("wo", [D, D], F32, kind="ExternalInput").ap()
    gate_w = nc.dram_tensor("gate_w", [I_FFN, D], I32, kind="ExternalInput").ap()
    up_w = nc.dram_tensor("up_w", [I_FFN, D], I32, kind="ExternalInput").ap()
    down_w = nc.dram_tensor("down_w", [D, I_FFN], I32, kind="ExternalInput").ap()
    vecs_in = nc.dram_tensor("vecs", [P, NV_TOT], F32, kind="ExternalInput").ap()

    outT = nc.dram_tensor("outT", [D, T], F32, kind="ExternalOutput").ap()

    wq_bf = nc.dram_tensor("wq_bf", [D, D], BF16, kind="Internal").ap()
    wk_bf = nc.dram_tensor("wk_bf", [D, D], BF16, kind="Internal").ap()
    wv_bf = nc.dram_tensor("wv_bf", [D, D], BF16, kind="Internal").ap()
    wo_bf = nc.dram_tensor("wo_bf", [D, D], BF16, kind="Internal").ap()
    gate_bf = nc.dram_tensor("gate_bf", [I_FFN, D], BF16, kind="Internal").ap()
    up_bf = nc.dram_tensor("up_bf", [I_FFN, D], BF16, kind="Internal").ap()
    down_bf = nc.dram_tensor("down_bf", [D, I_FFN], BF16, kind="Internal").ap()

    spill_kind = "ExternalOutput" if dbg else "Internal"
    acc_dbg = nc.dram_tensor("acc_dbg", [D, T], F32, kind=spill_kind).ap() if dbg else None
    sg_dbg = nc.dram_tensor("sg_dbg", [D, T], F32, kind=spill_kind).ap() if dbg else None
    xn_sp = nc.dram_tensor("xn_sp", [D, E], BF16, kind=spill_kind).ap()
    k_sp = nc.dram_tensor("k_sp", [D, E], BF16, kind=spill_kind).ap()
    v_sp = nc.dram_tensor("v_sp", [D, E], BF16, kind=spill_kind).ap()
    qm_sp = nc.dram_tensor("qm_sp", [D, T], BF16, kind=spill_kind).ap()
    at_sp = nc.dram_tensor("at_sp", [D, T], BF16, kind=spill_kind).ap()
    h_sp = nc.dram_tensor("h_sp", [I_FFN, T], BF16, kind=spill_kind).ap()

    xT_r = xT.rearrange("(c p) j -> p c j", p=P)
    xn_r = xn_sp.rearrange("(c p) j -> p c j", p=P)
    k_r = k_sp.rearrange("(c p) j -> p c j", p=P)
    v_r = v_sp.rearrange("(c p) j -> p c j", p=P)
    qm_r = qm_sp.rearrange("(c p) j -> p c j", p=P)
    at_r = at_sp.rearrange("(c p) j -> p c j", p=P)
    h_r = h_sp.rearrange("(a p) t -> p a t", p=P)
    outT_r = outT.rearrange("(c p) t -> p c t", p=P)

    NSP = NS * P
    cb = np.zeros((P, 128 + 1 + 128 + 256 + 2048 + 16 + 512 + 33), dtype=ml_dtypes.bfloat16)
    off = {}
    o = 0
    def put_b(name, arr):
        global_o = None
        nonlocal o
        h, w = arr.shape
        cb[0:h, o:o + w] = arr
        off[name] = (o, w)
        o += w
    put_b("id128b", np.eye(P, dtype=ml_dtypes.bfloat16))
    put_b("ones_col_b", np.ones((P, 1), ml_dtypes.bfloat16))
    put_b("ones_row_b", np.ones((1, P), ml_dtypes.bfloat16))
    ec_np = np.zeros((P, NS * NS), dtype=ml_dtypes.bfloat16)
    for s in range(NS):
        ec_np[:, s * NS + s] = 1.0
    put_b("ecols", ec_np)
    er_np = np.zeros((NS, NS * P), dtype=ml_dtypes.bfloat16)
    for s in range(NS):
        er_np[s, s * P:(s + 1) * P] = 1.0
    put_b("erows", er_np)
    e4c_np = np.zeros((P, 16), dtype=ml_dtypes.bfloat16)
    for t4 in range(4):
        e4c_np[:, t4 * 4 + t4] = 1.0
    put_b("e4c", e4c_np)
    e4r_np = np.zeros((4, 4 * P), dtype=ml_dtypes.bfloat16)
    for t4 in range(4):
        e4r_np[t4, t4 * P:(t4 + 1) * P] = 1.0
    put_b("e4r", e4r_np)
    c32 = np.zeros((P, 33), dtype=ml_dtypes.bfloat16)
    c32[:, 32] = 1.0
    put_b("col32b", c32)
    consts_b = nc.inline_tensor(cb, "consts_b").ap()

    cf = np.zeros((P, 16 + 1 + 128 + 5 * P), dtype=np.float32)
    off_f = {}
    o = 0
    def put_f(name, arr):
        nonlocal o
        h, w = arr.shape
        cf[0:h, o:o + w] = arr
        off_f[name] = (o, w)
        o += w
    put_f("id16f", np.eye(16, dtype=np.float32))
    put_f("ones_col_f", np.ones((P, 1), np.float32))
    put_f("ones_row_f", np.ones((1, P), np.float32))
    pm_np = np.zeros((P, 5 * P), dtype=np.float32)
    for n, c in enumerate([0] + CH_SHIFTS):
        pm_np[:, n * P:(n + 1) * P] = np.roll(np.eye(P, dtype=np.float32), c, axis=1) * SCALE
    put_f("pmix", pm_np)
    consts_f = nc.inline_tensor(cf, "consts_f").ap()

    es = ExitStack()
    with es:
        sch = Sched(nc, es)
        sb = lambda n_, s_, d_: es.enter_context(nc.sbuf_tensor(n_, s_, d_))

        def _finish():
            sch.op("sp", lambda e: e.nop(), deps=sch.all_tokens())
            block = es.enter_context(nc.Block())
            sch.emit(block)

        cbsb = sb("cbsb", [P, cb.shape[1]], BF16)
        cfsb = sb("cfsb", [P, cf.shape[1]], F32)
        vecs_sb = sb("vecs_sb", [P, NV_TOT], F32)
        mixb_sb = sb("mixb_sb", [P, 5], F32)
        mmix_sb = sb("mmix_sb", [P, P], F32)
        hsumneg_sb = sb("hsumneg_sb", [4, NT], BF16)

        def cslice(name, h):
            o, w = off[name]
            return cbsb[0:h, o:o + w]

        def fslice(name, h):
            o, w = off_f[name]
            return cfsb[0:h, o:o + w]

        id128b_sb = cslice("id128b", P)
        onescb_sb = cslice("ones_col_b", P)
        onesrb_sb = cslice("ones_row_b", 1)
        ecols_sb = cslice("ecols", P)
        erows_sb = cslice("erows", NS)
        e4c_sb = cslice("e4c", P)
        e4r_sb = cslice("e4r", 4)
        col32b_sb = cslice("col32b", P)
        id16f_sb = fslice("id16f", 16)
        onescf_sb = fslice("ones_col_f", P)
        onesrf_sb = fslice("ones_row_f", 1)
        pmix_sb = fslice("pmix", P)

        def vslice(name):
            o, w = VOFF[name]
            return vecs_sb[:, o:o + w]

        bq_sb = vslice("bq"); bk_sb = vslice("bk"); bv_sb = vslice("bv")
        bo_sb = vslice("bo"); qnw_sb = vslice("qnw"); knw_sb = vslice("knw")
        dsc_sb = vslice("dsc"); gsc_sb = vslice("gsc"); usc_sb = vslice("usc")
        mixw_sb = vecs_sb[0:1, VOFF["mixw"][0]:VOFF["mixw"][0] + 5]

        t1 = sch.dma("sp", lambda e: e.dma_start(out=cbsb[:, :], in_=consts_b))
        t2 = sch.dma("sp", lambda e: e.dma_start(out=cfsb[:, :], in_=consts_f))
        t3 = sch.dma("sp", lambda e: e.dma_start(out=vecs_sb[:, :], in_=vecs_in))
        t_consts = t3

        t_wcast = {}
        _wcast_src = {"wk": (wk_bf, wk), "wv": (wv_bf, wv), "wq": (wq_bf, wq),
                      "gate": (gate_bf, gate_w), "up": (up_bf, up_w),
                      "down": (down_bf, down_w), "wo": (wo_bf, wo)}

        def emit_wcast(nm, nchunks=1, pace=()):
            d, s = _wcast_src[nm]
            R = d.shape[0]
            step = R // nchunks
            tok = None
            for i in range(nchunks):
                dep = [pace[i]] if i < len(pace) else []
                tok = sch.dma("gq", lambda e, d=d, s=s, i=i, step=step: e.dma_start(
                    out=d[i * step:(i + 1) * step, :], in_=s[i * step:(i + 1) * step, :]),
                    deps=dep)
            t_wcast[nm] = tok

        psm_es = ExitStack()
        ps_small = psm_es.enter_context(nc.psum_tensor("ps_small", [P, 8], F32))
        t_mixbc = sch.op("pe", lambda e: e.matmul(ps_small[:, 0:5], onesrf_sb[:, :],
                                                  mixw_sb[:, :], start=True, stop=True),
                         deps=[t_consts], inc=True)
        t_mixb = sch.op("dve", lambda e: e.tensor_copy(mixb_sb[:, :], ps_small[:, 0:5]),
                        deps=[t_mixbc], inc=True)
        t_mm = sch.op("dve", lambda e: e.tensor_scalar(mmix_sb[:, :], pmix_sb[:, 0:P],
                                                       mixb_sb[:, 0:1], None, ALU.mult),
                      deps=[t_mixb], inc=True)
        for n in range(1, 5):
            t_mm = sch.op("dve", lambda e, n=n: e.scalar_tensor_tensor(
                mmix_sb[:, :], pmix_sb[:, n * P:(n + 1) * P], mixb_sb[:, n:n + 1],
                mmix_sb[:, :], ALU.mult, ALU.add), deps=[t_mm], inc=True)
        psm_es.close()

        def load_weightT(w_bf_ap, wT_sb, nch_out, start_deps):
            return sch.dma("sp", lambda e: e.dma_start(
                out=wT_sb[:, :].rearrange("p (c w) -> p c w", c=nch_out),
                in_=w_bf_ap, transpose=True), deps=list(start_deps))

        a1_es = ExitStack()
        if True:
            wkT = a1_es.enter_context(nc.sbuf_tensor("wkT", [P, NCH * D], BF16))
            xA = a1_es.enter_context(nc.sbuf_tensor("xA", [P, NCH * NT], F32))
            xB = a1_es.enter_context(nc.sbuf_tensor("xB", [P, NCH * NT], F32))
            xntA = a1_es.enter_context(nc.sbuf_tensor("xntA", [P, NCH * NT], BF16))
            xntB = a1_es.enter_context(nc.sbuf_tensor("xntB", [P, NCH * NT], BF16))
            kbf = a1_es.enter_context(nc.sbuf_tensor("kbf", [P, NCH * NT], BF16))
            sqbf = a1_es.enter_context(nc.sbuf_tensor("sqbf", [P, 4 * NT], BF16))
            tf32 = a1_es.enter_context(nc.sbuf_tensor("tf32", [P, NT], F32))
            rows = a1_es.enter_context(nc.sbuf_tensor("rows", [1, 6 * NT], F32))
            lnst = a1_es.enter_context(nc.psum_tensor("lnst", [P, NT], F32))
            lnsq2 = a1_es.enter_context(nc.psum_tensor("lnsq2", [1, NT], F32))
            bca = a1_es.enter_context(nc.psum_tensor("bca", [P, NT], F32))
            bcb = a1_es.enter_context(nc.psum_tensor("bcb", [P, NT], F32))
            mma = a1_es.enter_context(nc.psum_tensor("mma", [P, NT], F32))
            mmb = a1_es.enter_context(nc.psum_tensor("mmb", [P, NT], F32))
            bck = a1_es.enter_context(nc.psum_tensor("bck", [P, NT], F32))
            ksq = a1_es.enter_context(nc.psum_tensor("ksq", [1, NT], F32))

            t_xpre = []
            for te0 in range(2):
                xb0 = (xA, xB)[te0 % 2]
                t_xpre.append(sch.dma("sp", lambda e, xb=xb0, j0=te0 * NT: e.dma_start(
                    out=xb[:, :].rearrange("p (c w) -> p c w", c=NCH),
                    in_=xT_r[:, :, j0:j0 + NT])))
            emit_wcast("wk")
            emit_wcast("wv")
            emit_wcast("wq")
            t_wk = load_weightT(wk_bf, wkT, NCH, [t_wcast["wk"]])

            xbufs = (xA, xB)
            xnbufs1 = (xntA, xntB)
            t_xload = [None] * NTE
            t_tile_done = [None] * NTE
            t_kn_last = [None] * NTE
            t_spill_xn = [None] * NTE
            t_spill_k = [None] * NTE
            t_bc_consumed = None
            t_mean_prev = None
            t_ex2_prev = None
            t_m2_prev = None
            t_uk_prev = None
            t_ksq_war = {}
            t_sq_war_prev = {}
            ROW_MEAN, ROW_EX2, ROW_VAR, ROW_RSTD, ROW_MR, ROW_U = 0, 1, 2, 3, 4, 5

            def r(i):
                return rows[0:1, i * NT:(i + 1) * NT]

            mm_ps = (mma, mmb)
            for te in range(NTE):
                xb = xbufs[te % 2]
                xnt = xnbufs1[te % 2]
                j0 = te * NT
                if te < 2:
                    t_xload[te] = t_xpre[te]
                else:
                    t_xload[te] = sch.dma("sp", lambda e, xb=xb, j0=j0: e.dma_start(
                        out=xb[:, :].rearrange("p (c w) -> p c w", c=NCH),
                        in_=xT_r[:, :, j0:j0 + NT]),
                        deps=[t_tile_done[te - 2]])
                t_sum = None
                for ci in range(NCH):
                    t_sum = sch.op("pe", lambda e, xb=xb, ci=ci: e.matmul(
                        lnst[0:1, :], onescf_sb[:, :], xb[:, ci * NT:(ci + 1) * NT],
                        start=(ci == 0), stop=(ci == NCH - 1), skip_group_check=True),
                        deps=[t_xload[te]] + ([t_mean_prev, t_ex2_prev] if ci == 0 else []),
                        inc=(ci == NCH - 1))
                t_sq_tok = {}
                t_sumsq_tok = dict(t_sq_war_prev)
                for ci in range(NCH):
                    half = ci % 2
                    t_sq_tok[ci] = sch.op("act", lambda e, xb=xb, ci=ci, half=half: e.activation(
                        sqbf[:, half * NT:(half + 1) * NT], xb[:, ci * NT:(ci + 1) * NT],
                        AF.Square),
                        deps=[t_xload[te], t_sumsq_tok.get(ci - 2)], inc=True)
                    t_sumsq_tok[ci] = sch.op("pe", lambda e, ci=ci, half=half: e.matmul(
                        lnsq2[:, :], onescb_sb[:, :], sqbf[:, half * NT:(half + 1) * NT],
                        start=(ci == 0), stop=(ci == NCH - 1), skip_group_check=True),
                        deps=[t_sq_tok[ci]] + ([t_ex2_prev] if ci == 0 else []), inc=True)
                t_sumsq = t_sumsq_tok[NCH - 1]
                t_mean = sch.op("dve", lambda e: e.tensor_scalar(
                    r(ROW_MEAN), lnst[0:1, :], 1.0 / D, None, ALU.mult),
                    deps=[t_sum, t_sumsq, t_m2_prev], inc=True)
                t_mean_prev = t_mean
                t_ex2 = sch.op("dve", lambda e: e.tensor_scalar(
                    r(ROW_EX2), lnsq2[:, :], 1.0 / D, None, ALU.mult),
                    deps=[t_sumsq], inc=True)
                t_ex2_prev = t_ex2
                t_m2 = sch.op("act", lambda e: e.activation(
                    r(ROW_U), r(ROW_MEAN), AF.Square), deps=[t_mean], inc=True)
                t_m2_prev = t_m2
                t_var = sch.op("dve", lambda e: e.scalar_tensor_tensor(
                    r(ROW_VAR), r(ROW_EX2), LN_EPS, r(ROW_U), ALU.add, ALU.subtract),
                    deps=[t_m2, t_ex2], inc=True)
                t_sd = sch.op("act", lambda e: e.activation(
                    r(ROW_U), r(ROW_VAR), AF.Sqrt), deps=[t_var], inc=True)
                t_rstd = sch.op("dve", lambda e: e.reciprocal(r(ROW_RSTD), r(ROW_U)),
                                deps=[t_sd], inc=True)
                t_mr = sch.op("dve", lambda e: e.tensor_mul(
                    r(ROW_MR), r(ROW_MEAN), r(ROW_RSTD)), deps=[t_rstd], inc=True)
                t_bc_r = sch.op("pe", lambda e: e.matmul(
                    bca[:, :], onesrf_sb[:, :], r(ROW_RSTD), start=True, stop=True),
                    deps=[t_rstd, t_bc_consumed], inc=True)
                t_bc_m = sch.op("pe", lambda e: e.matmul(
                    bcb[:, :], onesrf_sb[:, :], r(ROW_MR), start=True, stop=True),
                    deps=[t_mr, t_bc_consumed], inc=True)
                t_xn = None
                for ci in range(NCH):
                    extra = []
                    if ci == 0 and te > 1:
                        extra = [t_kn_last[te - 2], t_spill_xn[te - 2]]
                    sch.op("dve", lambda e, xb=xb, ci=ci: e.tensor_tensor(
                        tf32[:, :], xb[:, ci * NT:(ci + 1) * NT], bca[:, :], ALU.mult),
                        deps=[t_bc_r] + extra)
                    t_xn = sch.op("dve", lambda e, ci=ci, xnt=xnt: e.tensor_tensor(
                        xnt[:, ci * NT:(ci + 1) * NT], tf32[:, :], bcb[:, :], ALU.subtract),
                        deps=[t_bc_m], inc=True)
                t_tile_done[te] = t_xn
                t_bc_consumed = t_xn
                t_spill_xn[te] = sch.dma("gq", lambda e, j0=j0, xnt=xnt: e.dma_start(
                    out=xn_r[:, :, j0:j0 + NT],
                    in_=xnt[:, :].rearrange("p (c w) -> p c w", c=NCH)), deps=[t_xn])
                t_evac_mi = {}
                t_ksq_tok = {}
                t_sqk_tok = {}
                t_ksum = None
                for mi in range(NCH):
                    ps = mm_ps[mi % 2]
                    t_mmk = None
                    for ci in range(NCH):
                        t_mmk = sch.op("pe", lambda e, ps=ps, mi=mi, ci=ci, xnt=xnt: e.matmul(
                            ps[:, :], wkT[:, ci * D + mi * P: ci * D + mi * P + P],
                            xnt[:, ci * NT:(ci + 1) * NT],
                            start=(ci == 0), stop=(ci == NCH - 1), skip_group_check=True),
                            deps=[t_xn, t_wk, t_evac_mi.get(mi - 2)],
                            inc=(ci == NCH - 1))
                    extra = [t_spill_k[te - 1]] if (mi == 0 and te > 0) else []
                    t_ev = sch.op("dve", lambda e, ps=ps, mi=mi: e.tensor_scalar(
                        kbf[:, mi * NT:(mi + 1) * NT], ps[:, :], bk_sb[:, mi:mi + 1],
                        None, ALU.add), deps=[t_mmk] + extra, inc=True)
                    t_evac_mi[mi] = t_ev
                    half = mi % 4
                    if mi >= 4:
                        war_sq = t_ksq_tok.get(mi - 4)
                    elif mi < 2:
                        war_sq = t_sumsq_tok.get(NCH - 2 + mi)
                    else:
                        war_sq = t_sumsq_tok.get(NCH - 1)
                    t_sqk = sch.op("act", lambda e, mi=mi, half=half: e.activation(
                        sqbf[:, half * NT:(half + 1) * NT], kbf[:, mi * NT:(mi + 1) * NT],
                        AF.Square),
                        deps=[t_ev, war_sq], inc=True)
                    t_sqk_tok[mi] = t_sqk
                    if mi >= 2:
                        dk = mi - 2
                        t_ksq_tok[dk] = sch.op("pe", lambda e, dk=dk: e.matmul(
                            ksq[:, :], onescb_sb[:, :], sqbf[:, (dk % 4) * NT:((dk % 4) + 1) * NT],
                            start=(dk == 0), stop=False, skip_group_check=True),
                            deps=[t_sqk_tok[dk]] + ([t_uk_prev] if dk == 0 else []), inc=True)
                for dk in (NCH - 2, NCH - 1):
                    t_ksq_tok[dk] = sch.op("pe", lambda e, dk=dk: e.matmul(
                        ksq[:, :], onescb_sb[:, :], sqbf[:, (dk % 4) * NT:((dk % 4) + 1) * NT],
                        start=False, stop=(dk == NCH - 1), skip_group_check=True),
                        deps=[t_sqk_tok[dk]], inc=True)
                t_ksum = t_ksq_tok[NCH - 1]
                t_uk = sch.op("dve", lambda e: e.tensor_scalar(
                    r(ROW_U), ksq[:, :], 1.0 / D, RMS_EPS, ALU.mult, ALU.add),
                    deps=[t_ksum], inc=True)
                t_uk_prev = t_uk
                t_sdk = sch.op("act", lambda e: e.activation(
                    r(ROW_U), r(ROW_U), AF.Sqrt), deps=[t_uk], inc=True)
                t_rsk = sch.op("dve", lambda e: e.reciprocal(r(ROW_RSTD), r(ROW_U)),
                               deps=[t_sdk], inc=True)
                t_rows_prev = t_rsk
                t_bck = sch.op("pe", lambda e: e.matmul(
                    bck[:, :], onesrf_sb[:, :], r(ROW_RSTD), start=True, stop=True),
                    deps=[t_rsk, t_kn_last[te - 1] if te > 0 else None], inc=True)
                t_kn = None
                for mi in range(NCH):
                    t_kn = sch.op("dve", lambda e, mi=mi: e.scalar_tensor_tensor(
                        kbf[:, mi * NT:(mi + 1) * NT], kbf[:, mi * NT:(mi + 1) * NT],
                        knw_sb[:, mi:mi + 1], bck[:, :], ALU.mult, ALU.mult),
                        deps=[t_bck], inc=(mi == NCH - 1))
                t_kn_last[te] = t_kn
                t_sq_war_prev = {-2: t_ksq_tok[NCH - 4], -1: t_ksq_tok[NCH - 3]}
                t_spill_k[te] = sch.dma("gq", lambda e, j0=j0: e.dma_start(
                    out=k_r[:, :, j0:j0 + NT],
                    in_=kbf[:, :].rearrange("p (c w) -> p c w", c=NCH)), deps=[t_kn])
            tA1_done = [t_spill_xn[-1], t_spill_k[-1]]
        a1_es.close()
        if stop_after == "A1":
            _finish()
            return nc

        a2_es = ExitStack()
        if True:
            wvT = a2_es.enter_context(nc.sbuf_tensor("wvT", [P, NCH * D], BF16))
            xnA2 = a2_es.enter_context(nc.sbuf_tensor("xnA2", [P, NCH * NT], BF16))
            xnB2 = a2_es.enter_context(nc.sbuf_tensor("xnB2", [P, NCH * NT], BF16))
            vbf = a2_es.enter_context(nc.sbuf_tensor("vbf", [P, NCH * NT], BF16))
            mma2 = a2_es.enter_context(nc.psum_tensor("mma2", [P, NT], F32))
            mmb2 = a2_es.enter_context(nc.psum_tensor("mmb2", [P, NT], F32))
            mmc2 = a2_es.enter_context(nc.psum_tensor("mmc2", [P, NT], F32))
            mmd2 = a2_es.enter_context(nc.psum_tensor("mmd2", [P, NT], F32))

            emit_wcast("gate")
            emit_wcast("up")
            t_wv = load_weightT(wv_bf, wvT, NCH, [t_wcast["wv"]] + tA1_done)
            xnbufs = (xnA2, xnB2)
            t_ld = [None] * NTE
            t_done = [None] * NTE
            t_spill_v = [None] * NTE
            mm_ps2 = (mma2, mmb2, mmc2, mmd2)
            t_ev2 = {}
            for te in range(NTE):
                xb = xnbufs[te % 2]
                j0 = te * NT
                t_ld[te] = sch.dma("sp", lambda e, xb=xb, j0=j0: e.dma_start(
                    out=xb[:, :].rearrange("p (c w) -> p c w", c=NCH),
                    in_=xn_r[:, :, j0:j0 + NT]),
                    deps=([t_done[te - 2]] if te >= 2 else tA1_done))
                t_vn = None
                for mi in range(NCH):
                    ps = mm_ps2[mi % 4]
                    war = t_ev2.get((te, mi - 4)) if mi >= 4 else t_ev2.get((te - 1, NCH - 4 + mi))
                    t_mmv = None
                    for ci in range(NCH):
                        t_mmv = sch.op("pe", lambda e, ps=ps, mi=mi, ci=ci, xb=xb: e.matmul(
                            ps[:, :], wvT[:, ci * D + mi * P: ci * D + mi * P + P],
                            xb[:, ci * NT:(ci + 1) * NT],
                            start=(ci == 0), stop=(ci == NCH - 1), skip_group_check=True),
                            deps=[t_ld[te], t_wv, war], inc=(ci == NCH - 1))
                    extra = [t_spill_v[te - 1]] if (mi == 0 and te > 0) else []
                    t_vn = sch.op("dve", lambda e, ps=ps, mi=mi: e.tensor_scalar(
                        vbf[:, mi * NT:(mi + 1) * NT], ps[:, :], bv_sb[:, mi:mi + 1],
                        None, ALU.add), deps=[t_mmv] + extra, inc=True)
                    t_ev2[(te, mi)] = t_vn
                t_done[te] = t_vn
                t_spill_v[te] = sch.dma("gq", lambda e, j0=j0: e.dma_start(
                    out=v_r[:, :, j0:j0 + NT],
                    in_=vbf[:, :].rearrange("p (c w) -> p c w", c=NCH)), deps=[t_vn])
            tA2_done = [t_spill_v[-1]]
        a2_es.close()
        if stop_after == "A2":
            _finish()
            return nc

        a3_es = ExitStack()
        if True:
            wqT = a3_es.enter_context(nc.sbuf_tensor("wqT", [P, NCH * D], BF16))
            xnA3 = a3_es.enter_context(nc.sbuf_tensor("xnA3", [P, NCH * NT], BF16))
            xnB3 = a3_es.enter_context(nc.sbuf_tensor("xnB3", [P, NCH * NT], BF16))
            qbf = a3_es.enter_context(nc.sbuf_tensor("qbf", [P, NCH * NT], BF16))
            qmx = a3_es.enter_context(nc.sbuf_tensor("qmx", [P, NCH * NT], BF16))
            qnf = a3_es.enter_context(nc.sbuf_tensor("qnf", [P, NT], F32))
            sqbf3 = a3_es.enter_context(nc.sbuf_tensor("sqbf3", [P, 2 * NT], BF16))
            rows3 = a3_es.enter_context(nc.sbuf_tensor("rows3", [1, 2 * NT], F32))
            mma3 = a3_es.enter_context(nc.psum_tensor("mma3", [P, NT], F32))
            mmb3 = a3_es.enter_context(nc.psum_tensor("mmb3", [P, NT], F32))
            mmc3 = a3_es.enter_context(nc.psum_tensor("mmc3", [P, NT], F32))
            bca3 = a3_es.enter_context(nc.psum_tensor("bca3", [P, NT], F32))
            qsq = a3_es.enter_context(nc.psum_tensor("qsq", [1, NT], F32))

            t_wq = load_weightT(wq_bf, wqT, NCH, [t_wcast["wq"]] + tA2_done)
            xnbufs3 = (xnA3, xnB3)
            t_ld3 = [None] * NTO
            t_done3 = [None] * NTO
            t_spill_q = [None] * NTO
            mm_ps3 = (mma3, mmb3, mmc3)
            rq = lambda i: rows3[0:1, i * NT:(i + 1) * NT]
            t_bcq_prev = None
            for to in range(NTO):
                xb = xnbufs3[to % 2]
                j0 = HALO_L + to * NT
                t_ld3[to] = sch.dma("sp", lambda e, xb=xb, j0=j0: e.dma_start(
                    out=xb[:, :].rearrange("p (c w) -> p c w", c=NCH),
                    in_=xn_r[:, :, j0:j0 + NT]),
                    deps=([t_done3[to - 2]] if to >= 2 else tA2_done))
                t_ev3 = {}
                t_qsq_tok = {}
                for mi in range(NCH):
                    ps = mm_ps3[mi % 3]
                    war = t_ev3.get(mi - 3) if mi >= 3 else (t_done3[to - 1] if to > 0 else None)
                    t_mmq = None
                    for ci in range(NCH):
                        t_mmq = sch.op("pe", lambda e, ps=ps, mi=mi, ci=ci, xb=xb: e.matmul(
                            ps[:, :], wqT[:, ci * D + mi * P: ci * D + mi * P + P],
                            xb[:, ci * NT:(ci + 1) * NT],
                            start=(ci == 0), stop=(ci == NCH - 1), skip_group_check=True),
                            deps=[t_ld3[to], t_wq, war], inc=(ci == NCH - 1))
                    extra = [t_spill_q[to - 1]] if (mi == 0 and to > 0) else []
                    t_evq = sch.op("dve", lambda e, ps=ps, mi=mi: e.tensor_scalar(
                        qbf[:, mi * NT:(mi + 1) * NT], ps[:, :], bq_sb[:, mi:mi + 1],
                        None, ALU.add), deps=[t_mmq] + extra, inc=True)
                    t_ev3[mi] = t_evq
                    half = mi % 2
                    t_sqq = sch.op("act", lambda e, mi=mi, half=half: e.activation(
                        sqbf3[:, half * NT:(half + 1) * NT], qbf[:, mi * NT:(mi + 1) * NT],
                        AF.Square), deps=[t_evq, t_qsq_tok.get(mi - 2)], inc=True)
                    t_qsq_tok[mi] = sch.op("pe", lambda e, mi=mi, half=half: e.matmul(
                        qsq[:, :], onescb_sb[:, :], sqbf3[:, half * NT:(half + 1) * NT],
                        start=(mi == 0), stop=(mi == NCH - 1), skip_group_check=True),
                        deps=[t_sqq], inc=True)
                t_uq = sch.op("dve", lambda e: e.tensor_scalar(
                    rq(0), qsq[:, :], 1.0 / D, RMS_EPS, ALU.mult, ALU.add),
                    deps=[t_qsq_tok[NCH - 1]], inc=True)
                t_sdq = sch.op("act", lambda e: e.activation(rq(0), rq(0), AF.Sqrt),
                               deps=[t_uq], inc=True)
                t_rsq = sch.op("dve", lambda e: e.reciprocal(rq(1), rq(0)),
                               deps=[t_sdq], inc=True)
                t_bcq = sch.op("pe", lambda e: e.matmul(
                    bca3[:, :], onesrf_sb[:, :], rq(1), start=True, stop=True),
                    deps=[t_rsq, t_bcq_prev], inc=True)
                t_qm = None
                t_qmm_prev = None
                for mi in range(NCH):
                    ps = mm_ps3[mi % 3]
                    t_qn = sch.op("dve", lambda e, mi=mi: e.scalar_tensor_tensor(
                        qnf[:, :], qbf[:, mi * NT:(mi + 1) * NT], qnw_sb[:, mi:mi + 1],
                        bca3[:, :], ALU.mult, ALU.mult),
                        deps=[t_bcq, t_qmm_prev], inc=True)
                    t_qmm = sch.op("pe", lambda e, ps=ps: e.matmul(
                        ps[:, :], mmix_sb[:, :], qnf[:, :], start=True, stop=True),
                        deps=[t_qn, t_mm], inc=True)
                    t_qmm_prev = t_qmm
                    extra = [t_spill_q[to - 1]] if (mi == 0 and to > 0) else []
                    t_qm = sch.op("dve", lambda e, ps=ps, mi=mi: e.tensor_copy(
                        qmx[:, mi * NT:(mi + 1) * NT], ps[:, :]),
                        deps=[t_qmm] + extra, inc=True)
                t_bcq_prev = t_qm
                t_done3[to] = t_qm
                t_spill_q[to] = sch.dma("gq", lambda e, j0=j0: e.dma_start(
                    out=qm_r[:, :, j0 - HALO_L:j0 - HALO_L + NT],
                    in_=qmx[:, :].rearrange("p (c w) -> p c w", c=NCH)), deps=[t_qm])
            tA3_done = [t_spill_q[-1]]
        a3_es.close()
        if stop_after == "A3":
            _finish()
            return nc

        bc_es = ExitStack()
        if True:
            xno = bc_es.enter_context(nc.sbuf_tensor("xno", [P, NCH * T], BF16))
            wgT = bc_es.enter_context(nc.sbuf_tensor("wgT", [P, NCH * NT], BF16))
            wuT = bc_es.enter_context(nc.sbuf_tensor("wuT", [P, NCH * NT], BF16))
            hbfA = bc_es.enter_context(nc.sbuf_tensor("hbfA", [P, 4 * T], BF16))
            hbfB = bc_es.enter_context(nc.sbuf_tensor("hbfB", [P, 4 * T], BF16))
            csneg = bc_es.enter_context(nc.sbuf_tensor("csneg", [1, T], BF16))
            siluCA = bc_es.enter_context(nc.sbuf_tensor("siluCA", [P, NT], F32))
            siluCB = siluCA
            KE = 4864
            kh1 = bc_es.enter_context(nc.sbuf_tensor("kh1", [P, KE], BF16))
            khs = [kh1, kh1]
            vhs = [bc_es.enter_context(nc.sbuf_tensor(f"vh{k}", [P, KE], BF16)) for k in range(2)]
            qh1 = bc_es.enter_context(nc.sbuf_tensor("qh1", [P, T], BF16))
            qhs = [qh1, qh1]
            acch = bc_es.enter_context(nc.sbuf_tensor("acch", [P, T], F32))
            tmpPA = bc_es.enter_context(nc.sbuf_tensor("tmpPA", [P, NS * NT], BF16))
            tmpPB = tmpPA
            scsb = bc_es.enter_context(nc.sbuf_tensor("scsb", [16, NT], F32))
            ptm = bc_es.enter_context(nc.sbuf_tensor("ptm", [P, 64], BF16))
            pfm = bc_es.enter_context(nc.sbuf_tensor("pfm", [16, NT], BF16))
            nmax = bc_es.enter_context(nc.sbuf_tensor("nmax", [P, 4], F32))
            esum = bc_es.enter_context(nc.sbuf_tensor("esum", [P, 8], F32))
            expv = bc_es.enter_context(nc.sbuf_tensor("expv", [P, 64], F32))
            prodA = bc_es.enter_context(nc.sbuf_tensor("prodA", [P, NT], BF16))
            prodB = bc_es.enter_context(nc.sbuf_tensor("prodB", [P, NT], BF16))

            emit_wcast("down")
            emit_wcast("wo")
            t_xno = sch.dma("sp", lambda e: e.dma_start(
                out=xno[:, :].rearrange("p (c w) -> p c w", c=NCH),
                in_=xn_r[:, :, HALO_L:HALO_L + T]), deps=tA3_done)
            cs_es = ExitStack()
            csps = cs_es.enter_context(nc.psum_tensor("csps", [1, NT], F32))
            t_cs = [None] * NTO
            for to in range(NTO):
                t_c = None
                for ci in range(NCH):
                    t_c = sch.op("pe", lambda e, ci=ci, to=to: e.matmul(
                        csps[:, :], onescb_sb[:, :],
                        xno[:, ci * T + to * NT: ci * T + (to + 1) * NT],
                        start=(ci == 0), stop=(ci == NCH - 1), skip_group_check=True),
                        deps=[t_xno] + ([t_cs[to - 1]] if to > 0 else []),
                        inc=(ci == NCH - 1))
                t_cs[to] = sch.op("dve", lambda e, to=to: e.tensor_scalar(
                    csneg[0:1, to * NT:(to + 1) * NT], csps[:, :], -128.0, None, ALU.mult),
                    deps=[t_c], inc=True)
            t_cs_done = t_cs[-1]
            cs_es.close()

            gpsA = bc_es.enter_context(nc.psum_tensor("gpsA", [P, NT], F32))
            upsA = bc_es.enter_context(nc.psum_tensor("upsA", [P, NT], F32))
            upsB = bc_es.enter_context(nc.psum_tensor("upsB", [P, NT], F32))
            scps = bc_es.enter_context(nc.psum_tensor("scps", [16, NT], F32))
            tmps = bc_es.enter_context(nc.psum_tensor("tmps", [P, 64], F32))
            pfps = bc_es.enter_context(nc.psum_tensor("pfps", [16, NT], BF16))
            pbcA = bc_es.enter_context(nc.psum_tensor("pbcA", [P, NT], F32))
            pbcB = bc_es.enter_context(nc.psum_tensor("pbcB", [P, NT], F32))

            n_ic = ICH
            t_scevac = {}
            t_stack = {}
            t_soft_prev = None
            t_pfevac_prev = None
            t_mult = {}
            t_add = {}
            t_slab = {}
            t_asp = {}
            t_exp_tok = {}
            t_trdone = {}
            t_mm_last = {}
            t_si_tok = {}
            t_hh_tok = {}
            t_h = {}
            t_hspill = [None] * ICH
            t_hsum_last = None

            def emit_b_head(it):
                nonlocal t_soft_prev
                t_sce = sch.op("dve", lambda e: e.tensor_copy(scsb[:, :], scps[:, :]),
                               deps=[t_stack[it]], inc=True)
                t_scevac[it] = t_sce
                t_tr = None
                for i4 in range(4):
                    extra = [t_soft_prev] if i4 == 0 else []
                    t_tr = sch.op("pe", lambda e, i4=i4: e.transpose(
                        tmps[:, i4 * 16:(i4 + 1) * 16], scsb[:, i4 * P:(i4 + 1) * P],
                        id16f_sb[:, :]), deps=[t_sce] + extra, inc=(i4 == 3))
                return t_tr

            t_pr_last = {}

            def emit_b_products(it):
                hh = it // NTO
                to = it % NTO
                tb = (tmpPA, tmpPB)[it % 2]
                kh = khs[hh % 2]
                qh = qhs[hh % 2]
                pj = HALO_L + to * NT
                t_pr = None
                for si, s in enumerate(SEQ_SHIFTS):
                    extra = [t_stack.get(it - 1)] if si == 0 else []
                    t_pr = sch.op("dve",
                                  lambda e, tb=tb, si=si, s=s, pj=pj, to=to, qh=qh, kh=kh:
                                  e.tensor_tensor(
                                      tb[:, si * NT:(si + 1) * NT],
                                      qh[:, to * NT:(to + 1) * NT],
                                      kh[:, pj - s: pj - s + NT], ALU.mult),
                                  deps=[t_slab[hh]] + extra, inc=(si == NS - 1))
                t_pr_last[hh] = t_pr
                return t_pr

            def emit_b_stack(it, t_pr):
                t_st = None
                for si in range(NS):
                    extra = [t_scevac.get(it - 1)] if si == 0 else []
                    t_st = sch.op("pe", lambda e, tb=(tmpPA, tmpPB)[it % 2], si=si: e.matmul(
                        scps[:, :], ecols_sb[:, si * NS:(si + 1) * NS],
                        tb[:, si * NT:(si + 1) * NT],
                        start=(si == 0), stop=(si == NS - 1), skip_group_check=True),
                        deps=[t_pr] + extra, inc=(si == NS - 1))
                t_stack[it] = t_st

            def emit_b_rest(it, t_tr):
                nonlocal t_soft_prev, t_pfevac_prev
                hh = it // NTO
                to = it % NTO
                pj = HALO_L + to * NT
                t_rm = None
                for i4 in range(4):
                    t_rm = sch.op("dve", lambda e, i4=i4: e.reduce_max(
                        nmax[:, i4:i4 + 1], tmps[:, i4 * 16:(i4 + 1) * 16],
                        axis=AX.X, negate=True), deps=[t_tr], inc=(i4 == 3))
                t_ex = None
                for i4 in range(4):
                    t_ex = sch.op("act", lambda e, i4=i4: e.activation(
                        expv[:, i4 * 16:(i4 + 1) * 16], tmps[:, i4 * 16:(i4 + 1) * 16],
                        AF.Exp, bias=nmax[:, i4:i4 + 1]), deps=[t_rm], inc=(i4 == 3))
                t_exp_tok[it] = t_ex
                t_rs = None
                for i4 in range(4):
                    t_rs = sch.op("dve", lambda e, i4=i4: e.reduce_sum(
                        esum[:, i4:i4 + 1], expv[:, i4 * 16:(i4 + 1) * 16],
                        axis=AX.X), deps=[t_ex], inc=(i4 == 3))
                t_rc = sch.op("dve", lambda e: e.reciprocal(esum[:, 4:8], esum[:, 0:4]),
                              deps=[t_rs], inc=True)
                t_pn = None
                for i4 in range(4):
                    t_pn = sch.op("dve", lambda e, i4=i4: e.tensor_scalar(
                        ptm[:, i4 * 16:(i4 + 1) * 16], expv[:, i4 * 16:(i4 + 1) * 16],
                        esum[:, 4 + i4:5 + i4], None, ALU.mult),
                        deps=[t_rc], inc=(i4 == 3))
                t_soft_prev = t_pn
                t_pf = None
                for i4 in range(4):
                    extra = [t_pfevac_prev] if i4 == 0 else []
                    t_pf = sch.op("pe", lambda e, i4=i4: e.transpose(
                        pfps[:, i4 * P:(i4 + 1) * P], ptm[:, i4 * 16:(i4 + 1) * 16],
                        id128b_sb[:, :]), deps=[t_pn] + extra, inc=(i4 == 3))
                t_pfe = sch.op("dve", lambda e: e.tensor_copy(pfm[:, :], pfps[:, :]),
                               deps=[t_pf], inc=True)
                t_pfevac_prev = t_pfe
                for si, s in enumerate(SEQ_SHIFTS):
                    pb = (pbcA, pbcB)[si % 2]
                    pr = (prodA, prodB)[si % 2]
                    war_b = t_mult.get((it, si - 2)) if si >= 2 else t_mult.get((it - 1, NS - 2 + si))
                    t_b = sch.op("pe", lambda e, pb=pb, si=si: e.matmul(
                        pb[:, :], erows_sb[:, si * P:(si + 1) * P], pfm[:, :],
                        start=True, stop=True), deps=[t_pfe, war_b], inc=True)
                    war_p = t_add.get((it, si - 2)) if si >= 2 else t_add.get((it - 1, NS - 2 + si))
                    t_mu = sch.op("dve",
                                  lambda e, pb=pb, pr=pr, pj=pj, s=s, vh=vhs[hh % 2]:
                                  e.tensor_tensor(
                                      pr[:, :], vh[:, pj - s: pj - s + NT],
                                      pb[:, :], ALU.mult),
                                  deps=[t_b, war_p], inc=True)
                    t_mult[(it, si)] = t_mu
                    if si == 0:
                        extra = [t_asp.get(hh - 1)]
                        t_a = sch.op("pool", lambda e, pr=pr, to=to: e.tensor_copy(
                            acch[:, to * NT:(to + 1) * NT], pr[:, :]),
                            deps=[t_mu] + extra, inc=True)
                    else:
                        t_a = sch.op("pool", lambda e, pr=pr, to=to: e.tensor_tensor(
                            acch[:, to * NT:(to + 1) * NT],
                            acch[:, to * NT:(to + 1) * NT],
                            pr[:, :], ALU.add), deps=[t_mu], inc=True)
                    t_add[(it, si)] = t_a

            def emit_c_unit(ic, mi, to):
                idx = ic * 16 + mi * NTO + to
                icol = ic * 4 + mi
                hb = (hbfA, hbfB)[ic % 2]
                gp = gpsA
                up = (upsA, upsB)[idx % 2]
                for ci in range(NCH):
                    sch.op("pe", lambda e, gp=gp, mi=mi, to=to, ci=ci: e.matmul(
                        gp[:, :], wgT[:, ci * NT + mi * P: ci * NT + mi * P + P],
                        xno[:, ci * T + to * NT: ci * T + (to + 1) * NT],
                        start=(ci == 0), stop=False, skip_group_check=True),
                        deps=[t_trdone[(ic, "g")], t_cs_done, t_si_tok.get(idx - 1)]
                        if ci == 0 else [])
                t_g = sch.op("pe", lambda e, gp=gp, to=to: e.matmul(
                    gp[:, :], onesrb_sb[:, :], csneg[0:1, to * NT:(to + 1) * NT],
                    start=False, stop=True, skip_group_check=True), deps=[], inc=True)
                t_mm_last[(ic, "g")] = t_g
                for ci in range(NCH):
                    sch.op("pe", lambda e, up=up, mi=mi, to=to, ci=ci: e.matmul(
                        up[:, :], wuT[:, ci * NT + mi * P: ci * NT + mi * P + P],
                        xno[:, ci * T + to * NT: ci * T + (to + 1) * NT],
                        start=(ci == 0), stop=False, skip_group_check=True),
                        deps=[t_trdone[(ic, "u")], t_hh_tok.get(idx - 2)]
                        if ci == 0 else [])
                t_u = sch.op("pe", lambda e, up=up, to=to: e.matmul(
                    up[:, :], onesrb_sb[:, :], csneg[0:1, to * NT:(to + 1) * NT],
                    start=False, stop=True, skip_group_check=True), deps=[], inc=True)
                t_mm_last[(ic, "u")] = t_u
                sl = siluCA
                t_si = sch.op("act", lambda e, gp=gp, sl=sl, icol=icol: e.activation(
                    sl[:, :], gp[:, :], AF.Silu, scale=gsc_sb[:, icol:icol + 1]),
                    deps=[t_g, t_hh_tok.get(idx - 1)], inc=True)
                t_si_tok[idx] = t_si
                extra = [t_hspill[ic - 2]] if (mi == 0 and to == 0 and ic >= 2) else []
                t_hh = sch.op("dve",
                              lambda e, up=up, sl=sl, hb=hb, mi=mi, to=to, icol=icol:
                              e.scalar_tensor_tensor(
                                  hb[:, mi * T + to * NT: mi * T + (to + 1) * NT],
                                  up[:, :], usc_sb[:, icol:icol + 1], sl[:, :],
                                  ALU.mult, ALU.mult),
                              deps=[t_u, t_si] + extra, inc=True)
                t_hh_tok[idx] = t_hh
                t_h[(ic, mi, to)] = t_hh

            def emit_step_loads(step):
                ic = step
                hh = step
                for (w_bf_ap, wTb, tag) in ((gate_bf, wgT, "g"), (up_bf, wuT, "u")):
                    t_trdone[(ic, tag)] = sch.dma("sp", lambda e, w_bf_ap=w_bf_ap, wTb=wTb,
                                                  ic=ic: e.dma_start(
                        out=wTb[:, :].rearrange("p (c w) -> p c w", c=NCH),
                        in_=w_bf_ap[ic * NT:(ic + 1) * NT, :], transpose=True),
                        deps=[t_wcast["gate" if tag == "g" else "up"],
                              t_mm_last.get((ic - 1, tag))] +
                             ([t_xno] if ic == 0 else []))
                kq_war = [t_pr_last.get(hh - 1)]
                v_war = [t_mult[(4 * (hh - 2) + 3, NS - 1)]] if hh >= 2 else []
                t_k = sch.dma("sp", lambda e, hh=hh, kh=khs[hh % 2]: e.dma_start(
                    out=kh[:, :], in_=k_r[:, hh, 0:KE]),
                    deps=kq_war + (tA3_done if hh == 0 else []))
                t_v = sch.dma("sp", lambda e, hh=hh, vh=vhs[hh % 2]: e.dma_start(
                    out=vh[:, :], in_=v_r[:, hh, 0:KE]), deps=v_war)
                t_q = sch.dma("sp", lambda e, hh=hh, qh=qhs[hh % 2]: e.dma_start(
                    out=qh[:, :], in_=qm_r[:, hh, :]), deps=kq_war)
                t_slab[hh] = t_q

            emit_step_loads(0)
            for step in range(16):
                ic = step
                hh = step
                b2 = ic % 2
                hb = (hbfA, hbfB)[b2]
                its = [4 * hh + j for j in range(4)]
                cu = [(mi, to) for mi in range(4) for to in range(NTO)]
                t_pr_pend = {}
                t_tr_pend = {}
                for j4 in range(4):
                    itj = its[j4]
                    emit_c_unit(ic, *cu[4 * j4 + 0])
                    emit_c_unit(ic, *cu[4 * j4 + 1])
                    if j4 > 0:
                        t_tr_pend[its[j4 - 1]] = emit_b_head(its[j4 - 1])
                    t_pr_pend[itj] = emit_b_products(itj)
                    emit_c_unit(ic, *cu[4 * j4 + 2])
                    emit_c_unit(ic, *cu[4 * j4 + 3])
                    emit_b_stack(itj, t_pr_pend[itj])
                    if j4 > 0:
                        emit_b_rest(its[j4 - 1], t_tr_pend[its[j4 - 1]])
                t_tr_pend[its[3]] = emit_b_head(its[3])
                emit_b_rest(its[3], t_tr_pend[its[3]])
                if step < 15:
                    emit_step_loads(step + 1)
                t_hspill[ic] = sch.dma("sp", lambda e, hb=hb, ic=ic: e.dma_start(
                    out=h_r[:, ic * 4:(ic + 1) * 4, :],
                    in_=hb[:, :].rearrange("p (c w) -> p c w", c=4)),
                    deps=[t_h[(ic, 3, NTO - 1)]])
                t_asp[hh] = sch.dma("gq", lambda e, hh=hh: e.dma_start(
                    out=at_r[:, hh, :], in_=acch[:, :]),
                    deps=[t_add[(its[3], NS - 1)]])
            tC_done = [t_hspill[-1], t_hspill[-2], t_asp[15]]
        bc_es.close()
        if stop_after is not None and (stop_after.startswith("C") or stop_after == "B"):
            _finish()
            return nc

        t_out_stores = []
        t_prev_half = list(tC_done)
        t_hsn_war = None
        for H in range(2):
            th0 = H * TH2
            acc_es = ExitStack()
            acc = acc_es.enter_context(nc.sbuf_tensor(f"accD{H}", [P, NCH * TH2], F32))
            d_es = ExitStack()
            wdTs = [d_es.enter_context(nc.sbuf_tensor(f"wdT{H}_{k}", [P, 4 * D], BF16))
                    for k in range(4)]
            hDs = [d_es.enter_context(nc.sbuf_tensor(f"hD{H}_{k}", [P, 4 * TH2], BF16))
                   for k in range(4)]
            opsA = d_es.enter_context(nc.psum_tensor(f"opsA{H}", [P, NT], F32))
            opsB = d_es.enter_context(nc.psum_tensor(f"opsB{H}", [P, NT], F32))
            opsC = d_es.enter_context(nc.psum_tensor(f"opsC{H}", [P, NT], F32))
            opsD = d_es.enter_context(nc.psum_tensor(f"opsD{H}", [P, NT], F32))
            hsumD = d_es.enter_context(nc.psum_tensor(f"hsumD{H}", [4, NT], F32))

            t_trD = {}
            t_hld = {}
            t_evD = {}
            NPAIR = 8
            for p8 in range(NPAIR):
                for k2 in range(2):
                    ic = p8 * 2 + k2
                    b4 = (p8 % 2) * 2 + k2
                    war = [t_evD.get((p8 - 2, NCH - 1, 1))] + (t_prev_half if p8 < 2 else [])
                    t_trD[ic] = sch.dma("sp", lambda e, ic=ic, b4=b4, wdTs=wdTs: e.dma_start(
                        out=wdTs[b4][:, :].rearrange("p (c w) -> p c w", c=4),
                        in_=down_bf[:, ic * NT:(ic + 1) * NT], transpose=True),
                        deps=[t_wcast["down"]] + war)
                    t_hld[ic] = sch.dma("sp", lambda e, ic=ic, b4=b4, hDs=hDs, th0=th0: e.dma_start(
                        out=hDs[b4][:, :].rearrange("p (c w) -> p c w", c=4),
                        in_=h_r[:, ic * 4:(ic + 1) * 4, th0:th0 + TH2]),
                        deps=war)
                    for r4 in range(4):
                        for th in range(2):
                            first = (ic == 0 and r4 == 0 and th == 0)
                            last = (ic == ICH - 1 and r4 == 3 and th == 1)
                            tg = 2 * H + th
                            t_hsum_lastD = sch.op("pe",
                                                  lambda e, b4=b4, r4=r4, th=th, tg=tg,
                                                  first=first, last=last, hDs=hDs, hsumD=hsumD:
                                                  e.matmul(hsumD[:, :],
                                                           e4c_sb[:, tg * 4:(tg + 1) * 4],
                                                           hDs[b4][:, r4 * TH2 + th * NT: r4 * TH2 + (th + 1) * NT],
                                                           start=first, stop=last,
                                                           skip_group_check=True),
                                                  deps=[t_hld[ic]], inc=last)
                            if last:
                                t_hsnD = sch.op("dve", lambda e, hsumD=hsumD: e.tensor_scalar(
                                    hsumneg_sb[:, :], hsumD[:, :], -128.0, None, ALU.mult),
                                    deps=[t_hsum_lastD, t_hsn_war], inc=True)
                for mi in range(NCH):
                    for th in range(2):
                        tg = 2 * H + th
                        par = (mi * 2 + th) % 4
                        ops = (opsA, opsB, opsC, opsD)[par]
                        war = t_evD.get((p8, mi - 2, th)) if mi >= 2 else \
                            t_evD.get((p8 - 1, NCH - 2 + mi, th))
                        t_mmlast = None
                        cnt = 0
                        nmm = 8
                        for k2 in range(2):
                            ic = p8 * 2 + k2
                            b4 = (p8 % 2) * 2 + k2
                            for r4 in range(4):
                                first = (cnt == 0)
                                is_last_real = (cnt == nmm - 1)
                                stop = is_last_real and (p8 != NPAIR - 1)
                                deps = [t_trD[ic], t_hld[ic]]
                                if first:
                                    deps.append(war)
                                t_mmlast = sch.op("pe",
                                                  lambda e, ops=ops, b4=b4, r4=r4, mi=mi, th=th,
                                                  first=first, stop=stop, wdTs=wdTs, hDs=hDs:
                                                  e.matmul(
                                                      ops[:, :],
                                                      wdTs[b4][:, r4 * D + mi * P: r4 * D + mi * P + P],
                                                      hDs[b4][:, r4 * TH2 + th * NT: r4 * TH2 + (th + 1) * NT],
                                                      start=first, stop=stop,
                                                      skip_group_check=True),
                                                  deps=deps,
                                                  inc=(is_last_real and p8 != NPAIR - 1))
                                cnt += 1
                        if p8 == NPAIR - 1:
                            t_mmlast = sch.op("pe", lambda e, ops=ops, tg=tg: e.matmul(
                                ops[:, :], e4r_sb[:, tg * P:(tg + 1) * P],
                                hsumneg_sb[0:4, 0:NT],
                                start=False, stop=True, skip_group_check=True),
                                deps=[t_hsnD], inc=True)
                        if p8 == 0:
                            t_e = sch.op("dve", lambda e, ops=ops, mi=mi, th=th, acc=acc: e.tensor_copy(
                                acc[:, mi * TH2 + th * NT: mi * TH2 + (th + 1) * NT],
                                ops[:, :]), deps=[t_mmlast], inc=True)
                        else:
                            t_e = sch.op("dve", lambda e, ops=ops, mi=mi, th=th, acc=acc: e.tensor_tensor(
                                acc[:, mi * TH2 + th * NT: mi * TH2 + (th + 1) * NT],
                                acc[:, mi * TH2 + th * NT: mi * TH2 + (th + 1) * NT],
                                ops[:, :], ALU.add), deps=[t_mmlast], inc=True)
                        t_evD[(p8, mi, th)] = t_e
            t_D_done = t_evD[(NPAIR - 1, NCH - 1, 1)]
            t_hsn_war = t_D_done
            if dbg:
                t_D_done = sch.dma("sp", lambda e, acc=acc, th0=th0: e.dma_start(
                    out=acc_dbg.rearrange("(c p) t -> p c t", p=P)[:, :, th0:th0 + TH2],
                    in_=acc[:, :].rearrange("p (c w) -> p c w", c=NCH)),
                    deps=[t_evD[(NPAIR - 1, mi, th)] for mi in range(NCH) for th in range(2)])
            d_es.close()

            e_es = ExitStack()
            attnE = e_es.enter_context(nc.sbuf_tensor(f"attnE{H}", [P, NCH * TH2], BF16))
            woT_bufs = [e_es.enter_context(nc.sbuf_tensor(f"woT{H}_{k}", [P, D], BF16))
                        for k in range(4)]
            xE = [e_es.enter_context(nc.sbuf_tensor(f"xE{H}_{k}", [P, NCH * NT], F32))
                  for k in range(2)]
            siluE = [e_es.enter_context(nc.sbuf_tensor(f"siluE{H}_{k}", [P, NT], F32))
                     for k in range(4)]
            sgE = [e_es.enter_context(nc.sbuf_tensor(f"sgE{H}_{k}", [P, NT], F32))
                   for k in range(2)]
            soPS = [e_es.enter_context(nc.psum_tensor(f"so{H}_{k}", [P, NT], F32))
                    for k in range(6)]

            t_attn_ld = sch.dma("sp", lambda e, attnE=attnE, th0=th0: e.dma_start(
                out=attnE[:, :].rearrange("p (c w) -> p c w", c=NCH),
                in_=at_r[:, :, th0:th0 + TH2]), deps=[t_D_done])
            t_x_ld = [None, None]
            for th in range(2):
                t_x_ld[th] = sch.dma("sp", lambda e, th=th, xE=xE, th0=th0: e.dma_start(
                    out=xE[th][:, :].rearrange("p (c w) -> p c w", c=NCH),
                    in_=xT_r[:, :, HALO_L + th0 + th * NT: HALO_L + th0 + (th + 1) * NT]),
                    deps=[t_D_done])
            t_woT = {}
            t_sg_tok = {}
            t_outdone = {}
            for rc in range(NCH):
                wb = woT_bufs[rc % 4]
                t_woT[rc] = sch.dma("sp", lambda e, rc=rc, wb=wb: e.dma_start(
                    out=wb[:, :].rearrange("p (c w) -> p c w", c=NCH),
                    in_=wo_bf[rc * P:(rc + 1) * P, :], transpose=True),
                    deps=[t_wcast["wo"], t_sg_tok.get((rc - 4, 1))] +
                         ([t_D_done] if rc == 0 else []))
                for th in range(2):
                    par = (rc * 2 + th) % 6
                    so = soPS[par]
                    war = t_sg_tok.get((rc - 3, th))
                    t_so = None
                    for ci in range(NCH):
                        deps = [t_woT[rc], t_attn_ld]
                        if ci == 0:
                            deps.append(war)
                        t_so = sch.op("pe", lambda e, so=so, wb=wb, ci=ci, th=th, attnE=attnE: e.matmul(
                            so[:, :], wb[:, ci * P:(ci + 1) * P],
                            attnE[:, ci * TH2 + th * NT: ci * TH2 + (th + 1) * NT],
                            start=(ci == 0), stop=(ci == NCH - 1), skip_group_check=True),
                            deps=deps, inc=(ci == NCH - 1))
                    sil = siluE[(rc * 2 + th) % 4]
                    t_si = sch.op("act", lambda e, rc=rc, th=th, sil=sil, acc=acc: e.activation(
                        sil[:, :],
                        acc[:, rc * TH2 + th * NT: rc * TH2 + (th + 1) * NT],
                        AF.Silu, scale=dsc_sb[:, rc:rc + 1]),
                        deps=[t_evD[(NPAIR - 1, rc, th)], t_sg_tok.get((rc - 2, th))], inc=True)
                    t_sg = sch.op("dve", lambda e, so=so, rc=rc, th=th, sgE=sgE, sil=sil: e.scalar_tensor_tensor(
                        sgE[th][:, :], so[:, :], bo_sb[:, rc:rc + 1], sil[:, :],
                        ALU.add, ALU.mult), deps=[t_so, t_si], inc=True)
                    t_sg_tok[(rc, th)] = t_sg
                    if dbg:
                        t_sgd = sch.dma("sp", lambda e, rc=rc, th=th, sgE=sgE, th0=th0: e.dma_start(
                            out=sg_dbg.rearrange("(c p) t -> p c t", p=P)[
                                :, rc, th0 + th * NT: th0 + (th + 1) * NT],
                            in_=sgE[th][:, :]),
                            deps=[t_sg])
                        t_sg = t_sgd if False else t_sg
                        t_sg_tok[(rc, th)] = (t_sg_tok[(rc, th)], t_sgd)[0]
                    t_o = sch.op("dve", lambda e, rc=rc, th=th, xE=xE, sgE=sgE: e.tensor_tensor(
                        xE[th][:, rc * NT:(rc + 1) * NT], xE[th][:, rc * NT:(rc + 1) * NT],
                        sgE[th][:, :], ALU.add),
                        deps=[t_sg, t_x_ld[th]] + ([t_sgd] if dbg else []), inc=True)
                    t_outdone[(rc, th)] = t_o
            for th in range(2):
                t_st = sch.dma("sp", lambda e, th=th, xE=xE, th0=th0: e.dma_start(
                    out=outT_r[:, :, th0 + th * NT: th0 + (th + 1) * NT],
                    in_=xE[th][:, :].rearrange("p (c w) -> p c w", c=NCH)),
                    deps=[t_outdone[(NCH - 1, th)]])
                t_out_stores.append(t_st)
            e_es.close()
            acc_es.close()
            t_prev_half = [t_out_stores[-1], t_out_stores[-2]]

        sch.op("sp", lambda e: e.nop(), deps=t_out_stores + sch.all_tokens())

        block = es.enter_context(nc.Block())
        sch.emit(block)
    return nc


def kernel(L0_attn_x, L0_attn_wq, L0_attn_bq, L0_attn_wk, L0_attn_bk, L0_attn_wv,
           L0_attn_bv, L0_attn_q_norm_w, L0_attn_k_norm_w, L0_attn_score_mix_w,
           L0_attn_wo, L0_attn_bo, L0_ffn_gate_w, L0_ffn_gate_scale, L0_ffn_up_w,
           L0_ffn_up_scale, L0_ffn_down_w, L0_ffn_down_scale):
    x = np.asarray(L0_attn_x)[0]
    fvec = lambda v, nch: np.ascontiguousarray(np.asarray(v, np.float32).reshape(nch, P).T)
    vecs = np.zeros((P, NV_TOT), np.float32)
    fields = {"bq": fvec(L0_attn_bq, NCH), "bk": fvec(L0_attn_bk, NCH),
              "bv": fvec(L0_attn_bv, NCH), "bo": fvec(L0_attn_bo, NCH),
              "qnw": fvec(L0_attn_q_norm_w, NCH), "knw": fvec(L0_attn_k_norm_w, NCH),
              "dsc": fvec(L0_ffn_down_scale, NCH),
              "gsc": fvec(L0_ffn_gate_scale, I_FFN // P),
              "usc": fvec(L0_ffn_up_scale, I_FFN // P)}
    for nm, arr in fields.items():
        o, w = VOFF[nm]
        vecs[:, o:o + w] = arr
    o, w = VOFF["mixw"]
    vecs[0:1, o:o + 5] = np.asarray(L0_attn_score_mix_w, np.float32)
    shared = {
        "wq": np.ascontiguousarray(L0_attn_wq, np.float32),
        "wk": np.ascontiguousarray(L0_attn_wk, np.float32),
        "wv": np.ascontiguousarray(L0_attn_wv, np.float32),
        "wo": np.ascontiguousarray(L0_attn_wo, np.float32),
        "gate_w": np.ascontiguousarray(L0_ffn_gate_w, np.int32),
        "up_w": np.ascontiguousarray(L0_ffn_up_w, np.int32),
        "down_w": np.ascontiguousarray(L0_ffn_down_w, np.int32),
        "vecs": vecs,
    }
    in_maps = []
    for c in range(NCORES):
        t0 = c * T
        idx = (np.arange(-HALO_L, T + HALO_R) + t0) % L
        m = dict(shared)
        m["xT"] = np.ascontiguousarray(x[idx].T)
        in_maps.append(m)

    nc = build_program()
    res = run_bass_kernel_spmd(nc, in_maps, core_ids=list(range(NCORES)))
    out = np.empty((1, L, D), dtype=np.float32)
    for c in range(NCORES):
        out[0, c * T:(c + 1) * T, :] = np.asarray(res.results[c]["outT"]).T
    return out


# version 19
# speedup vs baseline: 1.1716x; 1.1716x over previous
import numpy as np
import ml_dtypes
from contextlib import ExitStack

import concourse.bass as bass
import concourse.mybir as mybir
from concourse.bass_utils import run_bass_kernel_spmd

F32 = mybir.dt.float32
BF16 = mybir.dt.bfloat16
I32 = mybir.dt.int32
AF = mybir.ActivationFunctionType
ALU = mybir.AluOpType
AX = mybir.AxisListType

D = 2048
NCH = 16
P = 128
L = 16384
NCORES = 8
T = 2048
HALO_L = 2048
HALO_R = 1024
E = HALO_L + T + HALO_R
NT = 512
NTE = E // NT
NTO = T // NT
I_FFN = 8192
ICH = 16
SCALE = 128.0 ** -0.5
CH_SHIFTS = [1, 2, 4, 8]
SEQ_SHIFTS = [0, 1, -1, 3, -3, 9, -9, 26, -26, 78, -78, 232, -232, 689, -689, 2048]
NS = 16
LN_EPS = 1e-5
RMS_EPS = 1e-6
TH2 = T // 2

_V_FIELDS = [("bq", NCH), ("bk", NCH), ("bv", NCH), ("bo", NCH), ("qnw", NCH),
             ("knw", NCH), ("dsc", NCH), ("gsc", I_FFN // P), ("usc", I_FFN // P),
             ("mixw", 5)]
VOFF = {}
_o = 0
for _n, _w in _V_FIELDS:
    VOFF[_n] = (_o, _w)
    _o += _w
NV_TOT = _o

ENGINES = ("pe", "act", "dve", "pool", "sp", "gq")
DMA_ENGINES = ("sp", "gq")
PHYS = {"pe": "pe", "act": "act", "dve": "dve", "pool": "gp", "sp": "sp", "gq": "gp"}


class Sched:
    def __init__(self, nc, es):
        self.nc = nc
        self.phys_streams = {"pe": [], "act": [], "dve": [], "sp": [], "gp": []}
        self.counts = {e: 0 for e in ENGINES}
        self.sems = {e: es.enter_context(nc.semaphore(f"sem_{e}")) for e in ENGINES}

    def op(self, eng, fn, deps=(), inc=False):
        tok = None
        if inc:
            self.counts[eng] += 16 if eng in DMA_ENGINES else 1
            tok = (eng, self.counts[eng])
        self.phys_streams[PHYS[eng]].append(
            (eng, fn, tuple(d for d in deps if d is not None), tok))
        return tok

    def dma(self, eng, fn, deps=()):
        return self.op(eng, fn, deps=deps, inc=True)

    def all_tokens(self):
        return [(e, self.counts[e]) for e in ENGINES if self.counts[e] > 0]

    def emit(self, block):
        sems = self.sems

        def make(ops):
            def body(e):
                hwm = {}
                for (src_eng, fn, deps, tok) in ops:
                    for (de, dv) in deps:
                        if hwm.get(de, 0) >= dv:
                            continue
                        e.wait_ge(sems[de], dv)
                        hwm[de] = dv
                    inst = fn(e)
                    if tok is not None:
                        amt = 16 if src_eng in DMA_ENGINES else 1
                        inst.then_inc(sems[src_eng], amt)
            return body

        decos = {"pe": block.tensor, "act": block.scalar, "dve": block.vector,
                 "sp": block.sync, "gp": block.gpsimd}
        for ph in ("sp", "gp", "pe", "act", "dve"):
            if self.phys_streams[ph]:
                decos[ph](make(self.phys_streams[ph]))


def build_program(dbg=False, stop_after=None):
    import os as _os
    CPART = int(_os.environ.get("CPART", "4"))
    nc = bass.Bass("TRN2", target_bir_lowering=False, debug=False)

    xT = nc.dram_tensor("xT", [D, E], F32, kind="ExternalInput").ap()
    wq = nc.dram_tensor("wq", [D, D], F32, kind="ExternalInput").ap()
    wk = nc.dram_tensor("wk", [D, D], F32, kind="ExternalInput").ap()
    wv = nc.dram_tensor("wv", [D, D], F32, kind="ExternalInput").ap()
    wo = nc.dram_tensor("wo", [D, D], F32, kind="ExternalInput").ap()
    gate_w = nc.dram_tensor("gate_w", [I_FFN, D], I32, kind="ExternalInput").ap()
    up_w = nc.dram_tensor("up_w", [I_FFN, D], I32, kind="ExternalInput").ap()
    down_w = nc.dram_tensor("down_w", [D, I_FFN], I32, kind="ExternalInput").ap()
    vecs_in = nc.dram_tensor("vecs", [P, NV_TOT], F32, kind="ExternalInput").ap()

    outT = nc.dram_tensor("outT", [D, T], F32, kind="ExternalOutput").ap()

    wq_bf = nc.dram_tensor("wq_bf", [D, D], BF16, kind="Internal").ap()
    wk_bf = nc.dram_tensor("wk_bf", [D, D], BF16, kind="Internal").ap()
    wv_bf = nc.dram_tensor("wv_bf", [D, D], BF16, kind="Internal").ap()
    wo_bf = nc.dram_tensor("wo_bf", [D, D], BF16, kind="Internal").ap()
    gate_bf = nc.dram_tensor("gate_bf", [I_FFN, D], BF16, kind="Internal").ap()
    up_bf = nc.dram_tensor("up_bf", [I_FFN, D], BF16, kind="Internal").ap()
    down_bf = nc.dram_tensor("down_bf", [D, I_FFN], BF16, kind="Internal").ap()

    spill_kind = "ExternalOutput" if dbg else "Internal"
    acc_dbg = nc.dram_tensor("acc_dbg", [D, T], F32, kind=spill_kind).ap() if dbg else None
    sg_dbg = nc.dram_tensor("sg_dbg", [D, T], F32, kind=spill_kind).ap() if dbg else None
    xn_sp = nc.dram_tensor("xn_sp", [D, E], BF16, kind=spill_kind).ap()
    k_sp = nc.dram_tensor("k_sp", [D, E], BF16, kind=spill_kind).ap()
    v_sp = nc.dram_tensor("v_sp", [D, E], BF16, kind=spill_kind).ap()
    qm_sp = nc.dram_tensor("qm_sp", [D, T], BF16, kind=spill_kind).ap()
    at_sp = nc.dram_tensor("at_sp", [D, T], BF16, kind=spill_kind).ap()
    h_sp = nc.dram_tensor("h_sp", [I_FFN, T], BF16, kind=spill_kind).ap()

    xT_r = xT.rearrange("(c p) j -> p c j", p=P)
    xn_r = xn_sp.rearrange("(c p) j -> p c j", p=P)
    k_r = k_sp.rearrange("(c p) j -> p c j", p=P)
    v_r = v_sp.rearrange("(c p) j -> p c j", p=P)
    qm_r = qm_sp.rearrange("(c p) j -> p c j", p=P)
    at_r = at_sp.rearrange("(c p) j -> p c j", p=P)
    h_r = h_sp.rearrange("(a p) t -> p a t", p=P)
    outT_r = outT.rearrange("(c p) t -> p c t", p=P)

    NSP = NS * P
    cb = np.zeros((P, 128 + 1 + 128 + 256 + 2048 + 16 + 512 + 33), dtype=ml_dtypes.bfloat16)
    off = {}
    o = 0
    def put_b(name, arr):
        global_o = None
        nonlocal o
        h, w = arr.shape
        cb[0:h, o:o + w] = arr
        off[name] = (o, w)
        o += w
    put_b("id128b", np.eye(P, dtype=ml_dtypes.bfloat16))
    put_b("ones_col_b", np.ones((P, 1), ml_dtypes.bfloat16))
    put_b("ones_row_b", np.ones((1, P), ml_dtypes.bfloat16))
    ec_np = np.zeros((P, NS * NS), dtype=ml_dtypes.bfloat16)
    for s in range(NS):
        ec_np[:, s * NS + s] = 1.0
    put_b("ecols", ec_np)
    er_np = np.zeros((NS, NS * P), dtype=ml_dtypes.bfloat16)
    for s in range(NS):
        er_np[s, s * P:(s + 1) * P] = 1.0
    put_b("erows", er_np)
    e4c_np = np.zeros((P, 16), dtype=ml_dtypes.bfloat16)
    for t4 in range(4):
        e4c_np[:, t4 * 4 + t4] = 1.0
    put_b("e4c", e4c_np)
    e4r_np = np.zeros((4, 4 * P), dtype=ml_dtypes.bfloat16)
    for t4 in range(4):
        e4r_np[t4, t4 * P:(t4 + 1) * P] = 1.0
    put_b("e4r", e4r_np)
    c32 = np.zeros((P, 33), dtype=ml_dtypes.bfloat16)
    c32[:, 32] = 1.0
    put_b("col32b", c32)
    consts_b = nc.inline_tensor(cb, "consts_b").ap()

    cf = np.zeros((P, 16 + 1 + 128 + 5 * P), dtype=np.float32)
    off_f = {}
    o = 0
    def put_f(name, arr):
        nonlocal o
        h, w = arr.shape
        cf[0:h, o:o + w] = arr
        off_f[name] = (o, w)
        o += w
    put_f("id16f", np.eye(16, dtype=np.float32))
    put_f("ones_col_f", np.ones((P, 1), np.float32))
    put_f("ones_row_f", np.ones((1, P), np.float32))
    pm_np = np.zeros((P, 5 * P), dtype=np.float32)
    for n, c in enumerate([0] + CH_SHIFTS):
        pm_np[:, n * P:(n + 1) * P] = np.roll(np.eye(P, dtype=np.float32), c, axis=1) * SCALE
    put_f("pmix", pm_np)
    consts_f = nc.inline_tensor(cf, "consts_f").ap()

    es = ExitStack()
    with es:
        sch = Sched(nc, es)
        sb = lambda n_, s_, d_: es.enter_context(nc.sbuf_tensor(n_, s_, d_))

        def _finish():
            sch.op("sp", lambda e: e.nop(), deps=sch.all_tokens())
            block = es.enter_context(nc.Block())
            sch.emit(block)

        cbsb = sb("cbsb", [P, cb.shape[1]], BF16)
        cfsb = sb("cfsb", [P, cf.shape[1]], F32)
        vecs_sb = sb("vecs_sb", [P, NV_TOT], F32)
        mixb_sb = sb("mixb_sb", [P, 5], F32)
        mmix_sb = sb("mmix_sb", [P, P], F32)
        hsumneg_sb = sb("hsumneg_sb", [4, NT], BF16)

        def cslice(name, h):
            o, w = off[name]
            return cbsb[0:h, o:o + w]

        def fslice(name, h):
            o, w = off_f[name]
            return cfsb[0:h, o:o + w]

        id128b_sb = cslice("id128b", P)
        onescb_sb = cslice("ones_col_b", P)
        onesrb_sb = cslice("ones_row_b", 1)
        ecols_sb = cslice("ecols", P)
        erows_sb = cslice("erows", NS)
        e4c_sb = cslice("e4c", P)
        e4r_sb = cslice("e4r", 4)
        col32b_sb = cslice("col32b", P)
        id16f_sb = fslice("id16f", 16)
        onescf_sb = fslice("ones_col_f", P)
        onesrf_sb = fslice("ones_row_f", 1)
        pmix_sb = fslice("pmix", P)

        def vslice(name):
            o, w = VOFF[name]
            return vecs_sb[:, o:o + w]

        bq_sb = vslice("bq"); bk_sb = vslice("bk"); bv_sb = vslice("bv")
        bo_sb = vslice("bo"); qnw_sb = vslice("qnw"); knw_sb = vslice("knw")
        dsc_sb = vslice("dsc"); gsc_sb = vslice("gsc"); usc_sb = vslice("usc")
        mixw_sb = vecs_sb[0:1, VOFF["mixw"][0]:VOFF["mixw"][0] + 5]

        t1 = sch.dma("sp", lambda e: e.dma_start(out=cbsb[:, :], in_=consts_b))
        t2 = sch.dma("sp", lambda e: e.dma_start(out=cfsb[:, :], in_=consts_f))
        t3 = sch.dma("sp", lambda e: e.dma_start(out=vecs_sb[:, :], in_=vecs_in))
        t_consts = t3

        t_wcast = {}
        _wcast_src = {"wk": (wk_bf, wk), "wv": (wv_bf, wv), "wq": (wq_bf, wq),
                      "gate": (gate_bf, gate_w), "up": (up_bf, up_w),
                      "down": (down_bf, down_w), "wo": (wo_bf, wo)}

        def emit_wcast(nm, nchunks=1, pace=()):
            d, s = _wcast_src[nm]
            R = d.shape[0]
            step = R // nchunks
            tok = None
            for i in range(nchunks):
                dep = [pace[i]] if i < len(pace) else []
                tok = sch.dma("gq", lambda e, d=d, s=s, i=i, step=step: e.dma_start(
                    out=d[i * step:(i + 1) * step, :], in_=s[i * step:(i + 1) * step, :]),
                    deps=dep)
            t_wcast[nm] = tok

        psm_es = ExitStack()
        ps_small = psm_es.enter_context(nc.psum_tensor("ps_small", [P, 8], F32))
        t_mixbc = sch.op("pe", lambda e: e.matmul(ps_small[:, 0:5], onesrf_sb[:, :],
                                                  mixw_sb[:, :], start=True, stop=True),
                         deps=[t_consts], inc=True)
        t_mixb = sch.op("dve", lambda e: e.tensor_copy(mixb_sb[:, :], ps_small[:, 0:5]),
                        deps=[t_mixbc], inc=True)
        t_mm = sch.op("dve", lambda e: e.tensor_scalar(mmix_sb[:, :], pmix_sb[:, 0:P],
                                                       mixb_sb[:, 0:1], None, ALU.mult),
                      deps=[t_mixb], inc=True)
        for n in range(1, 5):
            t_mm = sch.op("dve", lambda e, n=n: e.scalar_tensor_tensor(
                mmix_sb[:, :], pmix_sb[:, n * P:(n + 1) * P], mixb_sb[:, n:n + 1],
                mmix_sb[:, :], ALU.mult, ALU.add), deps=[t_mm], inc=True)
        psm_es.close()

        def load_weightT(w_bf_ap, wT_sb, nch_out, start_deps):
            return sch.dma("sp", lambda e: e.dma_start(
                out=wT_sb[:, :].rearrange("p (c w) -> p c w", c=nch_out),
                in_=w_bf_ap, transpose=True), deps=list(start_deps))

        a1_es = ExitStack()
        if True:
            wkT = a1_es.enter_context(nc.sbuf_tensor("wkT", [P, NCH * D], BF16))
            xA = a1_es.enter_context(nc.sbuf_tensor("xA", [P, NCH * NT], F32))
            xB = a1_es.enter_context(nc.sbuf_tensor("xB", [P, NCH * NT], F32))
            xntA = a1_es.enter_context(nc.sbuf_tensor("xntA", [P, NCH * NT], BF16))
            xntB = a1_es.enter_context(nc.sbuf_tensor("xntB", [P, NCH * NT], BF16))
            kbf = a1_es.enter_context(nc.sbuf_tensor("kbf", [P, NCH * NT], BF16))
            sqbf = a1_es.enter_context(nc.sbuf_tensor("sqbf", [P, 4 * NT], BF16))
            tf32 = a1_es.enter_context(nc.sbuf_tensor("tf32", [P, NT], F32))
            rows = a1_es.enter_context(nc.sbuf_tensor("rows", [1, 6 * NT], F32))
            lnst = a1_es.enter_context(nc.psum_tensor("lnst", [P, NT], F32))
            lnsq2 = a1_es.enter_context(nc.psum_tensor("lnsq2", [1, NT], F32))
            bca = a1_es.enter_context(nc.psum_tensor("bca", [P, NT], F32))
            bcb = a1_es.enter_context(nc.psum_tensor("bcb", [P, NT], F32))
            mma = a1_es.enter_context(nc.psum_tensor("mma", [P, NT], F32))
            mmb = a1_es.enter_context(nc.psum_tensor("mmb", [P, NT], F32))
            bck = a1_es.enter_context(nc.psum_tensor("bck", [P, NT], F32))
            ksq = a1_es.enter_context(nc.psum_tensor("ksq", [1, NT], F32))

            t_xpre = []
            for te0 in range(2):
                xb0 = (xA, xB)[te0 % 2]
                t_xpre.append(sch.dma("sp", lambda e, xb=xb0, j0=te0 * NT: e.dma_start(
                    out=xb[:, :].rearrange("p (c w) -> p c w", c=NCH),
                    in_=xT_r[:, :, j0:j0 + NT])))
            emit_wcast("wk")
            emit_wcast("wv")
            emit_wcast("wq")
            t_wk = load_weightT(wk_bf, wkT, NCH, [t_wcast["wk"]])

            xbufs = (xA, xB)
            xnbufs1 = (xntA, xntB)
            t_xload = [None] * NTE
            t_tile_done = [None] * NTE
            t_kn_last = [None] * NTE
            t_spill_xn = [None] * NTE
            t_spill_k = [None] * NTE
            t_bc_consumed = None
            t_mean_prev = None
            t_ex2_prev = None
            t_m2_prev = None
            t_uk_prev = None
            t_ksq_war = {}
            t_sq_war_prev = {}
            ROW_MEAN, ROW_EX2, ROW_VAR, ROW_RSTD, ROW_MR, ROW_U = 0, 1, 2, 3, 4, 5

            def r(i):
                return rows[0:1, i * NT:(i + 1) * NT]

            mm_ps = (mma, mmb)
            for te in range(NTE):
                xb = xbufs[te % 2]
                xnt = xnbufs1[te % 2]
                j0 = te * NT
                if te < 2:
                    t_xload[te] = t_xpre[te]
                else:
                    t_xload[te] = sch.dma("sp", lambda e, xb=xb, j0=j0: e.dma_start(
                        out=xb[:, :].rearrange("p (c w) -> p c w", c=NCH),
                        in_=xT_r[:, :, j0:j0 + NT]),
                        deps=[t_tile_done[te - 2]])
                t_sum = None
                for ci in range(NCH):
                    t_sum = sch.op("pe", lambda e, xb=xb, ci=ci: e.matmul(
                        lnst[0:1, :], onescf_sb[:, :], xb[:, ci * NT:(ci + 1) * NT],
                        start=(ci == 0), stop=(ci == NCH - 1), skip_group_check=True),
                        deps=[t_xload[te]] + ([t_mean_prev, t_ex2_prev] if ci == 0 else []),
                        inc=(ci == NCH - 1))
                t_sq_tok = {}
                t_sumsq_tok = dict(t_sq_war_prev)
                for ci in range(NCH):
                    half = ci % 2
                    t_sq_tok[ci] = sch.op("act", lambda e, xb=xb, ci=ci, half=half: e.activation(
                        sqbf[:, half * NT:(half + 1) * NT], xb[:, ci * NT:(ci + 1) * NT],
                        AF.Square),
                        deps=[t_xload[te], t_sumsq_tok.get(ci - 2)], inc=True)
                    t_sumsq_tok[ci] = sch.op("pe", lambda e, ci=ci, half=half: e.matmul(
                        lnsq2[:, :], onescb_sb[:, :], sqbf[:, half * NT:(half + 1) * NT],
                        start=(ci == 0), stop=(ci == NCH - 1), skip_group_check=True),
                        deps=[t_sq_tok[ci]] + ([t_ex2_prev] if ci == 0 else []), inc=True)
                t_sumsq = t_sumsq_tok[NCH - 1]
                t_mean = sch.op("dve", lambda e: e.tensor_scalar(
                    r(ROW_MEAN), lnst[0:1, :], 1.0 / D, None, ALU.mult),
                    deps=[t_sum, t_sumsq, t_m2_prev], inc=True)
                t_mean_prev = t_mean
                t_ex2 = sch.op("dve", lambda e: e.tensor_scalar(
                    r(ROW_EX2), lnsq2[:, :], 1.0 / D, None, ALU.mult),
                    deps=[t_sumsq], inc=True)
                t_ex2_prev = t_ex2
                t_m2 = sch.op("act", lambda e: e.activation(
                    r(ROW_U), r(ROW_MEAN), AF.Square), deps=[t_mean], inc=True)
                t_m2_prev = t_m2
                t_var = sch.op("dve", lambda e: e.scalar_tensor_tensor(
                    r(ROW_VAR), r(ROW_EX2), LN_EPS, r(ROW_U), ALU.add, ALU.subtract),
                    deps=[t_m2, t_ex2], inc=True)
                t_sd = sch.op("act", lambda e: e.activation(
                    r(ROW_U), r(ROW_VAR), AF.Sqrt), deps=[t_var], inc=True)
                t_rstd = sch.op("dve", lambda e: e.reciprocal(r(ROW_RSTD), r(ROW_U)),
                                deps=[t_sd], inc=True)
                t_mr = sch.op("dve", lambda e: e.tensor_mul(
                    r(ROW_MR), r(ROW_MEAN), r(ROW_RSTD)), deps=[t_rstd], inc=True)
                t_bc_r = sch.op("pe", lambda e: e.matmul(
                    bca[:, :], onesrf_sb[:, :], r(ROW_RSTD), start=True, stop=True),
                    deps=[t_rstd, t_bc_consumed], inc=True)
                t_bc_m = sch.op("pe", lambda e: e.matmul(
                    bcb[:, :], onesrf_sb[:, :], r(ROW_MR), start=True, stop=True),
                    deps=[t_mr, t_bc_consumed], inc=True)
                t_xn = None
                for ci in range(NCH):
                    extra = []
                    if ci == 0 and te > 1:
                        extra = [t_kn_last[te - 2], t_spill_xn[te - 2]]
                    sch.op("dve", lambda e, xb=xb, ci=ci: e.tensor_tensor(
                        tf32[:, :], xb[:, ci * NT:(ci + 1) * NT], bca[:, :], ALU.mult),
                        deps=[t_bc_r] + extra)
                    t_xn = sch.op("dve", lambda e, ci=ci, xnt=xnt: e.tensor_tensor(
                        xnt[:, ci * NT:(ci + 1) * NT], tf32[:, :], bcb[:, :], ALU.subtract),
                        deps=[t_bc_m], inc=True)
                t_tile_done[te] = t_xn
                t_bc_consumed = t_xn
                t_spill_xn[te] = sch.dma("gq", lambda e, j0=j0, xnt=xnt: e.dma_start(
                    out=xn_r[:, :, j0:j0 + NT],
                    in_=xnt[:, :].rearrange("p (c w) -> p c w", c=NCH)), deps=[t_xn])
                t_evac_mi = {}
                t_ksq_tok = {}
                t_sqk_tok = {}
                t_ksum = None
                for mi in range(NCH):
                    ps = mm_ps[mi % 2]
                    t_mmk = None
                    for ci in range(NCH):
                        t_mmk = sch.op("pe", lambda e, ps=ps, mi=mi, ci=ci, xnt=xnt: e.matmul(
                            ps[:, :], wkT[:, ci * D + mi * P: ci * D + mi * P + P],
                            xnt[:, ci * NT:(ci + 1) * NT],
                            start=(ci == 0), stop=(ci == NCH - 1), skip_group_check=True),
                            deps=[t_xn, t_wk, t_evac_mi.get(mi - 2)],
                            inc=(ci == NCH - 1))
                    extra = [t_spill_k[te - 1]] if (mi == 0 and te > 0) else []
                    t_ev = sch.op("dve", lambda e, ps=ps, mi=mi: e.tensor_scalar(
                        kbf[:, mi * NT:(mi + 1) * NT], ps[:, :], bk_sb[:, mi:mi + 1],
                        None, ALU.add), deps=[t_mmk] + extra, inc=True)
                    t_evac_mi[mi] = t_ev
                    half = mi % 4
                    if mi >= 4:
                        war_sq = t_ksq_tok.get(mi - 4)
                    elif mi < 2:
                        war_sq = t_sumsq_tok.get(NCH - 2 + mi)
                    else:
                        war_sq = t_sumsq_tok.get(NCH - 1)
                    t_sqk = sch.op("act", lambda e, mi=mi, half=half: e.activation(
                        sqbf[:, half * NT:(half + 1) * NT], kbf[:, mi * NT:(mi + 1) * NT],
                        AF.Square),
                        deps=[t_ev, war_sq], inc=True)
                    t_sqk_tok[mi] = t_sqk
                    if mi >= 2:
                        dk = mi - 2
                        t_ksq_tok[dk] = sch.op("pe", lambda e, dk=dk: e.matmul(
                            ksq[:, :], onescb_sb[:, :], sqbf[:, (dk % 4) * NT:((dk % 4) + 1) * NT],
                            start=(dk == 0), stop=False, skip_group_check=True),
                            deps=[t_sqk_tok[dk]] + ([t_uk_prev] if dk == 0 else []), inc=True)
                for dk in (NCH - 2, NCH - 1):
                    t_ksq_tok[dk] = sch.op("pe", lambda e, dk=dk: e.matmul(
                        ksq[:, :], onescb_sb[:, :], sqbf[:, (dk % 4) * NT:((dk % 4) + 1) * NT],
                        start=False, stop=(dk == NCH - 1), skip_group_check=True),
                        deps=[t_sqk_tok[dk]], inc=True)
                t_ksum = t_ksq_tok[NCH - 1]
                t_uk = sch.op("dve", lambda e: e.tensor_scalar(
                    r(ROW_U), ksq[:, :], 1.0 / D, RMS_EPS, ALU.mult, ALU.add),
                    deps=[t_ksum], inc=True)
                t_uk_prev = t_uk
                t_sdk = sch.op("act", lambda e: e.activation(
                    r(ROW_U), r(ROW_U), AF.Sqrt), deps=[t_uk], inc=True)
                t_rsk = sch.op("dve", lambda e: e.reciprocal(r(ROW_RSTD), r(ROW_U)),
                               deps=[t_sdk], inc=True)
                t_rows_prev = t_rsk
                t_bck = sch.op("pe", lambda e: e.matmul(
                    bck[:, :], onesrf_sb[:, :], r(ROW_RSTD), start=True, stop=True),
                    deps=[t_rsk, t_kn_last[te - 1] if te > 0 else None], inc=True)
                t_kn = None
                for mi in range(NCH):
                    t_kn = sch.op("dve", lambda e, mi=mi: e.scalar_tensor_tensor(
                        kbf[:, mi * NT:(mi + 1) * NT], kbf[:, mi * NT:(mi + 1) * NT],
                        knw_sb[:, mi:mi + 1], bck[:, :], ALU.mult, ALU.mult),
                        deps=[t_bck], inc=(mi == NCH - 1))
                t_kn_last[te] = t_kn
                t_sq_war_prev = {-2: t_ksq_tok[NCH - 4], -1: t_ksq_tok[NCH - 3]}
                t_spill_k[te] = sch.dma("gq", lambda e, j0=j0: e.dma_start(
                    out=k_r[:, :, j0:j0 + NT],
                    in_=kbf[:, :].rearrange("p (c w) -> p c w", c=NCH)), deps=[t_kn])
            tA1_done = [t_spill_xn[-1], t_spill_k[-1]]
        a1_es.close()
        if stop_after == "A1":
            _finish()
            return nc

        a2_es = ExitStack()
        if True:
            wvT = a2_es.enter_context(nc.sbuf_tensor("wvT", [P, NCH * D], BF16))
            xnA2 = a2_es.enter_context(nc.sbuf_tensor("xnA2", [P, NCH * NT], BF16))
            xnB2 = a2_es.enter_context(nc.sbuf_tensor("xnB2", [P, NCH * NT], BF16))
            vbf = a2_es.enter_context(nc.sbuf_tensor("vbf", [P, NCH * NT], BF16))
            mma2 = a2_es.enter_context(nc.psum_tensor("mma2", [P, NT], F32))
            mmb2 = a2_es.enter_context(nc.psum_tensor("mmb2", [P, NT], F32))
            mmc2 = a2_es.enter_context(nc.psum_tensor("mmc2", [P, NT], F32))
            mmd2 = a2_es.enter_context(nc.psum_tensor("mmd2", [P, NT], F32))

            emit_wcast("gate")
            emit_wcast("up")
            t_wv = load_weightT(wv_bf, wvT, NCH, [t_wcast["wv"]] + tA1_done)
            xnbufs = (xnA2, xnB2)
            t_ld = [None] * NTE
            t_done = [None] * NTE
            t_spill_v = [None] * NTE
            mm_ps2 = (mma2, mmb2, mmc2, mmd2)
            t_ev2 = {}
            for te in range(NTE):
                xb = xnbufs[te % 2]
                j0 = te * NT
                t_ld[te] = sch.dma("sp", lambda e, xb=xb, j0=j0: e.dma_start(
                    out=xb[:, :].rearrange("p (c w) -> p c w", c=NCH),
                    in_=xn_r[:, :, j0:j0 + NT]),
                    deps=([t_done[te - 2]] if te >= 2 else tA1_done))
                t_vn = None
                for mi in range(NCH):
                    ps = mm_ps2[mi % 4]
                    war = t_ev2.get((te, mi - 4)) if mi >= 4 else t_ev2.get((te - 1, NCH - 4 + mi))
                    t_mmv = None
                    for ci in range(NCH):
                        t_mmv = sch.op("pe", lambda e, ps=ps, mi=mi, ci=ci, xb=xb: e.matmul(
                            ps[:, :], wvT[:, ci * D + mi * P: ci * D + mi * P + P],
                            xb[:, ci * NT:(ci + 1) * NT],
                            start=(ci == 0), stop=(ci == NCH - 1), skip_group_check=True),
                            deps=[t_ld[te], t_wv, war], inc=(ci == NCH - 1))
                    extra = [t_spill_v[te - 1]] if (mi == 0 and te > 0) else []
                    t_vn = sch.op("dve", lambda e, ps=ps, mi=mi: e.tensor_scalar(
                        vbf[:, mi * NT:(mi + 1) * NT], ps[:, :], bv_sb[:, mi:mi + 1],
                        None, ALU.add), deps=[t_mmv] + extra, inc=True)
                    t_ev2[(te, mi)] = t_vn
                t_done[te] = t_vn
                t_spill_v[te] = sch.dma("gq", lambda e, j0=j0: e.dma_start(
                    out=v_r[:, :, j0:j0 + NT],
                    in_=vbf[:, :].rearrange("p (c w) -> p c w", c=NCH)), deps=[t_vn])
            tA2_done = [t_spill_v[-1]]
        a2_es.close()
        if stop_after == "A2":
            _finish()
            return nc

        a3_es = ExitStack()
        if True:
            wqT = a3_es.enter_context(nc.sbuf_tensor("wqT", [P, NCH * D], BF16))
            xnA3 = a3_es.enter_context(nc.sbuf_tensor("xnA3", [P, NCH * NT], BF16))
            xnB3 = a3_es.enter_context(nc.sbuf_tensor("xnB3", [P, NCH * NT], BF16))
            qbf = a3_es.enter_context(nc.sbuf_tensor("qbf", [P, NCH * NT], BF16))
            qmx = a3_es.enter_context(nc.sbuf_tensor("qmx", [P, NCH * NT], BF16))
            qnf = a3_es.enter_context(nc.sbuf_tensor("qnf", [P, NT], F32))
            sqbf3 = a3_es.enter_context(nc.sbuf_tensor("sqbf3", [P, 2 * NT], BF16))
            rows3 = a3_es.enter_context(nc.sbuf_tensor("rows3", [1, 2 * NT], F32))
            mma3 = a3_es.enter_context(nc.psum_tensor("mma3", [P, NT], F32))
            mmb3 = a3_es.enter_context(nc.psum_tensor("mmb3", [P, NT], F32))
            mmc3 = a3_es.enter_context(nc.psum_tensor("mmc3", [P, NT], F32))
            bca3 = a3_es.enter_context(nc.psum_tensor("bca3", [P, NT], F32))
            qsq = a3_es.enter_context(nc.psum_tensor("qsq", [1, NT], F32))

            t_wq = load_weightT(wq_bf, wqT, NCH, [t_wcast["wq"]] + tA2_done)
            xnbufs3 = (xnA3, xnB3)
            t_ld3 = [None] * NTO
            t_done3 = [None] * NTO
            t_spill_q = [None] * NTO
            mm_ps3 = (mma3, mmb3, mmc3)
            rq = lambda i: rows3[0:1, i * NT:(i + 1) * NT]
            t_bcq_prev = None
            for to in range(NTO):
                xb = xnbufs3[to % 2]
                j0 = HALO_L + to * NT
                t_ld3[to] = sch.dma("sp", lambda e, xb=xb, j0=j0: e.dma_start(
                    out=xb[:, :].rearrange("p (c w) -> p c w", c=NCH),
                    in_=xn_r[:, :, j0:j0 + NT]),
                    deps=([t_done3[to - 2]] if to >= 2 else tA2_done))
                t_ev3 = {}
                t_qsq_tok = {}
                for mi in range(NCH):
                    ps = mm_ps3[mi % 3]
                    war = t_ev3.get(mi - 3) if mi >= 3 else (t_done3[to - 1] if to > 0 else None)
                    t_mmq = None
                    for ci in range(NCH):
                        t_mmq = sch.op("pe", lambda e, ps=ps, mi=mi, ci=ci, xb=xb: e.matmul(
                            ps[:, :], wqT[:, ci * D + mi * P: ci * D + mi * P + P],
                            xb[:, ci * NT:(ci + 1) * NT],
                            start=(ci == 0), stop=(ci == NCH - 1), skip_group_check=True),
                            deps=[t_ld3[to], t_wq, war], inc=(ci == NCH - 1))
                    extra = [t_spill_q[to - 1]] if (mi == 0 and to > 0) else []
                    t_evq = sch.op("dve", lambda e, ps=ps, mi=mi: e.tensor_scalar(
                        qbf[:, mi * NT:(mi + 1) * NT], ps[:, :], bq_sb[:, mi:mi + 1],
                        None, ALU.add), deps=[t_mmq] + extra, inc=True)
                    t_ev3[mi] = t_evq
                    half = mi % 2
                    t_sqq = sch.op("act", lambda e, mi=mi, half=half: e.activation(
                        sqbf3[:, half * NT:(half + 1) * NT], qbf[:, mi * NT:(mi + 1) * NT],
                        AF.Square), deps=[t_evq, t_qsq_tok.get(mi - 2)], inc=True)
                    t_qsq_tok[mi] = sch.op("pe", lambda e, mi=mi, half=half: e.matmul(
                        qsq[:, :], onescb_sb[:, :], sqbf3[:, half * NT:(half + 1) * NT],
                        start=(mi == 0), stop=(mi == NCH - 1), skip_group_check=True),
                        deps=[t_sqq], inc=True)
                t_uq = sch.op("dve", lambda e: e.tensor_scalar(
                    rq(0), qsq[:, :], 1.0 / D, RMS_EPS, ALU.mult, ALU.add),
                    deps=[t_qsq_tok[NCH - 1]], inc=True)
                t_sdq = sch.op("act", lambda e: e.activation(rq(0), rq(0), AF.Sqrt),
                               deps=[t_uq], inc=True)
                t_rsq = sch.op("dve", lambda e: e.reciprocal(rq(1), rq(0)),
                               deps=[t_sdq], inc=True)
                t_bcq = sch.op("pe", lambda e: e.matmul(
                    bca3[:, :], onesrf_sb[:, :], rq(1), start=True, stop=True),
                    deps=[t_rsq, t_bcq_prev], inc=True)
                t_qm = None
                t_qmm_prev = None
                for mi in range(NCH):
                    ps = mm_ps3[mi % 3]
                    t_qn = sch.op("dve", lambda e, mi=mi: e.scalar_tensor_tensor(
                        qnf[:, :], qbf[:, mi * NT:(mi + 1) * NT], qnw_sb[:, mi:mi + 1],
                        bca3[:, :], ALU.mult, ALU.mult),
                        deps=[t_bcq, t_qmm_prev], inc=True)
                    t_qmm = sch.op("pe", lambda e, ps=ps: e.matmul(
                        ps[:, :], mmix_sb[:, :], qnf[:, :], start=True, stop=True),
                        deps=[t_qn, t_mm], inc=True)
                    t_qmm_prev = t_qmm
                    extra = [t_spill_q[to - 1]] if (mi == 0 and to > 0) else []
                    t_qm = sch.op("dve", lambda e, ps=ps, mi=mi: e.tensor_copy(
                        qmx[:, mi * NT:(mi + 1) * NT], ps[:, :]),
                        deps=[t_qmm] + extra, inc=True)
                t_bcq_prev = t_qm
                t_done3[to] = t_qm
                t_spill_q[to] = sch.dma("gq", lambda e, j0=j0: e.dma_start(
                    out=qm_r[:, :, j0 - HALO_L:j0 - HALO_L + NT],
                    in_=qmx[:, :].rearrange("p (c w) -> p c w", c=NCH)), deps=[t_qm])
            tA3_done = [t_spill_q[-1]]
        a3_es.close()
        if stop_after == "A3":
            _finish()
            return nc

        bc_es = ExitStack()
        if True:
            xno = bc_es.enter_context(nc.sbuf_tensor("xno", [P, NCH * T], BF16))
            wgT = bc_es.enter_context(nc.sbuf_tensor("wgT", [P, NCH * NT], BF16))
            wuT = bc_es.enter_context(nc.sbuf_tensor("wuT", [P, NCH * NT], BF16))
            hbfA = bc_es.enter_context(nc.sbuf_tensor("hbfA", [P, 4 * T], BF16))
            hbfB = bc_es.enter_context(nc.sbuf_tensor("hbfB", [P, 4 * T], BF16))
            csneg = bc_es.enter_context(nc.sbuf_tensor("csneg", [1, T], BF16))
            siluCA = bc_es.enter_context(nc.sbuf_tensor("siluCA", [P, NT], F32))
            siluCB = siluCA
            KE = 4864
            kh1 = bc_es.enter_context(nc.sbuf_tensor("kh1", [P, KE], BF16))
            khs = [kh1, kh1]
            vhs = [bc_es.enter_context(nc.sbuf_tensor(f"vh{k}", [P, KE], BF16)) for k in range(2)]
            qh1 = bc_es.enter_context(nc.sbuf_tensor("qh1", [P, T], BF16))
            qhs = [qh1, qh1]
            acch = bc_es.enter_context(nc.sbuf_tensor("acch", [P, T], F32))
            tmpPA = bc_es.enter_context(nc.sbuf_tensor("tmpPA", [P, NS * NT], BF16))
            tmpPB = tmpPA
            scsb = bc_es.enter_context(nc.sbuf_tensor("scsb", [16, NT], F32))
            ptm = bc_es.enter_context(nc.sbuf_tensor("ptm", [P, 64], BF16))
            pfm = bc_es.enter_context(nc.sbuf_tensor("pfm", [16, NT], BF16))
            nmax = bc_es.enter_context(nc.sbuf_tensor("nmax", [P, 4], F32))
            esum = bc_es.enter_context(nc.sbuf_tensor("esum", [P, 8], F32))
            expv = bc_es.enter_context(nc.sbuf_tensor("expv", [P, 64], F32))
            prodA = bc_es.enter_context(nc.sbuf_tensor("prodA", [P, NT], BF16))
            prodB = bc_es.enter_context(nc.sbuf_tensor("prodB", [P, NT], BF16))

            emit_wcast("down")
            emit_wcast("wo")
            t_xno = sch.dma("sp", lambda e: e.dma_start(
                out=xno[:, :].rearrange("p (c w) -> p c w", c=NCH),
                in_=xn_r[:, :, HALO_L:HALO_L + T]), deps=tA3_done)
            cs_es = ExitStack()
            csps = cs_es.enter_context(nc.psum_tensor("csps", [1, NT], F32))
            t_cs = [None] * NTO
            for to in range(NTO):
                t_c = None
                for ci in range(NCH):
                    t_c = sch.op("pe", lambda e, ci=ci, to=to: e.matmul(
                        csps[:, :], onescb_sb[:, :],
                        xno[:, ci * T + to * NT: ci * T + (to + 1) * NT],
                        start=(ci == 0), stop=(ci == NCH - 1), skip_group_check=True),
                        deps=[t_xno] + ([t_cs[to - 1]] if to > 0 else []),
                        inc=(ci == NCH - 1))
                t_cs[to] = sch.op("dve", lambda e, to=to: e.tensor_scalar(
                    csneg[0:1, to * NT:(to + 1) * NT], csps[:, :], -128.0, None, ALU.mult),
                    deps=[t_c], inc=True)
            t_cs_done = t_cs[-1]
            cs_es.close()

            gpsA = bc_es.enter_context(nc.psum_tensor("gpsA", [P, NT], F32))
            upsA = bc_es.enter_context(nc.psum_tensor("upsA", [P, NT], F32))
            upsB = bc_es.enter_context(nc.psum_tensor("upsB", [P, NT], F32))
            scps = bc_es.enter_context(nc.psum_tensor("scps", [16, NT], F32))
            tmps = bc_es.enter_context(nc.psum_tensor("tmps", [P, 64], F32))
            pfps = bc_es.enter_context(nc.psum_tensor("pfps", [16, NT], BF16))
            pbcA = bc_es.enter_context(nc.psum_tensor("pbcA", [P, NT], F32))
            pbcB = bc_es.enter_context(nc.psum_tensor("pbcB", [P, NT], F32))

            n_ic = ICH
            t_scevac = {}
            t_stack = {}
            t_soft_prev = None
            t_pfevac_prev = None
            t_mult = {}
            t_add = {}
            t_slab = {}
            t_asp = {}
            t_exp_tok = {}
            t_trdone = {}
            t_mm_last = {}
            t_si_tok = {}
            t_uc_tok = {}
            t_hh_tok = {}
            t_h = {}
            t_hspill = [None] * ICH
            t_hsum_last = None

            def emit_b_head(it):
                nonlocal t_soft_prev
                t_sce = sch.op("dve", lambda e: e.tensor_copy(scsb[:, :], scps[:, :]),
                               deps=[t_stack[it]], inc=True)
                t_scevac[it] = t_sce
                t_tr = None
                for i4 in range(4):
                    extra = [t_soft_prev] if i4 == 0 else []
                    t_tr = sch.op("pe", lambda e, i4=i4: e.transpose(
                        tmps[:, i4 * 16:(i4 + 1) * 16], scsb[:, i4 * P:(i4 + 1) * P],
                        id16f_sb[:, :]), deps=[t_sce] + extra, inc=(i4 == 3))
                return t_tr

            t_pr_last = {}

            def emit_b_products(it):
                hh = it // NTO
                to = it % NTO
                tb = (tmpPA, tmpPB)[it % 2]
                kh = khs[hh % 2]
                qh = qhs[hh % 2]
                pj = HALO_L + to * NT
                t_pr = None
                for si, s in enumerate(SEQ_SHIFTS):
                    extra = [t_stack.get(it - 1)] if si == 0 else []
                    t_pr = sch.op("dve",
                                  lambda e, tb=tb, si=si, s=s, pj=pj, to=to, qh=qh, kh=kh:
                                  e.tensor_tensor(
                                      tb[:, si * NT:(si + 1) * NT],
                                      qh[:, to * NT:(to + 1) * NT],
                                      kh[:, pj - s: pj - s + NT], ALU.mult),
                                  deps=[t_slab[hh]] + extra, inc=(si == NS - 1))
                t_pr_last[hh] = t_pr
                return t_pr

            def emit_b_stack(it, t_pr):
                t_st = None
                for si in range(NS):
                    extra = [t_scevac.get(it - 1)] if si == 0 else []
                    t_st = sch.op("pe", lambda e, tb=(tmpPA, tmpPB)[it % 2], si=si: e.matmul(
                        scps[:, :], ecols_sb[:, si * NS:(si + 1) * NS],
                        tb[:, si * NT:(si + 1) * NT],
                        start=(si == 0), stop=(si == NS - 1), skip_group_check=True),
                        deps=[t_pr] + extra, inc=(si == NS - 1))
                t_stack[it] = t_st

            def emit_b_rest(it, t_tr):
                nonlocal t_soft_prev, t_pfevac_prev
                hh = it // NTO
                to = it % NTO
                pj = HALO_L + to * NT
                t_rm = None
                for i4 in range(4):
                    t_rm = sch.op("dve", lambda e, i4=i4: e.reduce_max(
                        nmax[:, i4:i4 + 1], tmps[:, i4 * 16:(i4 + 1) * 16],
                        axis=AX.X, negate=True), deps=[t_tr], inc=(i4 == 3))
                t_ex = None
                for i4 in range(4):
                    t_ex = sch.op("act", lambda e, i4=i4: e.activation(
                        expv[:, i4 * 16:(i4 + 1) * 16], tmps[:, i4 * 16:(i4 + 1) * 16],
                        AF.Exp, bias=nmax[:, i4:i4 + 1]), deps=[t_rm], inc=(i4 == 3))
                t_exp_tok[it] = t_ex
                t_rs = None
                for i4 in range(4):
                    t_rs = sch.op("dve", lambda e, i4=i4: e.reduce_sum(
                        esum[:, i4:i4 + 1], expv[:, i4 * 16:(i4 + 1) * 16],
                        axis=AX.X), deps=[t_ex], inc=(i4 == 3))
                t_rc = sch.op("dve", lambda e: e.reciprocal(esum[:, 4:8], esum[:, 0:4]),
                              deps=[t_rs], inc=True)
                t_pn = None
                for i4 in range(4):
                    t_pn = sch.op("dve", lambda e, i4=i4: e.tensor_scalar(
                        ptm[:, i4 * 16:(i4 + 1) * 16], expv[:, i4 * 16:(i4 + 1) * 16],
                        esum[:, 4 + i4:5 + i4], None, ALU.mult),
                        deps=[t_rc], inc=(i4 == 3))
                t_soft_prev = t_pn
                t_pf = None
                for i4 in range(4):
                    extra = [t_pfevac_prev] if i4 == 0 else []
                    t_pf = sch.op("pe", lambda e, i4=i4: e.transpose(
                        pfps[:, i4 * P:(i4 + 1) * P], ptm[:, i4 * 16:(i4 + 1) * 16],
                        id128b_sb[:, :]), deps=[t_pn] + extra, inc=(i4 == 3))
                t_pfe = sch.op("dve", lambda e: e.tensor_copy(pfm[:, :], pfps[:, :]),
                               deps=[t_pf], inc=True)
                t_pfevac_prev = t_pfe
                for si, s in enumerate(SEQ_SHIFTS):
                    pb = (pbcA, pbcB)[si % 2]
                    pr = (prodA, prodB)[si % 2]
                    war_b = t_mult.get((it, si - 2)) if si >= 2 else t_mult.get((it - 1, NS - 2 + si))
                    t_b = sch.op("pe", lambda e, pb=pb, si=si: e.matmul(
                        pb[:, :], erows_sb[:, si * P:(si + 1) * P], pfm[:, :],
                        start=True, stop=True), deps=[t_pfe, war_b], inc=True)
                    war_p = t_add.get((it, si - 2)) if si >= 2 else t_add.get((it - 1, NS - 2 + si))
                    t_mu = sch.op("dve",
                                  lambda e, pb=pb, pr=pr, pj=pj, s=s, vh=vhs[hh % 2]:
                                  e.tensor_tensor(
                                      pr[:, :], vh[:, pj - s: pj - s + NT],
                                      pb[:, :], ALU.mult),
                                  deps=[t_b, war_p], inc=True)
                    t_mult[(it, si)] = t_mu
                    if si == 0:
                        extra = [t_asp.get(hh - 1)]
                        t_a = sch.op("pool", lambda e, pr=pr, to=to: e.tensor_copy(
                            acch[:, to * NT:(to + 1) * NT], pr[:, :]),
                            deps=[t_mu] + extra, inc=True)
                    else:
                        t_a = sch.op("pool", lambda e, pr=pr, to=to: e.tensor_tensor(
                            acch[:, to * NT:(to + 1) * NT],
                            acch[:, to * NT:(to + 1) * NT],
                            pr[:, :], ALU.add), deps=[t_mu], inc=True)
                    t_add[(it, si)] = t_a

            def emit_c_unit(ic, mi, to):
                idx = ic * 16 + mi * NTO + to
                icol = ic * 4 + mi
                hb = (hbfA, hbfB)[ic % 2]
                gp = gpsA
                up = (upsA, upsB)[idx % 2]
                for ci in range(NCH):
                    sch.op("pe", lambda e, gp=gp, mi=mi, to=to, ci=ci: e.matmul(
                        gp[:, :], wgT[:, ci * NT + mi * P: ci * NT + mi * P + P],
                        xno[:, ci * T + to * NT: ci * T + (to + 1) * NT],
                        start=(ci == 0), stop=False, skip_group_check=True),
                        deps=[t_trdone[(ic, "g")], t_cs_done, t_si_tok.get(idx - 1)]
                        if ci == 0 else [])
                t_g = sch.op("pe", lambda e, gp=gp, to=to: e.matmul(
                    gp[:, :], onesrb_sb[:, :], csneg[0:1, to * NT:(to + 1) * NT],
                    start=False, stop=True, skip_group_check=True), deps=[], inc=True)
                t_mm_last[(ic, "g")] = t_g
                for ci in range(NCH):
                    sch.op("pe", lambda e, up=up, mi=mi, to=to, ci=ci: e.matmul(
                        up[:, :], wuT[:, ci * NT + mi * P: ci * NT + mi * P + P],
                        xno[:, ci * T + to * NT: ci * T + (to + 1) * NT],
                        start=(ci == 0), stop=False, skip_group_check=True),
                        deps=[t_trdone[(ic, "u")], t_uc_tok.get(idx - 2)]
                        if ci == 0 else [])
                t_u = sch.op("pe", lambda e, up=up, to=to: e.matmul(
                    up[:, :], onesrb_sb[:, :], csneg[0:1, to * NT:(to + 1) * NT],
                    start=False, stop=True, skip_group_check=True), deps=[], inc=True)
                t_mm_last[(ic, "u")] = t_u
                sl = siluCA
                t_si = sch.op("act", lambda e, gp=gp, sl=sl, icol=icol: e.activation(
                    sl[:, :], gp[:, :], AF.Silu, scale=gsc_sb[:, icol:icol + 1]),
                    deps=[t_g, t_hh_tok.get(idx - 1)], inc=True)
                t_si_tok[idx] = t_si
                extra = [t_hspill[ic - 2]] if (mi == 0 and to == 0 and ic >= 2) else []
                t_uc = sch.op("act", lambda e, up=up, hb=hb, mi=mi, to=to, icol=icol:
                              e.activation(
                                  hb[:, mi * T + to * NT: mi * T + (to + 1) * NT],
                                  up[:, :], AF.Copy, scale=usc_sb[:, icol:icol + 1]),
                              deps=[t_u] + extra, inc=True)
                t_uc_tok[idx] = t_uc
                t_hh = sch.op("pool",
                              lambda e, sl=sl, hb=hb, mi=mi, to=to:
                              e.tensor_tensor(
                                  hb[:, mi * T + to * NT: mi * T + (to + 1) * NT],
                                  hb[:, mi * T + to * NT: mi * T + (to + 1) * NT],
                                  sl[:, :], ALU.mult),
                              deps=[t_uc, t_si], inc=True)
                t_hh_tok[idx] = t_hh
                t_h[(ic, mi, to)] = t_hh

            def emit_step_loads(step):
                ic = step
                hh = step
                for (w_bf_ap, wTb, tag) in ((gate_bf, wgT, "g"), (up_bf, wuT, "u")):
                    t_trdone[(ic, tag)] = sch.dma("sp", lambda e, w_bf_ap=w_bf_ap, wTb=wTb,
                                                  ic=ic: e.dma_start(
                        out=wTb[:, :].rearrange("p (c w) -> p c w", c=NCH),
                        in_=w_bf_ap[ic * NT:(ic + 1) * NT, :], transpose=True),
                        deps=[t_wcast["gate" if tag == "g" else "up"],
                              t_mm_last.get((ic - 1, tag))] +
                             ([t_xno] if ic == 0 else []))
                kq_war = [t_pr_last.get(hh - 1)]
                v_war = [t_mult[(4 * (hh - 2) + 3, NS - 1)]] if hh >= 2 else []
                t_k = sch.dma("sp", lambda e, hh=hh, kh=khs[hh % 2]: e.dma_start(
                    out=kh[:, :], in_=k_r[:, hh, 0:KE]),
                    deps=kq_war + (tA3_done if hh == 0 else []))
                t_v = sch.dma("sp", lambda e, hh=hh, vh=vhs[hh % 2]: e.dma_start(
                    out=vh[:, :], in_=v_r[:, hh, 0:KE]), deps=v_war)
                t_q = sch.dma("sp", lambda e, hh=hh, qh=qhs[hh % 2]: e.dma_start(
                    out=qh[:, :], in_=qm_r[:, hh, :]), deps=kq_war)
                t_slab[hh] = t_q

            emit_step_loads(0)
            for step in range(16):
                ic = step
                hh = step
                b2 = ic % 2
                hb = (hbfA, hbfB)[b2]
                its = [4 * hh + j for j in range(4)]
                cu = [(mi, to) for mi in range(4) for to in range(NTO)]
                t_pr_pend = {}
                t_tr_pend = {}
                for j4 in range(4):
                    itj = its[j4]
                    emit_c_unit(ic, *cu[4 * j4 + 0])
                    emit_c_unit(ic, *cu[4 * j4 + 1])
                    if j4 > 0:
                        t_tr_pend[its[j4 - 1]] = emit_b_head(its[j4 - 1])
                    t_pr_pend[itj] = emit_b_products(itj)
                    emit_c_unit(ic, *cu[4 * j4 + 2])
                    emit_c_unit(ic, *cu[4 * j4 + 3])
                    emit_b_stack(itj, t_pr_pend[itj])
                    if j4 > 0:
                        emit_b_rest(its[j4 - 1], t_tr_pend[its[j4 - 1]])
                t_tr_pend[its[3]] = emit_b_head(its[3])
                emit_b_rest(its[3], t_tr_pend[its[3]])
                if step < 15:
                    emit_step_loads(step + 1)
                t_hspill[ic] = sch.dma("sp", lambda e, hb=hb, ic=ic: e.dma_start(
                    out=h_r[:, ic * 4:(ic + 1) * 4, :],
                    in_=hb[:, :].rearrange("p (c w) -> p c w", c=4)),
                    deps=[t_h[(ic, 3, NTO - 1)]])
                t_asp[hh] = sch.dma("gq", lambda e, hh=hh: e.dma_start(
                    out=at_r[:, hh, :], in_=acch[:, :]),
                    deps=[t_add[(its[3], NS - 1)]])
            tC_done = [t_hspill[-1], t_hspill[-2], t_asp[15]]
        bc_es.close()
        if stop_after is not None and (stop_after.startswith("C") or stop_after == "B"):
            _finish()
            return nc

        t_out_stores = []
        t_prev_half = list(tC_done)
        t_hsn_war = None
        for H in range(2):
            th0 = H * TH2
            acc_es = ExitStack()
            acc = acc_es.enter_context(nc.sbuf_tensor(f"accD{H}", [P, NCH * TH2], F32))
            d_es = ExitStack()
            wdTs = [d_es.enter_context(nc.sbuf_tensor(f"wdT{H}_{k}", [P, 4 * D], BF16))
                    for k in range(4)]
            hDs = [d_es.enter_context(nc.sbuf_tensor(f"hD{H}_{k}", [P, 4 * TH2], BF16))
                   for k in range(4)]
            opsA = d_es.enter_context(nc.psum_tensor(f"opsA{H}", [P, NT], F32))
            opsB = d_es.enter_context(nc.psum_tensor(f"opsB{H}", [P, NT], F32))
            opsC = d_es.enter_context(nc.psum_tensor(f"opsC{H}", [P, NT], F32))
            opsD = d_es.enter_context(nc.psum_tensor(f"opsD{H}", [P, NT], F32))
            hsumD = d_es.enter_context(nc.psum_tensor(f"hsumD{H}", [4, NT], F32))

            t_trD = {}
            t_hld = {}
            t_evD = {}
            NPAIR = 8
            for p8 in range(NPAIR):
                for k2 in range(2):
                    ic = p8 * 2 + k2
                    b4 = (p8 % 2) * 2 + k2
                    war = [t_evD.get((p8 - 2, NCH - 1, 1))] + (t_prev_half if p8 < 2 else [])
                    t_trD[ic] = sch.dma("sp", lambda e, ic=ic, b4=b4, wdTs=wdTs: e.dma_start(
                        out=wdTs[b4][:, :].rearrange("p (c w) -> p c w", c=4),
                        in_=down_bf[:, ic * NT:(ic + 1) * NT], transpose=True),
                        deps=[t_wcast["down"]] + war)
                    t_hld[ic] = sch.dma("sp", lambda e, ic=ic, b4=b4, hDs=hDs, th0=th0: e.dma_start(
                        out=hDs[b4][:, :].rearrange("p (c w) -> p c w", c=4),
                        in_=h_r[:, ic * 4:(ic + 1) * 4, th0:th0 + TH2]),
                        deps=war)
                    for r4 in range(4):
                        for th in range(2):
                            first = (ic == 0 and r4 == 0 and th == 0)
                            last = (ic == ICH - 1 and r4 == 3 and th == 1)
                            tg = 2 * H + th
                            t_hsum_lastD = sch.op("pe",
                                                  lambda e, b4=b4, r4=r4, th=th, tg=tg,
                                                  first=first, last=last, hDs=hDs, hsumD=hsumD:
                                                  e.matmul(hsumD[:, :],
                                                           e4c_sb[:, tg * 4:(tg + 1) * 4],
                                                           hDs[b4][:, r4 * TH2 + th * NT: r4 * TH2 + (th + 1) * NT],
                                                           start=first, stop=last,
                                                           skip_group_check=True),
                                                  deps=[t_hld[ic]], inc=last)
                            if last:
                                t_hsnD = sch.op("dve", lambda e, hsumD=hsumD: e.tensor_scalar(
                                    hsumneg_sb[:, :], hsumD[:, :], -128.0, None, ALU.mult),
                                    deps=[t_hsum_lastD, t_hsn_war], inc=True)
                for mi in range(NCH):
                    for th in range(2):
                        tg = 2 * H + th
                        par = (mi * 2 + th) % 4
                        ops = (opsA, opsB, opsC, opsD)[par]
                        war = t_evD.get((p8, mi - 2, th)) if mi >= 2 else \
                            t_evD.get((p8 - 1, NCH - 2 + mi, th))
                        t_mmlast = None
                        cnt = 0
                        nmm = 8
                        for k2 in range(2):
                            ic = p8 * 2 + k2
                            b4 = (p8 % 2) * 2 + k2
                            for r4 in range(4):
                                first = (cnt == 0)
                                is_last_real = (cnt == nmm - 1)
                                stop = is_last_real and (p8 != NPAIR - 1)
                                deps = [t_trD[ic], t_hld[ic]]
                                if first:
                                    deps.append(war)
                                t_mmlast = sch.op("pe",
                                                  lambda e, ops=ops, b4=b4, r4=r4, mi=mi, th=th,
                                                  first=first, stop=stop, wdTs=wdTs, hDs=hDs:
                                                  e.matmul(
                                                      ops[:, :],
                                                      wdTs[b4][:, r4 * D + mi * P: r4 * D + mi * P + P],
                                                      hDs[b4][:, r4 * TH2 + th * NT: r4 * TH2 + (th + 1) * NT],
                                                      start=first, stop=stop,
                                                      skip_group_check=True),
                                                  deps=deps,
                                                  inc=(is_last_real and p8 != NPAIR - 1))
                                cnt += 1
                        if p8 == NPAIR - 1:
                            t_mmlast = sch.op("pe", lambda e, ops=ops, tg=tg: e.matmul(
                                ops[:, :], e4r_sb[:, tg * P:(tg + 1) * P],
                                hsumneg_sb[0:4, 0:NT],
                                start=False, stop=True, skip_group_check=True),
                                deps=[t_hsnD], inc=True)
                        if p8 == 0:
                            t_e = sch.op("dve", lambda e, ops=ops, mi=mi, th=th, acc=acc: e.tensor_copy(
                                acc[:, mi * TH2 + th * NT: mi * TH2 + (th + 1) * NT],
                                ops[:, :]), deps=[t_mmlast], inc=True)
                        else:
                            t_e = sch.op("dve", lambda e, ops=ops, mi=mi, th=th, acc=acc: e.tensor_tensor(
                                acc[:, mi * TH2 + th * NT: mi * TH2 + (th + 1) * NT],
                                acc[:, mi * TH2 + th * NT: mi * TH2 + (th + 1) * NT],
                                ops[:, :], ALU.add), deps=[t_mmlast], inc=True)
                        t_evD[(p8, mi, th)] = t_e
            t_D_done = t_evD[(NPAIR - 1, NCH - 1, 1)]
            t_hsn_war = t_D_done
            if dbg:
                t_D_done = sch.dma("sp", lambda e, acc=acc, th0=th0: e.dma_start(
                    out=acc_dbg.rearrange("(c p) t -> p c t", p=P)[:, :, th0:th0 + TH2],
                    in_=acc[:, :].rearrange("p (c w) -> p c w", c=NCH)),
                    deps=[t_evD[(NPAIR - 1, mi, th)] for mi in range(NCH) for th in range(2)])
            d_es.close()

            e_es = ExitStack()
            attnE = e_es.enter_context(nc.sbuf_tensor(f"attnE{H}", [P, NCH * TH2], BF16))
            woT_bufs = [e_es.enter_context(nc.sbuf_tensor(f"woT{H}_{k}", [P, D], BF16))
                        for k in range(4)]
            xE = [e_es.enter_context(nc.sbuf_tensor(f"xE{H}_{k}", [P, NCH * NT], F32))
                  for k in range(2)]
            siluE = [e_es.enter_context(nc.sbuf_tensor(f"siluE{H}_{k}", [P, NT], F32))
                     for k in range(4)]
            sgE = [e_es.enter_context(nc.sbuf_tensor(f"sgE{H}_{k}", [P, NT], F32))
                   for k in range(2)]
            soPS = [e_es.enter_context(nc.psum_tensor(f"so{H}_{k}", [P, NT], F32))
                    for k in range(6)]

            t_attn_ld = sch.dma("sp", lambda e, attnE=attnE, th0=th0: e.dma_start(
                out=attnE[:, :].rearrange("p (c w) -> p c w", c=NCH),
                in_=at_r[:, :, th0:th0 + TH2]), deps=[t_D_done])
            t_x_ld = [None, None]
            for th in range(2):
                t_x_ld[th] = sch.dma("sp", lambda e, th=th, xE=xE, th0=th0: e.dma_start(
                    out=xE[th][:, :].rearrange("p (c w) -> p c w", c=NCH),
                    in_=xT_r[:, :, HALO_L + th0 + th * NT: HALO_L + th0 + (th + 1) * NT]),
                    deps=[t_D_done])
            t_woT = {}
            t_sg_tok = {}
            t_outdone = {}
            for rc in range(NCH):
                wb = woT_bufs[rc % 4]
                t_woT[rc] = sch.dma("sp", lambda e, rc=rc, wb=wb: e.dma_start(
                    out=wb[:, :].rearrange("p (c w) -> p c w", c=NCH),
                    in_=wo_bf[rc * P:(rc + 1) * P, :], transpose=True),
                    deps=[t_wcast["wo"], t_sg_tok.get((rc - 4, 1))] +
                         ([t_D_done] if rc == 0 else []))
                for th in range(2):
                    par = (rc * 2 + th) % 6
                    so = soPS[par]
                    war = t_sg_tok.get((rc - 3, th))
                    t_so = None
                    for ci in range(NCH):
                        deps = [t_woT[rc], t_attn_ld]
                        if ci == 0:
                            deps.append(war)
                        t_so = sch.op("pe", lambda e, so=so, wb=wb, ci=ci, th=th, attnE=attnE: e.matmul(
                            so[:, :], wb[:, ci * P:(ci + 1) * P],
                            attnE[:, ci * TH2 + th * NT: ci * TH2 + (th + 1) * NT],
                            start=(ci == 0), stop=(ci == NCH - 1), skip_group_check=True),
                            deps=deps, inc=(ci == NCH - 1))
                    sil = siluE[(rc * 2 + th) % 4]
                    t_si = sch.op("act", lambda e, rc=rc, th=th, sil=sil, acc=acc: e.activation(
                        sil[:, :],
                        acc[:, rc * TH2 + th * NT: rc * TH2 + (th + 1) * NT],
                        AF.Silu, scale=dsc_sb[:, rc:rc + 1]),
                        deps=[t_evD[(NPAIR - 1, rc, th)], t_sg_tok.get((rc - 2, th))], inc=True)
                    t_sg = sch.op("dve", lambda e, so=so, rc=rc, th=th, sgE=sgE, sil=sil: e.scalar_tensor_tensor(
                        sgE[th][:, :], so[:, :], bo_sb[:, rc:rc + 1], sil[:, :],
                        ALU.add, ALU.mult), deps=[t_so, t_si], inc=True)
                    t_sg_tok[(rc, th)] = t_sg
                    if dbg:
                        t_sgd = sch.dma("sp", lambda e, rc=rc, th=th, sgE=sgE, th0=th0: e.dma_start(
                            out=sg_dbg.rearrange("(c p) t -> p c t", p=P)[
                                :, rc, th0 + th * NT: th0 + (th + 1) * NT],
                            in_=sgE[th][:, :]),
                            deps=[t_sg])
                        t_sg = t_sgd if False else t_sg
                        t_sg_tok[(rc, th)] = (t_sg_tok[(rc, th)], t_sgd)[0]
                    t_o = sch.op("dve", lambda e, rc=rc, th=th, xE=xE, sgE=sgE: e.tensor_tensor(
                        xE[th][:, rc * NT:(rc + 1) * NT], xE[th][:, rc * NT:(rc + 1) * NT],
                        sgE[th][:, :], ALU.add),
                        deps=[t_sg, t_x_ld[th]] + ([t_sgd] if dbg else []), inc=True)
                    t_outdone[(rc, th)] = t_o
            for th in range(2):
                t_st = sch.dma("sp", lambda e, th=th, xE=xE, th0=th0: e.dma_start(
                    out=outT_r[:, :, th0 + th * NT: th0 + (th + 1) * NT],
                    in_=xE[th][:, :].rearrange("p (c w) -> p c w", c=NCH)),
                    deps=[t_outdone[(NCH - 1, th)]])
                t_out_stores.append(t_st)
            e_es.close()
            acc_es.close()
            t_prev_half = [t_out_stores[-1], t_out_stores[-2]]

        sch.op("sp", lambda e: e.nop(), deps=t_out_stores + sch.all_tokens())

        block = es.enter_context(nc.Block())
        sch.emit(block)
    return nc


def kernel(L0_attn_x, L0_attn_wq, L0_attn_bq, L0_attn_wk, L0_attn_bk, L0_attn_wv,
           L0_attn_bv, L0_attn_q_norm_w, L0_attn_k_norm_w, L0_attn_score_mix_w,
           L0_attn_wo, L0_attn_bo, L0_ffn_gate_w, L0_ffn_gate_scale, L0_ffn_up_w,
           L0_ffn_up_scale, L0_ffn_down_w, L0_ffn_down_scale):
    x = np.asarray(L0_attn_x)[0]
    fvec = lambda v, nch: np.ascontiguousarray(np.asarray(v, np.float32).reshape(nch, P).T)
    vecs = np.zeros((P, NV_TOT), np.float32)
    fields = {"bq": fvec(L0_attn_bq, NCH), "bk": fvec(L0_attn_bk, NCH),
              "bv": fvec(L0_attn_bv, NCH), "bo": fvec(L0_attn_bo, NCH),
              "qnw": fvec(L0_attn_q_norm_w, NCH), "knw": fvec(L0_attn_k_norm_w, NCH),
              "dsc": fvec(L0_ffn_down_scale, NCH),
              "gsc": fvec(L0_ffn_gate_scale, I_FFN // P),
              "usc": fvec(L0_ffn_up_scale, I_FFN // P)}
    for nm, arr in fields.items():
        o, w = VOFF[nm]
        vecs[:, o:o + w] = arr
    o, w = VOFF["mixw"]
    vecs[0:1, o:o + 5] = np.asarray(L0_attn_score_mix_w, np.float32)
    shared = {
        "wq": np.ascontiguousarray(L0_attn_wq, np.float32),
        "wk": np.ascontiguousarray(L0_attn_wk, np.float32),
        "wv": np.ascontiguousarray(L0_attn_wv, np.float32),
        "wo": np.ascontiguousarray(L0_attn_wo, np.float32),
        "gate_w": np.ascontiguousarray(L0_ffn_gate_w, np.int32),
        "up_w": np.ascontiguousarray(L0_ffn_up_w, np.int32),
        "down_w": np.ascontiguousarray(L0_ffn_down_w, np.int32),
        "vecs": vecs,
    }
    in_maps = []
    for c in range(NCORES):
        t0 = c * T
        idx = (np.arange(-HALO_L, T + HALO_R) + t0) % L
        m = dict(shared)
        m["xT"] = np.ascontiguousarray(x[idx].T)
        in_maps.append(m)

    nc = build_program()
    res = run_bass_kernel_spmd(nc, in_maps, core_ids=list(range(NCORES)))
    out = np.empty((1, L, D), dtype=np.float32)
    for c in range(NCORES):
        out[0, c * T:(c + 1) * T, :] = np.asarray(res.results[c]["outT"]).T
    return out


# version 20 (final)
# speedup vs baseline: 1.1825x; 1.0093x over previous
import numpy as np
import ml_dtypes
from contextlib import ExitStack

import concourse.bass as bass
import concourse.mybir as mybir
from concourse.bass_utils import run_bass_kernel_spmd

F32 = mybir.dt.float32
BF16 = mybir.dt.bfloat16
I32 = mybir.dt.int32
AF = mybir.ActivationFunctionType
ALU = mybir.AluOpType
AX = mybir.AxisListType

D = 2048
NCH = 16
P = 128
L = 16384
NCORES = 8
T = 2048
HALO_L = 2048
HALO_R = 1024
E = HALO_L + T + HALO_R
NT = 512
NTE = E // NT
NTO = T // NT
I_FFN = 8192
ICH = 16
SCALE = 128.0 ** -0.5
CH_SHIFTS = [1, 2, 4, 8]
SEQ_SHIFTS = [0, 1, -1, 3, -3, 9, -9, 26, -26, 78, -78, 232, -232, 689, -689, 2048]
NS = 16
LN_EPS = 1e-5
RMS_EPS = 1e-6
TH2 = T // 2

_V_FIELDS = [("bq", NCH), ("bk", NCH), ("bv", NCH), ("bo", NCH), ("qnw", NCH),
             ("knw", NCH), ("dsc", NCH), ("gsc", I_FFN // P), ("usc", I_FFN // P),
             ("mixw", 5)]
VOFF = {}
_o = 0
for _n, _w in _V_FIELDS:
    VOFF[_n] = (_o, _w)
    _o += _w
NV_TOT = _o

ENGINES = ("pe", "act", "dve", "pool", "sp", "gq")
DMA_ENGINES = ("sp", "gq")
PHYS = {"pe": "pe", "act": "act", "dve": "dve", "pool": "gp", "sp": "sp", "gq": "gp"}


class Sched:
    def __init__(self, nc, es):
        self.nc = nc
        self.phys_streams = {"pe": [], "act": [], "dve": [], "sp": [], "gp": []}
        self.counts = {e: 0 for e in ENGINES}
        self.sems = {e: es.enter_context(nc.semaphore(f"sem_{e}")) for e in ENGINES}

    def op(self, eng, fn, deps=(), inc=False):
        tok = None
        if inc:
            self.counts[eng] += 16 if eng in DMA_ENGINES else 1
            tok = (eng, self.counts[eng])
        self.phys_streams[PHYS[eng]].append(
            (eng, fn, tuple(d for d in deps if d is not None), tok))
        return tok

    def dma(self, eng, fn, deps=()):
        return self.op(eng, fn, deps=deps, inc=True)

    def all_tokens(self):
        return [(e, self.counts[e]) for e in ENGINES if self.counts[e] > 0]

    def emit(self, block):
        sems = self.sems

        def make(ops):
            def body(e):
                hwm = {}
                for (src_eng, fn, deps, tok) in ops:
                    for (de, dv) in deps:
                        if hwm.get(de, 0) >= dv:
                            continue
                        e.wait_ge(sems[de], dv)
                        hwm[de] = dv
                    inst = fn(e)
                    if tok is not None:
                        amt = 16 if src_eng in DMA_ENGINES else 1
                        inst.then_inc(sems[src_eng], amt)
            return body

        decos = {"pe": block.tensor, "act": block.scalar, "dve": block.vector,
                 "sp": block.sync, "gp": block.gpsimd}
        for ph in ("sp", "gp", "pe", "act", "dve"):
            if self.phys_streams[ph]:
                decos[ph](make(self.phys_streams[ph]))


def build_program(dbg=False, stop_after=None):
    import os as _os
    CPART = int(_os.environ.get("CPART", "4"))
    nc = bass.Bass("TRN2", target_bir_lowering=False, debug=False)

    xT = nc.dram_tensor("xT", [D, E], F32, kind="ExternalInput").ap()
    wq = nc.dram_tensor("wq", [D, D], F32, kind="ExternalInput").ap()
    wk = nc.dram_tensor("wk", [D, D], F32, kind="ExternalInput").ap()
    wv = nc.dram_tensor("wv", [D, D], F32, kind="ExternalInput").ap()
    wo = nc.dram_tensor("wo", [D, D], F32, kind="ExternalInput").ap()
    gate_w = nc.dram_tensor("gate_w", [I_FFN, D], I32, kind="ExternalInput").ap()
    up_w = nc.dram_tensor("up_w", [I_FFN, D], I32, kind="ExternalInput").ap()
    down_w = nc.dram_tensor("down_w", [D, I_FFN], I32, kind="ExternalInput").ap()
    vecs_in = nc.dram_tensor("vecs", [P, NV_TOT], F32, kind="ExternalInput").ap()

    outT = nc.dram_tensor("outT", [D, T], F32, kind="ExternalOutput").ap()

    wq_bf = nc.dram_tensor("wq_bf", [D, D], BF16, kind="Internal").ap()
    wk_bf = nc.dram_tensor("wk_bf", [D, D], BF16, kind="Internal").ap()
    wv_bf = nc.dram_tensor("wv_bf", [D, D], BF16, kind="Internal").ap()
    wo_bf = nc.dram_tensor("wo_bf", [D, D], BF16, kind="Internal").ap()
    gate_bf = nc.dram_tensor("gate_bf", [I_FFN, D], BF16, kind="Internal").ap()
    up_bf = nc.dram_tensor("up_bf", [I_FFN, D], BF16, kind="Internal").ap()
    down_bf = nc.dram_tensor("down_bf", [D, I_FFN], BF16, kind="Internal").ap()

    spill_kind = "ExternalOutput" if dbg else "Internal"
    acc_dbg = nc.dram_tensor("acc_dbg", [D, T], F32, kind=spill_kind).ap() if dbg else None
    sg_dbg = nc.dram_tensor("sg_dbg", [D, T], F32, kind=spill_kind).ap() if dbg else None
    xn_sp = nc.dram_tensor("xn_sp", [D, E], BF16, kind=spill_kind).ap()
    k_sp = nc.dram_tensor("k_sp", [D, E], BF16, kind=spill_kind).ap()
    v_sp = nc.dram_tensor("v_sp", [D, E], BF16, kind=spill_kind).ap()
    qm_sp = nc.dram_tensor("qm_sp", [D, T], BF16, kind=spill_kind).ap()
    at_sp = nc.dram_tensor("at_sp", [D, T], BF16, kind=spill_kind).ap()
    h_sp = nc.dram_tensor("h_sp", [I_FFN, T], BF16, kind=spill_kind).ap()

    xT_r = xT.rearrange("(c p) j -> p c j", p=P)
    xn_r = xn_sp.rearrange("(c p) j -> p c j", p=P)
    k_r = k_sp.rearrange("(c p) j -> p c j", p=P)
    v_r = v_sp.rearrange("(c p) j -> p c j", p=P)
    qm_r = qm_sp.rearrange("(c p) j -> p c j", p=P)
    at_r = at_sp.rearrange("(c p) j -> p c j", p=P)
    h_r = h_sp.rearrange("(a p) t -> p a t", p=P)
    outT_r = outT.rearrange("(c p) t -> p c t", p=P)

    NSP = NS * P
    cb = np.zeros((P, 128 + 1 + 128 + 256 + 2048 + 16 + 512 + 33), dtype=ml_dtypes.bfloat16)
    off = {}
    o = 0
    def put_b(name, arr):
        global_o = None
        nonlocal o
        h, w = arr.shape
        cb[0:h, o:o + w] = arr
        off[name] = (o, w)
        o += w
    put_b("id128b", np.eye(P, dtype=ml_dtypes.bfloat16))
    put_b("ones_col_b", np.ones((P, 1), ml_dtypes.bfloat16))
    put_b("ones_row_b", np.ones((1, P), ml_dtypes.bfloat16))
    ec_np = np.zeros((P, NS * NS), dtype=ml_dtypes.bfloat16)
    for s in range(NS):
        ec_np[:, s * NS + s] = 1.0
    put_b("ecols", ec_np)
    er_np = np.zeros((NS, NS * P), dtype=ml_dtypes.bfloat16)
    for s in range(NS):
        er_np[s, s * P:(s + 1) * P] = 1.0
    put_b("erows", er_np)
    e4c_np = np.zeros((P, 16), dtype=ml_dtypes.bfloat16)
    for t4 in range(4):
        e4c_np[:, t4 * 4 + t4] = 1.0
    put_b("e4c", e4c_np)
    e4r_np = np.zeros((4, 4 * P), dtype=ml_dtypes.bfloat16)
    for t4 in range(4):
        e4r_np[t4, t4 * P:(t4 + 1) * P] = 1.0
    put_b("e4r", e4r_np)
    c32 = np.zeros((P, 33), dtype=ml_dtypes.bfloat16)
    c32[:, 32] = 1.0
    put_b("col32b", c32)
    consts_b = nc.inline_tensor(cb, "consts_b").ap()

    cf = np.zeros((P, 16 + 1 + 128 + 5 * P), dtype=np.float32)
    off_f = {}
    o = 0
    def put_f(name, arr):
        nonlocal o
        h, w = arr.shape
        cf[0:h, o:o + w] = arr
        off_f[name] = (o, w)
        o += w
    put_f("id16f", np.eye(16, dtype=np.float32))
    put_f("ones_col_f", np.ones((P, 1), np.float32))
    put_f("ones_row_f", np.ones((1, P), np.float32))
    pm_np = np.zeros((P, 5 * P), dtype=np.float32)
    for n, c in enumerate([0] + CH_SHIFTS):
        pm_np[:, n * P:(n + 1) * P] = np.roll(np.eye(P, dtype=np.float32), c, axis=1) * SCALE
    put_f("pmix", pm_np)
    consts_f = nc.inline_tensor(cf, "consts_f").ap()

    es = ExitStack()
    with es:
        sch = Sched(nc, es)
        sb = lambda n_, s_, d_: es.enter_context(nc.sbuf_tensor(n_, s_, d_))

        def _finish():
            sch.op("sp", lambda e: e.nop(), deps=sch.all_tokens())
            block = es.enter_context(nc.Block())
            sch.emit(block)

        cbsb = sb("cbsb", [P, cb.shape[1]], BF16)
        cfsb = sb("cfsb", [P, cf.shape[1]], F32)
        vecs_sb = sb("vecs_sb", [P, NV_TOT], F32)
        mixb_sb = sb("mixb_sb", [P, 5], F32)
        mmix_sb = sb("mmix_sb", [P, P], F32)
        hsumneg_sb = sb("hsumneg_sb", [4, NT], BF16)

        def cslice(name, h):
            o, w = off[name]
            return cbsb[0:h, o:o + w]

        def fslice(name, h):
            o, w = off_f[name]
            return cfsb[0:h, o:o + w]

        id128b_sb = cslice("id128b", P)
        onescb_sb = cslice("ones_col_b", P)
        onesrb_sb = cslice("ones_row_b", 1)
        ecols_sb = cslice("ecols", P)
        erows_sb = cslice("erows", NS)
        e4c_sb = cslice("e4c", P)
        e4r_sb = cslice("e4r", 4)
        col32b_sb = cslice("col32b", P)
        id16f_sb = fslice("id16f", 16)
        onescf_sb = fslice("ones_col_f", P)
        onesrf_sb = fslice("ones_row_f", 1)
        pmix_sb = fslice("pmix", P)

        def vslice(name):
            o, w = VOFF[name]
            return vecs_sb[:, o:o + w]

        bq_sb = vslice("bq"); bk_sb = vslice("bk"); bv_sb = vslice("bv")
        bo_sb = vslice("bo"); qnw_sb = vslice("qnw"); knw_sb = vslice("knw")
        dsc_sb = vslice("dsc"); gsc_sb = vslice("gsc"); usc_sb = vslice("usc")
        mixw_sb = vecs_sb[0:1, VOFF["mixw"][0]:VOFF["mixw"][0] + 5]

        t1 = sch.dma("sp", lambda e: e.dma_start(out=cbsb[:, :], in_=consts_b))
        t2 = sch.dma("sp", lambda e: e.dma_start(out=cfsb[:, :], in_=consts_f))
        t3 = sch.dma("sp", lambda e: e.dma_start(out=vecs_sb[:, :], in_=vecs_in))
        t_consts = t3

        t_wcast = {}
        _wcast_src = {"wk": (wk_bf, wk), "wv": (wv_bf, wv), "wq": (wq_bf, wq),
                      "gate": (gate_bf, gate_w), "up": (up_bf, up_w),
                      "down": (down_bf, down_w), "wo": (wo_bf, wo)}

        def emit_wcast(nm, nchunks=1, pace=()):
            d, s = _wcast_src[nm]
            R = d.shape[0]
            step = R // nchunks
            tok = None
            for i in range(nchunks):
                dep = [pace[i]] if i < len(pace) else []
                tok = sch.dma("gq", lambda e, d=d, s=s, i=i, step=step: e.dma_start(
                    out=d[i * step:(i + 1) * step, :], in_=s[i * step:(i + 1) * step, :]),
                    deps=dep)
            t_wcast[nm] = tok

        psm_es = ExitStack()
        ps_small = psm_es.enter_context(nc.psum_tensor("ps_small", [P, 8], F32))
        t_mixbc = sch.op("pe", lambda e: e.matmul(ps_small[:, 0:5], onesrf_sb[:, :],
                                                  mixw_sb[:, :], start=True, stop=True),
                         deps=[t_consts], inc=True)
        t_mixb = sch.op("dve", lambda e: e.tensor_copy(mixb_sb[:, :], ps_small[:, 0:5]),
                        deps=[t_mixbc], inc=True)
        t_mm = sch.op("dve", lambda e: e.tensor_scalar(mmix_sb[:, :], pmix_sb[:, 0:P],
                                                       mixb_sb[:, 0:1], None, ALU.mult),
                      deps=[t_mixb], inc=True)
        for n in range(1, 5):
            t_mm = sch.op("dve", lambda e, n=n: e.scalar_tensor_tensor(
                mmix_sb[:, :], pmix_sb[:, n * P:(n + 1) * P], mixb_sb[:, n:n + 1],
                mmix_sb[:, :], ALU.mult, ALU.add), deps=[t_mm], inc=True)
        psm_es.close()

        def load_weightT(w_bf_ap, wT_sb, nch_out, start_deps):
            return sch.dma("sp", lambda e: e.dma_start(
                out=wT_sb[:, :].rearrange("p (c w) -> p c w", c=nch_out),
                in_=w_bf_ap, transpose=True), deps=list(start_deps))

        a1_es = ExitStack()
        if True:
            wkT = a1_es.enter_context(nc.sbuf_tensor("wkT", [P, NCH * D], BF16))
            xA = a1_es.enter_context(nc.sbuf_tensor("xA", [P, NCH * NT], F32))
            xB = a1_es.enter_context(nc.sbuf_tensor("xB", [P, NCH * NT], F32))
            xntA = a1_es.enter_context(nc.sbuf_tensor("xntA", [P, NCH * NT], BF16))
            xntB = a1_es.enter_context(nc.sbuf_tensor("xntB", [P, NCH * NT], BF16))
            kbf = a1_es.enter_context(nc.sbuf_tensor("kbf", [P, NCH * NT], BF16))
            sqbf = a1_es.enter_context(nc.sbuf_tensor("sqbf", [P, 4 * NT], BF16))
            tf32 = a1_es.enter_context(nc.sbuf_tensor("tf32", [P, NT], F32))
            rows = a1_es.enter_context(nc.sbuf_tensor("rows", [1, 6 * NT], F32))
            lnst = a1_es.enter_context(nc.psum_tensor("lnst", [P, NT], F32))
            lnsq2 = a1_es.enter_context(nc.psum_tensor("lnsq2", [1, NT], F32))
            bca = a1_es.enter_context(nc.psum_tensor("bca", [P, NT], F32))
            bcb = a1_es.enter_context(nc.psum_tensor("bcb", [P, NT], F32))
            mma = a1_es.enter_context(nc.psum_tensor("mma", [P, NT], F32))
            mmb = a1_es.enter_context(nc.psum_tensor("mmb", [P, NT], F32))
            bck = a1_es.enter_context(nc.psum_tensor("bck", [P, NT], F32))
            ksq = a1_es.enter_context(nc.psum_tensor("ksq", [1, NT], F32))

            t_xpre = []
            for te0 in range(2):
                xb0 = (xA, xB)[te0 % 2]
                t_xpre.append(sch.dma("sp", lambda e, xb=xb0, j0=te0 * NT: e.dma_start(
                    out=xb[:, :].rearrange("p (c w) -> p c w", c=NCH),
                    in_=xT_r[:, :, j0:j0 + NT])))
            emit_wcast("wk")
            emit_wcast("wv")
            emit_wcast("wq")
            t_wk = load_weightT(wk_bf, wkT, NCH, [t_wcast["wk"]])

            xbufs = (xA, xB)
            xnbufs1 = (xntA, xntB)
            t_xload = [None] * NTE
            t_tile_done = [None] * NTE
            t_kn_last = [None] * NTE
            t_spill_xn = [None] * NTE
            t_spill_k = [None] * NTE
            t_bc_consumed = None
            t_mean_prev = None
            t_ex2_prev = None
            t_m2_prev = None
            t_uk_prev = None
            t_ksq_war = {}
            t_sq_war_prev = {}
            ROW_MEAN, ROW_EX2, ROW_VAR, ROW_RSTD, ROW_MR, ROW_U = 0, 1, 2, 3, 4, 5

            def r(i):
                return rows[0:1, i * NT:(i + 1) * NT]

            mm_ps = (mma, mmb)
            for te in range(NTE):
                xb = xbufs[te % 2]
                xnt = xnbufs1[te % 2]
                j0 = te * NT
                if te < 2:
                    t_xload[te] = t_xpre[te]
                else:
                    t_xload[te] = sch.dma("sp", lambda e, xb=xb, j0=j0: e.dma_start(
                        out=xb[:, :].rearrange("p (c w) -> p c w", c=NCH),
                        in_=xT_r[:, :, j0:j0 + NT]),
                        deps=[t_tile_done[te - 2]])
                t_sum = None
                for ci in range(NCH):
                    t_sum = sch.op("pe", lambda e, xb=xb, ci=ci: e.matmul(
                        lnst[0:1, :], onescf_sb[:, :], xb[:, ci * NT:(ci + 1) * NT],
                        start=(ci == 0), stop=(ci == NCH - 1), skip_group_check=True),
                        deps=[t_xload[te]] + ([t_mean_prev, t_ex2_prev] if ci == 0 else []),
                        inc=(ci == NCH - 1))
                t_sq_tok = {}
                t_sumsq_tok = dict(t_sq_war_prev)
                for ci in range(NCH):
                    half = ci % 2
                    t_sq_tok[ci] = sch.op("act", lambda e, xb=xb, ci=ci, half=half: e.activation(
                        sqbf[:, half * NT:(half + 1) * NT], xb[:, ci * NT:(ci + 1) * NT],
                        AF.Square),
                        deps=[t_xload[te], t_sumsq_tok.get(ci - 2)], inc=True)
                    t_sumsq_tok[ci] = sch.op("pe", lambda e, ci=ci, half=half: e.matmul(
                        lnsq2[:, :], onescb_sb[:, :], sqbf[:, half * NT:(half + 1) * NT],
                        start=(ci == 0), stop=(ci == NCH - 1), skip_group_check=True),
                        deps=[t_sq_tok[ci]] + ([t_ex2_prev] if ci == 0 else []), inc=True)
                t_sumsq = t_sumsq_tok[NCH - 1]
                t_mean = sch.op("dve", lambda e: e.tensor_scalar(
                    r(ROW_MEAN), lnst[0:1, :], 1.0 / D, None, ALU.mult),
                    deps=[t_sum, t_sumsq, t_m2_prev], inc=True)
                t_mean_prev = t_mean
                t_ex2 = sch.op("dve", lambda e: e.tensor_scalar(
                    r(ROW_EX2), lnsq2[:, :], 1.0 / D, None, ALU.mult),
                    deps=[t_sumsq], inc=True)
                t_ex2_prev = t_ex2
                t_m2 = sch.op("act", lambda e: e.activation(
                    r(ROW_U), r(ROW_MEAN), AF.Square), deps=[t_mean], inc=True)
                t_m2_prev = t_m2
                t_var = sch.op("dve", lambda e: e.scalar_tensor_tensor(
                    r(ROW_VAR), r(ROW_EX2), LN_EPS, r(ROW_U), ALU.add, ALU.subtract),
                    deps=[t_m2, t_ex2], inc=True)
                t_sd = sch.op("act", lambda e: e.activation(
                    r(ROW_U), r(ROW_VAR), AF.Sqrt), deps=[t_var], inc=True)
                t_rstd = sch.op("dve", lambda e: e.reciprocal(r(ROW_RSTD), r(ROW_U)),
                                deps=[t_sd], inc=True)
                t_mr = sch.op("dve", lambda e: e.tensor_mul(
                    r(ROW_MR), r(ROW_MEAN), r(ROW_RSTD)), deps=[t_rstd], inc=True)
                t_bc_r = sch.op("pe", lambda e: e.matmul(
                    bca[:, :], onesrf_sb[:, :], r(ROW_RSTD), start=True, stop=True),
                    deps=[t_rstd, t_bc_consumed], inc=True)
                t_bc_m = sch.op("pe", lambda e: e.matmul(
                    bcb[:, :], onesrf_sb[:, :], r(ROW_MR), start=True, stop=True),
                    deps=[t_mr, t_bc_consumed], inc=True)
                t_xn = None
                for ci in range(NCH):
                    extra = []
                    if ci == 0 and te > 1:
                        extra = [t_kn_last[te - 2], t_spill_xn[te - 2]]
                    sch.op("dve", lambda e, xb=xb, ci=ci: e.tensor_tensor(
                        tf32[:, :], xb[:, ci * NT:(ci + 1) * NT], bca[:, :], ALU.mult),
                        deps=[t_bc_r] + extra)
                    t_xn = sch.op("dve", lambda e, ci=ci, xnt=xnt: e.tensor_tensor(
                        xnt[:, ci * NT:(ci + 1) * NT], tf32[:, :], bcb[:, :], ALU.subtract),
                        deps=[t_bc_m], inc=True)
                t_tile_done[te] = t_xn
                t_bc_consumed = t_xn
                t_spill_xn[te] = sch.dma("gq", lambda e, j0=j0, xnt=xnt: e.dma_start(
                    out=xn_r[:, :, j0:j0 + NT],
                    in_=xnt[:, :].rearrange("p (c w) -> p c w", c=NCH)), deps=[t_xn])
                t_evac_mi = {}
                t_ksq_tok = {}
                t_sqk_tok = {}
                t_ksum = None
                for mi in range(NCH):
                    ps = mm_ps[mi % 2]
                    t_mmk = None
                    for ci in range(NCH):
                        t_mmk = sch.op("pe", lambda e, ps=ps, mi=mi, ci=ci, xnt=xnt: e.matmul(
                            ps[:, :], wkT[:, ci * D + mi * P: ci * D + mi * P + P],
                            xnt[:, ci * NT:(ci + 1) * NT],
                            start=(ci == 0), stop=(ci == NCH - 1), skip_group_check=True),
                            deps=[t_xn, t_wk, t_evac_mi.get(mi - 2)],
                            inc=(ci == NCH - 1))
                    extra = [t_spill_k[te - 1]] if (mi == 0 and te > 0) else []
                    t_ev = sch.op("dve", lambda e, ps=ps, mi=mi: e.tensor_scalar(
                        kbf[:, mi * NT:(mi + 1) * NT], ps[:, :], bk_sb[:, mi:mi + 1],
                        None, ALU.add), deps=[t_mmk] + extra, inc=True)
                    t_evac_mi[mi] = t_ev
                    half = mi % 4
                    if mi >= 4:
                        war_sq = t_ksq_tok.get(mi - 4)
                    elif mi < 2:
                        war_sq = t_sumsq_tok.get(NCH - 2 + mi)
                    else:
                        war_sq = t_sumsq_tok.get(NCH - 1)
                    t_sqk = sch.op("act", lambda e, mi=mi, half=half: e.activation(
                        sqbf[:, half * NT:(half + 1) * NT], kbf[:, mi * NT:(mi + 1) * NT],
                        AF.Square),
                        deps=[t_ev, war_sq], inc=True)
                    t_sqk_tok[mi] = t_sqk
                    if mi >= 2:
                        dk = mi - 2
                        t_ksq_tok[dk] = sch.op("pe", lambda e, dk=dk: e.matmul(
                            ksq[:, :], onescb_sb[:, :], sqbf[:, (dk % 4) * NT:((dk % 4) + 1) * NT],
                            start=(dk == 0), stop=False, skip_group_check=True),
                            deps=[t_sqk_tok[dk]] + ([t_uk_prev] if dk == 0 else []), inc=True)
                for dk in (NCH - 2, NCH - 1):
                    t_ksq_tok[dk] = sch.op("pe", lambda e, dk=dk: e.matmul(
                        ksq[:, :], onescb_sb[:, :], sqbf[:, (dk % 4) * NT:((dk % 4) + 1) * NT],
                        start=False, stop=(dk == NCH - 1), skip_group_check=True),
                        deps=[t_sqk_tok[dk]], inc=True)
                t_ksum = t_ksq_tok[NCH - 1]
                t_uk = sch.op("dve", lambda e: e.tensor_scalar(
                    r(ROW_U), ksq[:, :], 1.0 / D, RMS_EPS, ALU.mult, ALU.add),
                    deps=[t_ksum], inc=True)
                t_uk_prev = t_uk
                t_sdk = sch.op("act", lambda e: e.activation(
                    r(ROW_U), r(ROW_U), AF.Sqrt), deps=[t_uk], inc=True)
                t_rsk = sch.op("dve", lambda e: e.reciprocal(r(ROW_RSTD), r(ROW_U)),
                               deps=[t_sdk], inc=True)
                t_rows_prev = t_rsk
                t_bck = sch.op("pe", lambda e: e.matmul(
                    bck[:, :], onesrf_sb[:, :], r(ROW_RSTD), start=True, stop=True),
                    deps=[t_rsk, t_kn_last[te - 1] if te > 0 else None], inc=True)
                t_kn = None
                for mi in range(NCH):
                    t_kn = sch.op("dve", lambda e, mi=mi: e.scalar_tensor_tensor(
                        kbf[:, mi * NT:(mi + 1) * NT], kbf[:, mi * NT:(mi + 1) * NT],
                        knw_sb[:, mi:mi + 1], bck[:, :], ALU.mult, ALU.mult),
                        deps=[t_bck], inc=(mi == NCH - 1))
                t_kn_last[te] = t_kn
                t_sq_war_prev = {-2: t_ksq_tok[NCH - 4], -1: t_ksq_tok[NCH - 3]}
                t_spill_k[te] = sch.dma("gq", lambda e, j0=j0: e.dma_start(
                    out=k_r[:, :, j0:j0 + NT],
                    in_=kbf[:, :].rearrange("p (c w) -> p c w", c=NCH)), deps=[t_kn])
            tA1_done = [t_spill_xn[-1], t_spill_k[-1]]
        a1_es.close()
        if stop_after == "A1":
            _finish()
            return nc

        a2_es = ExitStack()
        if True:
            wvT = a2_es.enter_context(nc.sbuf_tensor("wvT", [P, NCH * D], BF16))
            xn2s = [a2_es.enter_context(nc.sbuf_tensor(f"xn2_{k}", [P, NCH * NT], BF16))
                    for k in range(4)]
            vbf = a2_es.enter_context(nc.sbuf_tensor("vbf", [P, NCH * NT], BF16))
            mma2 = a2_es.enter_context(nc.psum_tensor("mma2", [P, NT], F32))
            mmb2 = a2_es.enter_context(nc.psum_tensor("mmb2", [P, NT], F32))
            mmc2 = a2_es.enter_context(nc.psum_tensor("mmc2", [P, NT], F32))
            mmd2 = a2_es.enter_context(nc.psum_tensor("mmd2", [P, NT], F32))

            emit_wcast("gate")
            emit_wcast("up")
            t_wv = load_weightT(wv_bf, wvT, NCH, [t_wcast["wv"]] + tA1_done)
            xnbufs = xn2s
            t_ld = [None] * NTE
            t_done = [None] * NTE
            t_spill_v = [None] * NTE
            mm_ps2 = (mma2, mmb2, mmc2, mmd2)
            t_ev2 = {}
            for te in range(NTE):
                xb = xnbufs[te % 4]
                j0 = te * NT
                t_ld[te] = sch.dma("sp", lambda e, xb=xb, j0=j0: e.dma_start(
                    out=xb[:, :].rearrange("p (c w) -> p c w", c=NCH),
                    in_=xn_r[:, :, j0:j0 + NT]),
                    deps=([t_done[te - 4]] if te >= 4 else tA1_done))
                t_vn = None
                for mi in range(NCH):
                    ps = mm_ps2[mi % 4]
                    war = t_ev2.get((te, mi - 4)) if mi >= 4 else t_ev2.get((te - 1, NCH - 4 + mi))
                    t_mmv = None
                    for ci in range(NCH):
                        t_mmv = sch.op("pe", lambda e, ps=ps, mi=mi, ci=ci, xb=xb: e.matmul(
                            ps[:, :], wvT[:, ci * D + mi * P: ci * D + mi * P + P],
                            xb[:, ci * NT:(ci + 1) * NT],
                            start=(ci == 0), stop=(ci == NCH - 1), skip_group_check=True),
                            deps=[t_ld[te], t_wv, war], inc=(ci == NCH - 1))
                    extra = [t_spill_v[te - 1]] if (mi == 0 and te > 0) else []
                    t_vn = sch.op("dve", lambda e, ps=ps, mi=mi: e.tensor_scalar(
                        vbf[:, mi * NT:(mi + 1) * NT], ps[:, :], bv_sb[:, mi:mi + 1],
                        None, ALU.add), deps=[t_mmv] + extra, inc=True)
                    t_ev2[(te, mi)] = t_vn
                t_done[te] = t_vn
                t_spill_v[te] = sch.dma("gq", lambda e, j0=j0: e.dma_start(
                    out=v_r[:, :, j0:j0 + NT],
                    in_=vbf[:, :].rearrange("p (c w) -> p c w", c=NCH)), deps=[t_vn])
            tA2_done = [t_spill_v[-1]]
        a2_es.close()
        if stop_after == "A2":
            _finish()
            return nc

        a3_es = ExitStack()
        if True:
            wqT = a3_es.enter_context(nc.sbuf_tensor("wqT", [P, NCH * D], BF16))
            xn3s = [a3_es.enter_context(nc.sbuf_tensor(f"xn3_{k}", [P, NCH * NT], BF16))
                    for k in range(4)]
            qbf = a3_es.enter_context(nc.sbuf_tensor("qbf", [P, NCH * NT], BF16))
            qmx = a3_es.enter_context(nc.sbuf_tensor("qmx", [P, NCH * NT], BF16))
            qnf = a3_es.enter_context(nc.sbuf_tensor("qnf", [P, NT], F32))
            sqbf3 = a3_es.enter_context(nc.sbuf_tensor("sqbf3", [P, 2 * NT], BF16))
            rows3 = a3_es.enter_context(nc.sbuf_tensor("rows3", [1, 2 * NT], F32))
            mma3 = a3_es.enter_context(nc.psum_tensor("mma3", [P, NT], F32))
            mmb3 = a3_es.enter_context(nc.psum_tensor("mmb3", [P, NT], F32))
            mmc3 = a3_es.enter_context(nc.psum_tensor("mmc3", [P, NT], F32))
            bca3 = a3_es.enter_context(nc.psum_tensor("bca3", [P, NT], F32))
            qsq = a3_es.enter_context(nc.psum_tensor("qsq", [1, NT], F32))

            t_wq = load_weightT(wq_bf, wqT, NCH, [t_wcast["wq"]] + tA2_done)
            xnbufs3 = xn3s
            t_ld3 = [None] * NTO
            t_done3 = [None] * NTO
            t_spill_q = [None] * NTO
            mm_ps3 = (mma3, mmb3, mmc3)
            rq = lambda i: rows3[0:1, i * NT:(i + 1) * NT]
            t_bcq_prev = None
            for to in range(NTO):
                xb = xnbufs3[to % 4]
                j0 = HALO_L + to * NT
                t_ld3[to] = sch.dma("sp", lambda e, xb=xb, j0=j0: e.dma_start(
                    out=xb[:, :].rearrange("p (c w) -> p c w", c=NCH),
                    in_=xn_r[:, :, j0:j0 + NT]),
                    deps=tA2_done)
                t_ev3 = {}
                t_qsq_tok = {}
                for mi in range(NCH):
                    ps = mm_ps3[mi % 3]
                    war = t_ev3.get(mi - 3) if mi >= 3 else (t_done3[to - 1] if to > 0 else None)
                    t_mmq = None
                    for ci in range(NCH):
                        t_mmq = sch.op("pe", lambda e, ps=ps, mi=mi, ci=ci, xb=xb: e.matmul(
                            ps[:, :], wqT[:, ci * D + mi * P: ci * D + mi * P + P],
                            xb[:, ci * NT:(ci + 1) * NT],
                            start=(ci == 0), stop=(ci == NCH - 1), skip_group_check=True),
                            deps=[t_ld3[to], t_wq, war], inc=(ci == NCH - 1))
                    extra = [t_spill_q[to - 1]] if (mi == 0 and to > 0) else []
                    t_evq = sch.op("dve", lambda e, ps=ps, mi=mi: e.tensor_scalar(
                        qbf[:, mi * NT:(mi + 1) * NT], ps[:, :], bq_sb[:, mi:mi + 1],
                        None, ALU.add), deps=[t_mmq] + extra, inc=True)
                    t_ev3[mi] = t_evq
                    half = mi % 2
                    t_sqq = sch.op("act", lambda e, mi=mi, half=half: e.activation(
                        sqbf3[:, half * NT:(half + 1) * NT], qbf[:, mi * NT:(mi + 1) * NT],
                        AF.Square), deps=[t_evq, t_qsq_tok.get(mi - 2)], inc=True)
                    t_qsq_tok[mi] = sch.op("pe", lambda e, mi=mi, half=half: e.matmul(
                        qsq[:, :], onescb_sb[:, :], sqbf3[:, half * NT:(half + 1) * NT],
                        start=(mi == 0), stop=(mi == NCH - 1), skip_group_check=True),
                        deps=[t_sqq], inc=True)
                t_uq = sch.op("dve", lambda e: e.tensor_scalar(
                    rq(0), qsq[:, :], 1.0 / D, RMS_EPS, ALU.mult, ALU.add),
                    deps=[t_qsq_tok[NCH - 1]], inc=True)
                t_sdq = sch.op("act", lambda e: e.activation(rq(0), rq(0), AF.Sqrt),
                               deps=[t_uq], inc=True)
                t_rsq = sch.op("dve", lambda e: e.reciprocal(rq(1), rq(0)),
                               deps=[t_sdq], inc=True)
                t_bcq = sch.op("pe", lambda e: e.matmul(
                    bca3[:, :], onesrf_sb[:, :], rq(1), start=True, stop=True),
                    deps=[t_rsq, t_bcq_prev], inc=True)
                t_qm = None
                t_qmm_prev = None
                for mi in range(NCH):
                    ps = mm_ps3[mi % 3]
                    t_qn = sch.op("dve", lambda e, mi=mi: e.scalar_tensor_tensor(
                        qnf[:, :], qbf[:, mi * NT:(mi + 1) * NT], qnw_sb[:, mi:mi + 1],
                        bca3[:, :], ALU.mult, ALU.mult),
                        deps=[t_bcq, t_qmm_prev], inc=True)
                    t_qmm = sch.op("pe", lambda e, ps=ps: e.matmul(
                        ps[:, :], mmix_sb[:, :], qnf[:, :], start=True, stop=True),
                        deps=[t_qn, t_mm], inc=True)
                    t_qmm_prev = t_qmm
                    extra = [t_spill_q[to - 1]] if (mi == 0 and to > 0) else []
                    t_qm = sch.op("dve", lambda e, ps=ps, mi=mi: e.tensor_copy(
                        qmx[:, mi * NT:(mi + 1) * NT], ps[:, :]),
                        deps=[t_qmm] + extra, inc=True)
                t_bcq_prev = t_qm
                t_done3[to] = t_qm
                t_spill_q[to] = sch.dma("gq", lambda e, j0=j0: e.dma_start(
                    out=qm_r[:, :, j0 - HALO_L:j0 - HALO_L + NT],
                    in_=qmx[:, :].rearrange("p (c w) -> p c w", c=NCH)), deps=[t_qm])
            tA3_done = [t_spill_q[-1]]
        a3_es.close()
        if stop_after == "A3":
            _finish()
            return nc

        bc_es = ExitStack()
        if True:
            xno = bc_es.enter_context(nc.sbuf_tensor("xno", [P, NCH * T], BF16))
            wgT = bc_es.enter_context(nc.sbuf_tensor("wgT", [P, NCH * NT], BF16))
            wuT = bc_es.enter_context(nc.sbuf_tensor("wuT", [P, NCH * NT], BF16))
            hbfA = bc_es.enter_context(nc.sbuf_tensor("hbfA", [P, 4 * T], BF16))
            hbfB = bc_es.enter_context(nc.sbuf_tensor("hbfB", [P, 4 * T], BF16))
            csneg = bc_es.enter_context(nc.sbuf_tensor("csneg", [1, T], BF16))
            siluCA = bc_es.enter_context(nc.sbuf_tensor("siluCA", [P, NT], F32))
            siluCB = siluCA
            KE = 4864
            kh1 = bc_es.enter_context(nc.sbuf_tensor("kh1", [P, KE], BF16))
            khs = [kh1, kh1]
            vhs = [bc_es.enter_context(nc.sbuf_tensor(f"vh{k}", [P, KE], BF16)) for k in range(2)]
            qh1 = bc_es.enter_context(nc.sbuf_tensor("qh1", [P, T], BF16))
            qhs = [qh1, qh1]
            acch = bc_es.enter_context(nc.sbuf_tensor("acch", [P, T], F32))
            tmpPA = bc_es.enter_context(nc.sbuf_tensor("tmpPA", [P, NS * NT], BF16))
            tmpPB = tmpPA
            scsb = bc_es.enter_context(nc.sbuf_tensor("scsb", [16, NT], F32))
            ptm = bc_es.enter_context(nc.sbuf_tensor("ptm", [P, 64], BF16))
            pfm = bc_es.enter_context(nc.sbuf_tensor("pfm", [16, NT], BF16))
            nmax = bc_es.enter_context(nc.sbuf_tensor("nmax", [P, 4], F32))
            esum = bc_es.enter_context(nc.sbuf_tensor("esum", [P, 8], F32))
            expv = bc_es.enter_context(nc.sbuf_tensor("expv", [P, 64], F32))
            prodA = bc_es.enter_context(nc.sbuf_tensor("prodA", [P, NT], BF16))
            prodB = bc_es.enter_context(nc.sbuf_tensor("prodB", [P, NT], BF16))

            emit_wcast("down")
            emit_wcast("wo")
            t_xno = sch.dma("sp", lambda e: e.dma_start(
                out=xno[:, :].rearrange("p (c w) -> p c w", c=NCH),
                in_=xn_r[:, :, HALO_L:HALO_L + T]), deps=tA3_done)
            cs_es = ExitStack()
            csps = cs_es.enter_context(nc.psum_tensor("csps", [1, NT], F32))
            t_cs = [None] * NTO
            for to in range(NTO):
                t_c = None
                for ci in range(NCH):
                    t_c = sch.op("pe", lambda e, ci=ci, to=to: e.matmul(
                        csps[:, :], onescb_sb[:, :],
                        xno[:, ci * T + to * NT: ci * T + (to + 1) * NT],
                        start=(ci == 0), stop=(ci == NCH - 1), skip_group_check=True),
                        deps=[t_xno] + ([t_cs[to - 1]] if to > 0 else []),
                        inc=(ci == NCH - 1))
                t_cs[to] = sch.op("dve", lambda e, to=to: e.tensor_scalar(
                    csneg[0:1, to * NT:(to + 1) * NT], csps[:, :], -128.0, None, ALU.mult),
                    deps=[t_c], inc=True)
            t_cs_done = t_cs[-1]
            cs_es.close()

            gpsA = bc_es.enter_context(nc.psum_tensor("gpsA", [P, NT], F32))
            upsA = bc_es.enter_context(nc.psum_tensor("upsA", [P, NT], F32))
            upsB = bc_es.enter_context(nc.psum_tensor("upsB", [P, NT], F32))
            scps = bc_es.enter_context(nc.psum_tensor("scps", [16, NT], F32))
            tmps = bc_es.enter_context(nc.psum_tensor("tmps", [P, 64], F32))
            pfps = bc_es.enter_context(nc.psum_tensor("pfps", [16, NT], BF16))
            pbcA = bc_es.enter_context(nc.psum_tensor("pbcA", [P, NT], F32))
            pbcB = bc_es.enter_context(nc.psum_tensor("pbcB", [P, NT], F32))

            n_ic = ICH
            t_scevac = {}
            t_stack = {}
            t_soft_prev = None
            t_pfevac_prev = None
            t_mult = {}
            t_add = {}
            t_slab = {}
            t_asp = {}
            t_exp_tok = {}
            t_trdone = {}
            t_mm_last = {}
            t_si_tok = {}
            t_hh_tok = {}
            t_h = {}
            t_hspill = [None] * ICH
            t_hsum_last = None

            def emit_b_head(it):
                nonlocal t_soft_prev
                t_sce = sch.op("dve", lambda e: e.tensor_copy(scsb[:, :], scps[:, :]),
                               deps=[t_stack[it]], inc=True)
                t_scevac[it] = t_sce
                t_tr = None
                for i4 in range(4):
                    extra = [t_soft_prev] if i4 == 0 else []
                    t_tr = sch.op("pe", lambda e, i4=i4: e.transpose(
                        tmps[:, i4 * 16:(i4 + 1) * 16], scsb[:, i4 * P:(i4 + 1) * P],
                        id16f_sb[:, :]), deps=[t_sce] + extra, inc=(i4 == 3))
                return t_tr

            t_pr_last = {}

            def emit_b_products(it):
                hh = it // NTO
                to = it % NTO
                tb = (tmpPA, tmpPB)[it % 2]
                kh = khs[hh % 2]
                qh = qhs[hh % 2]
                pj = HALO_L + to * NT
                t_pr = None
                for si, s in enumerate(SEQ_SHIFTS):
                    extra = [t_stack.get(it - 1)] if si == 0 else []
                    t_pr = sch.op("dve",
                                  lambda e, tb=tb, si=si, s=s, pj=pj, to=to, qh=qh, kh=kh:
                                  e.tensor_tensor(
                                      tb[:, si * NT:(si + 1) * NT],
                                      qh[:, to * NT:(to + 1) * NT],
                                      kh[:, pj - s: pj - s + NT], ALU.mult),
                                  deps=[t_slab[hh]] + extra, inc=(si == NS - 1))
                t_pr_last[hh] = t_pr
                return t_pr

            def emit_b_stack(it, t_pr):
                t_st = None
                for si in range(NS):
                    extra = [t_scevac.get(it - 1)] if si == 0 else []
                    t_st = sch.op("pe", lambda e, tb=(tmpPA, tmpPB)[it % 2], si=si: e.matmul(
                        scps[:, :], ecols_sb[:, si * NS:(si + 1) * NS],
                        tb[:, si * NT:(si + 1) * NT],
                        start=(si == 0), stop=(si == NS - 1), skip_group_check=True),
                        deps=[t_pr] + extra, inc=(si == NS - 1))
                t_stack[it] = t_st

            def emit_b_rest(it, t_tr):
                nonlocal t_soft_prev, t_pfevac_prev
                hh = it // NTO
                to = it % NTO
                pj = HALO_L + to * NT
                t_rm = None
                for i4 in range(4):
                    t_rm = sch.op("dve", lambda e, i4=i4: e.reduce_max(
                        nmax[:, i4:i4 + 1], tmps[:, i4 * 16:(i4 + 1) * 16],
                        axis=AX.X, negate=True), deps=[t_tr], inc=(i4 == 3))
                t_ex = None
                for i4 in range(4):
                    t_ex = sch.op("act", lambda e, i4=i4: e.activation(
                        expv[:, i4 * 16:(i4 + 1) * 16], tmps[:, i4 * 16:(i4 + 1) * 16],
                        AF.Exp, bias=nmax[:, i4:i4 + 1]), deps=[t_rm], inc=(i4 == 3))
                t_exp_tok[it] = t_ex
                t_rs = None
                for i4 in range(4):
                    t_rs = sch.op("dve", lambda e, i4=i4: e.reduce_sum(
                        esum[:, i4:i4 + 1], expv[:, i4 * 16:(i4 + 1) * 16],
                        axis=AX.X), deps=[t_ex], inc=(i4 == 3))
                t_rc = sch.op("dve", lambda e: e.reciprocal(esum[:, 4:8], esum[:, 0:4]),
                              deps=[t_rs], inc=True)
                t_pn = None
                for i4 in range(4):
                    t_pn = sch.op("dve", lambda e, i4=i4: e.tensor_scalar(
                        ptm[:, i4 * 16:(i4 + 1) * 16], expv[:, i4 * 16:(i4 + 1) * 16],
                        esum[:, 4 + i4:5 + i4], None, ALU.mult),
                        deps=[t_rc], inc=(i4 == 3))
                t_soft_prev = t_pn
                t_pf = None
                for i4 in range(4):
                    extra = [t_pfevac_prev] if i4 == 0 else []
                    t_pf = sch.op("pe", lambda e, i4=i4: e.transpose(
                        pfps[:, i4 * P:(i4 + 1) * P], ptm[:, i4 * 16:(i4 + 1) * 16],
                        id128b_sb[:, :]), deps=[t_pn] + extra, inc=(i4 == 3))
                t_pfe = sch.op("dve", lambda e: e.tensor_copy(pfm[:, :], pfps[:, :]),
                               deps=[t_pf], inc=True)
                t_pfevac_prev = t_pfe
                for si, s in enumerate(SEQ_SHIFTS):
                    pb = (pbcA, pbcB)[si % 2]
                    pr = (prodA, prodB)[si % 2]
                    war_b = t_mult.get((it, si - 2)) if si >= 2 else t_mult.get((it - 1, NS - 2 + si))
                    t_b = sch.op("pe", lambda e, pb=pb, si=si: e.matmul(
                        pb[:, :], erows_sb[:, si * P:(si + 1) * P], pfm[:, :],
                        start=True, stop=True), deps=[t_pfe, war_b], inc=True)
                    war_p = t_add.get((it, si - 2)) if si >= 2 else t_add.get((it - 1, NS - 2 + si))
                    t_mu = sch.op("dve",
                                  lambda e, pb=pb, pr=pr, pj=pj, s=s, vh=vhs[hh % 2]:
                                  e.tensor_tensor(
                                      pr[:, :], vh[:, pj - s: pj - s + NT],
                                      pb[:, :], ALU.mult),
                                  deps=[t_b, war_p], inc=True)
                    t_mult[(it, si)] = t_mu
                    if si == 0:
                        extra = [t_asp.get(hh - 1)]
                        t_a = sch.op("pool", lambda e, pr=pr, to=to: e.tensor_copy(
                            acch[:, to * NT:(to + 1) * NT], pr[:, :]),
                            deps=[t_mu] + extra, inc=True)
                    else:
                        t_a = sch.op("pool", lambda e, pr=pr, to=to: e.tensor_tensor(
                            acch[:, to * NT:(to + 1) * NT],
                            acch[:, to * NT:(to + 1) * NT],
                            pr[:, :], ALU.add), deps=[t_mu], inc=True)
                    t_add[(it, si)] = t_a

            def emit_c_unit(ic, mi, to):
                idx = ic * 16 + mi * NTO + to
                icol = ic * 4 + mi
                hb = (hbfA, hbfB)[ic % 2]
                gp = gpsA
                up = (upsA, upsB)[idx % 2]
                for ci in range(NCH):
                    sch.op("pe", lambda e, gp=gp, mi=mi, to=to, ci=ci: e.matmul(
                        gp[:, :], wgT[:, ci * NT + mi * P: ci * NT + mi * P + P],
                        xno[:, ci * T + to * NT: ci * T + (to + 1) * NT],
                        start=(ci == 0), stop=False, skip_group_check=True),
                        deps=[t_trdone[(ic, "g")], t_cs_done, t_si_tok.get(idx - 1)]
                        if ci == 0 else [])
                t_g = sch.op("pe", lambda e, gp=gp, to=to: e.matmul(
                    gp[:, :], onesrb_sb[:, :], csneg[0:1, to * NT:(to + 1) * NT],
                    start=False, stop=True, skip_group_check=True), deps=[], inc=True)
                t_mm_last[(ic, "g")] = t_g
                for ci in range(NCH):
                    sch.op("pe", lambda e, up=up, mi=mi, to=to, ci=ci: e.matmul(
                        up[:, :], wuT[:, ci * NT + mi * P: ci * NT + mi * P + P],
                        xno[:, ci * T + to * NT: ci * T + (to + 1) * NT],
                        start=(ci == 0), stop=False, skip_group_check=True),
                        deps=[t_trdone[(ic, "u")], t_hh_tok.get(idx - 2)]
                        if ci == 0 else [])
                t_u = sch.op("pe", lambda e, up=up, to=to: e.matmul(
                    up[:, :], onesrb_sb[:, :], csneg[0:1, to * NT:(to + 1) * NT],
                    start=False, stop=True, skip_group_check=True), deps=[], inc=True)
                t_mm_last[(ic, "u")] = t_u
                sl = siluCA
                t_si = sch.op("act", lambda e, gp=gp, sl=sl, icol=icol: e.activation(
                    sl[:, :], gp[:, :], AF.Silu, scale=gsc_sb[:, icol:icol + 1]),
                    deps=[t_g, t_hh_tok.get(idx - 1)], inc=True)
                t_si_tok[idx] = t_si
                extra = [t_hspill[ic - 2]] if (mi == 0 and to == 0 and ic >= 2) else []
                t_hh = sch.op("dve",
                              lambda e, up=up, sl=sl, hb=hb, mi=mi, to=to, icol=icol:
                              e.scalar_tensor_tensor(
                                  hb[:, mi * T + to * NT: mi * T + (to + 1) * NT],
                                  up[:, :], usc_sb[:, icol:icol + 1], sl[:, :],
                                  ALU.mult, ALU.mult),
                              deps=[t_u, t_si] + extra, inc=True)
                t_hh_tok[idx] = t_hh
                t_h[(ic, mi, to)] = t_hh

            def emit_step_loads(step):
                ic = step
                hh = step
                for (w_bf_ap, wTb, tag) in ((gate_bf, wgT, "g"), (up_bf, wuT, "u")):
                    t_trdone[(ic, tag)] = sch.dma("sp", lambda e, w_bf_ap=w_bf_ap, wTb=wTb,
                                                  ic=ic: e.dma_start(
                        out=wTb[:, :].rearrange("p (c w) -> p c w", c=NCH),
                        in_=w_bf_ap[ic * NT:(ic + 1) * NT, :], transpose=True),
                        deps=[t_wcast["gate" if tag == "g" else "up"],
                              t_mm_last.get((ic - 1, tag))] +
                             ([t_xno] if ic == 0 else []))
                kq_war = [t_pr_last.get(hh - 1)]
                v_war = [t_mult[(4 * (hh - 2) + 3, NS - 1)]] if hh >= 2 else []
                t_k = sch.dma("sp", lambda e, hh=hh, kh=khs[hh % 2]: e.dma_start(
                    out=kh[:, :], in_=k_r[:, hh, 0:KE]),
                    deps=kq_war + (tA3_done if hh == 0 else []))
                t_v = sch.dma("sp", lambda e, hh=hh, vh=vhs[hh % 2]: e.dma_start(
                    out=vh[:, :], in_=v_r[:, hh, 0:KE]), deps=v_war)
                t_q = sch.dma("sp", lambda e, hh=hh, qh=qhs[hh % 2]: e.dma_start(
                    out=qh[:, :], in_=qm_r[:, hh, :]), deps=kq_war)
                t_slab[hh] = t_q

            emit_step_loads(0)
            for step in range(16):
                ic = step
                hh = step
                b2 = ic % 2
                hb = (hbfA, hbfB)[b2]
                its = [4 * hh + j for j in range(4)]
                cu = [(mi, to) for mi in range(4) for to in range(NTO)]
                t_pr_pend = {}
                t_tr_pend = {}
                for j4 in range(4):
                    itj = its[j4]
                    emit_c_unit(ic, *cu[4 * j4 + 0])
                    emit_c_unit(ic, *cu[4 * j4 + 1])
                    if j4 > 0:
                        t_tr_pend[its[j4 - 1]] = emit_b_head(its[j4 - 1])
                    t_pr_pend[itj] = emit_b_products(itj)
                    emit_c_unit(ic, *cu[4 * j4 + 2])
                    emit_c_unit(ic, *cu[4 * j4 + 3])
                    emit_b_stack(itj, t_pr_pend[itj])
                    if j4 > 0:
                        emit_b_rest(its[j4 - 1], t_tr_pend[its[j4 - 1]])
                t_tr_pend[its[3]] = emit_b_head(its[3])
                emit_b_rest(its[3], t_tr_pend[its[3]])
                if step < 15:
                    emit_step_loads(step + 1)
                t_hspill[ic] = sch.dma("sp", lambda e, hb=hb, ic=ic: e.dma_start(
                    out=h_r[:, ic * 4:(ic + 1) * 4, :],
                    in_=hb[:, :].rearrange("p (c w) -> p c w", c=4)),
                    deps=[t_h[(ic, 3, NTO - 1)]])
                t_asp[hh] = sch.dma("gq", lambda e, hh=hh: e.dma_start(
                    out=at_r[:, hh, :], in_=acch[:, :]),
                    deps=[t_add[(its[3], NS - 1)]])
            tC_done = [t_hspill[-1], t_hspill[-2], t_asp[15]]
        bc_es.close()
        if stop_after is not None and (stop_after.startswith("C") or stop_after == "B"):
            _finish()
            return nc

        t_out_stores = []
        t_prev_half = list(tC_done)
        t_hsn_war = None
        for H in range(2):
            th0 = H * TH2
            acc_es = ExitStack()
            acc = acc_es.enter_context(nc.sbuf_tensor(f"accD{H}", [P, NCH * TH2], F32))
            d_es = ExitStack()
            wdTs = [d_es.enter_context(nc.sbuf_tensor(f"wdT{H}_{k}", [P, 4 * D], BF16))
                    for k in range(4)]
            hDs = [d_es.enter_context(nc.sbuf_tensor(f"hD{H}_{k}", [P, 4 * TH2], BF16))
                   for k in range(4)]
            opsA = d_es.enter_context(nc.psum_tensor(f"opsA{H}", [P, NT], F32))
            opsB = d_es.enter_context(nc.psum_tensor(f"opsB{H}", [P, NT], F32))
            opsC = d_es.enter_context(nc.psum_tensor(f"opsC{H}", [P, NT], F32))
            opsD = d_es.enter_context(nc.psum_tensor(f"opsD{H}", [P, NT], F32))
            hsumD = d_es.enter_context(nc.psum_tensor(f"hsumD{H}", [4, NT], F32))

            t_trD = {}
            t_hld = {}
            t_evD = {}
            NPAIR = 8
            for p8 in range(NPAIR):
                for k2 in range(2):
                    ic = p8 * 2 + k2
                    b4 = (p8 % 2) * 2 + k2
                    war = [t_evD.get((p8 - 2, NCH - 1, 1))] + (t_prev_half if p8 < 2 else [])
                    t_trD[ic] = sch.dma("sp", lambda e, ic=ic, b4=b4, wdTs=wdTs: e.dma_start(
                        out=wdTs[b4][:, :].rearrange("p (c w) -> p c w", c=4),
                        in_=down_bf[:, ic * NT:(ic + 1) * NT], transpose=True),
                        deps=[t_wcast["down"]] + war)
                    t_hld[ic] = sch.dma("sp", lambda e, ic=ic, b4=b4, hDs=hDs, th0=th0: e.dma_start(
                        out=hDs[b4][:, :].rearrange("p (c w) -> p c w", c=4),
                        in_=h_r[:, ic * 4:(ic + 1) * 4, th0:th0 + TH2]),
                        deps=war)
                    for r4 in range(4):
                        for th in range(2):
                            first = (ic == 0 and r4 == 0 and th == 0)
                            last = (ic == ICH - 1 and r4 == 3 and th == 1)
                            tg = 2 * H + th
                            t_hsum_lastD = sch.op("pe",
                                                  lambda e, b4=b4, r4=r4, th=th, tg=tg,
                                                  first=first, last=last, hDs=hDs, hsumD=hsumD:
                                                  e.matmul(hsumD[:, :],
                                                           e4c_sb[:, tg * 4:(tg + 1) * 4],
                                                           hDs[b4][:, r4 * TH2 + th * NT: r4 * TH2 + (th + 1) * NT],
                                                           start=first, stop=last,
                                                           skip_group_check=True),
                                                  deps=[t_hld[ic]], inc=last)
                            if last:
                                t_hsnD = sch.op("dve", lambda e, hsumD=hsumD: e.tensor_scalar(
                                    hsumneg_sb[:, :], hsumD[:, :], -128.0, None, ALU.mult),
                                    deps=[t_hsum_lastD, t_hsn_war], inc=True)
                for mi in range(NCH):
                    for th in range(2):
                        tg = 2 * H + th
                        par = (mi * 2 + th) % 4
                        ops = (opsA, opsB, opsC, opsD)[par]
                        war = t_evD.get((p8, mi - 2, th)) if mi >= 2 else \
                            t_evD.get((p8 - 1, NCH - 2 + mi, th))
                        t_mmlast = None
                        cnt = 0
                        nmm = 8
                        for k2 in range(2):
                            ic = p8 * 2 + k2
                            b4 = (p8 % 2) * 2 + k2
                            for r4 in range(4):
                                first = (cnt == 0)
                                is_last_real = (cnt == nmm - 1)
                                stop = is_last_real and (p8 != NPAIR - 1)
                                deps = [t_trD[ic], t_hld[ic]]
                                if first:
                                    deps.append(war)
                                t_mmlast = sch.op("pe",
                                                  lambda e, ops=ops, b4=b4, r4=r4, mi=mi, th=th,
                                                  first=first, stop=stop, wdTs=wdTs, hDs=hDs:
                                                  e.matmul(
                                                      ops[:, :],
                                                      wdTs[b4][:, r4 * D + mi * P: r4 * D + mi * P + P],
                                                      hDs[b4][:, r4 * TH2 + th * NT: r4 * TH2 + (th + 1) * NT],
                                                      start=first, stop=stop,
                                                      skip_group_check=True),
                                                  deps=deps,
                                                  inc=(is_last_real and p8 != NPAIR - 1))
                                cnt += 1
                        if p8 == NPAIR - 1:
                            t_mmlast = sch.op("pe", lambda e, ops=ops, tg=tg: e.matmul(
                                ops[:, :], e4r_sb[:, tg * P:(tg + 1) * P],
                                hsumneg_sb[0:4, 0:NT],
                                start=False, stop=True, skip_group_check=True),
                                deps=[t_hsnD], inc=True)
                        if p8 == 0:
                            t_e = sch.op("dve", lambda e, ops=ops, mi=mi, th=th, acc=acc: e.tensor_copy(
                                acc[:, mi * TH2 + th * NT: mi * TH2 + (th + 1) * NT],
                                ops[:, :]), deps=[t_mmlast], inc=True)
                        else:
                            t_e = sch.op("dve", lambda e, ops=ops, mi=mi, th=th, acc=acc: e.tensor_tensor(
                                acc[:, mi * TH2 + th * NT: mi * TH2 + (th + 1) * NT],
                                acc[:, mi * TH2 + th * NT: mi * TH2 + (th + 1) * NT],
                                ops[:, :], ALU.add), deps=[t_mmlast], inc=True)
                        t_evD[(p8, mi, th)] = t_e
            t_D_done = t_evD[(NPAIR - 1, NCH - 1, 1)]
            t_hsn_war = t_D_done
            if dbg:
                t_D_done = sch.dma("sp", lambda e, acc=acc, th0=th0: e.dma_start(
                    out=acc_dbg.rearrange("(c p) t -> p c t", p=P)[:, :, th0:th0 + TH2],
                    in_=acc[:, :].rearrange("p (c w) -> p c w", c=NCH)),
                    deps=[t_evD[(NPAIR - 1, mi, th)] for mi in range(NCH) for th in range(2)])
            d_es.close()

            e_es = ExitStack()
            attnE = e_es.enter_context(nc.sbuf_tensor(f"attnE{H}", [P, NCH * TH2], BF16))
            woT_bufs = [e_es.enter_context(nc.sbuf_tensor(f"woT{H}_{k}", [P, D], BF16))
                        for k in range(4)]
            xE = [e_es.enter_context(nc.sbuf_tensor(f"xE{H}_{k}", [P, NCH * NT], F32))
                  for k in range(2)]
            siluE = [e_es.enter_context(nc.sbuf_tensor(f"siluE{H}_{k}", [P, NT], F32))
                     for k in range(4)]
            sgE = [e_es.enter_context(nc.sbuf_tensor(f"sgE{H}_{k}", [P, NT], F32))
                   for k in range(2)]
            soPS = [e_es.enter_context(nc.psum_tensor(f"so{H}_{k}", [P, NT], F32))
                    for k in range(6)]

            t_attn_ld = sch.dma("sp", lambda e, attnE=attnE, th0=th0: e.dma_start(
                out=attnE[:, :].rearrange("p (c w) -> p c w", c=NCH),
                in_=at_r[:, :, th0:th0 + TH2]), deps=[t_D_done])
            t_x_ld = [None, None]
            for th in range(2):
                t_x_ld[th] = sch.dma("sp", lambda e, th=th, xE=xE, th0=th0: e.dma_start(
                    out=xE[th][:, :].rearrange("p (c w) -> p c w", c=NCH),
                    in_=xT_r[:, :, HALO_L + th0 + th * NT: HALO_L + th0 + (th + 1) * NT]),
                    deps=[t_D_done])
            t_woT = {}
            t_sg_tok = {}
            t_outdone = {}
            for rc in range(NCH):
                wb = woT_bufs[rc % 4]
                t_woT[rc] = sch.dma("sp", lambda e, rc=rc, wb=wb: e.dma_start(
                    out=wb[:, :].rearrange("p (c w) -> p c w", c=NCH),
                    in_=wo_bf[rc * P:(rc + 1) * P, :], transpose=True),
                    deps=[t_wcast["wo"], t_sg_tok.get((rc - 4, 1))] +
                         ([t_D_done] if rc == 0 else []))
                for th in range(2):
                    par = (rc * 2 + th) % 6
                    so = soPS[par]
                    war = t_sg_tok.get((rc - 3, th))
                    t_so = None
                    for ci in range(NCH):
                        deps = [t_woT[rc], t_attn_ld]
                        if ci == 0:
                            deps.append(war)
                        t_so = sch.op("pe", lambda e, so=so, wb=wb, ci=ci, th=th, attnE=attnE: e.matmul(
                            so[:, :], wb[:, ci * P:(ci + 1) * P],
                            attnE[:, ci * TH2 + th * NT: ci * TH2 + (th + 1) * NT],
                            start=(ci == 0), stop=(ci == NCH - 1), skip_group_check=True),
                            deps=deps, inc=(ci == NCH - 1))
                    sil = siluE[(rc * 2 + th) % 4]
                    t_si = sch.op("act", lambda e, rc=rc, th=th, sil=sil, acc=acc: e.activation(
                        sil[:, :],
                        acc[:, rc * TH2 + th * NT: rc * TH2 + (th + 1) * NT],
                        AF.Silu, scale=dsc_sb[:, rc:rc + 1]),
                        deps=[t_evD[(NPAIR - 1, rc, th)], t_sg_tok.get((rc - 2, th))], inc=True)
                    t_sg = sch.op("dve", lambda e, so=so, rc=rc, th=th, sgE=sgE, sil=sil: e.scalar_tensor_tensor(
                        sgE[th][:, :], so[:, :], bo_sb[:, rc:rc + 1], sil[:, :],
                        ALU.add, ALU.mult), deps=[t_so, t_si], inc=True)
                    t_sg_tok[(rc, th)] = t_sg
                    if dbg:
                        t_sgd = sch.dma("sp", lambda e, rc=rc, th=th, sgE=sgE, th0=th0: e.dma_start(
                            out=sg_dbg.rearrange("(c p) t -> p c t", p=P)[
                                :, rc, th0 + th * NT: th0 + (th + 1) * NT],
                            in_=sgE[th][:, :]),
                            deps=[t_sg])
                        t_sg = t_sgd if False else t_sg
                        t_sg_tok[(rc, th)] = (t_sg_tok[(rc, th)], t_sgd)[0]
                    t_o = sch.op("dve", lambda e, rc=rc, th=th, xE=xE, sgE=sgE: e.tensor_tensor(
                        xE[th][:, rc * NT:(rc + 1) * NT], xE[th][:, rc * NT:(rc + 1) * NT],
                        sgE[th][:, :], ALU.add),
                        deps=[t_sg, t_x_ld[th]] + ([t_sgd] if dbg else []), inc=True)
                    t_outdone[(rc, th)] = t_o
            for th in range(2):
                t_st = sch.dma("sp", lambda e, th=th, xE=xE, th0=th0: e.dma_start(
                    out=outT_r[:, :, th0 + th * NT: th0 + (th + 1) * NT],
                    in_=xE[th][:, :].rearrange("p (c w) -> p c w", c=NCH)),
                    deps=[t_outdone[(NCH - 1, th)]])
                t_out_stores.append(t_st)
            e_es.close()
            acc_es.close()
            t_prev_half = [t_out_stores[-1], t_out_stores[-2]]

        sch.op("sp", lambda e: e.nop(), deps=t_out_stores + sch.all_tokens())

        block = es.enter_context(nc.Block())
        sch.emit(block)
    return nc


def kernel(L0_attn_x, L0_attn_wq, L0_attn_bq, L0_attn_wk, L0_attn_bk, L0_attn_wv,
           L0_attn_bv, L0_attn_q_norm_w, L0_attn_k_norm_w, L0_attn_score_mix_w,
           L0_attn_wo, L0_attn_bo, L0_ffn_gate_w, L0_ffn_gate_scale, L0_ffn_up_w,
           L0_ffn_up_scale, L0_ffn_down_w, L0_ffn_down_scale):
    x = np.asarray(L0_attn_x)[0]
    fvec = lambda v, nch: np.ascontiguousarray(np.asarray(v, np.float32).reshape(nch, P).T)
    vecs = np.zeros((P, NV_TOT), np.float32)
    fields = {"bq": fvec(L0_attn_bq, NCH), "bk": fvec(L0_attn_bk, NCH),
              "bv": fvec(L0_attn_bv, NCH), "bo": fvec(L0_attn_bo, NCH),
              "qnw": fvec(L0_attn_q_norm_w, NCH), "knw": fvec(L0_attn_k_norm_w, NCH),
              "dsc": fvec(L0_ffn_down_scale, NCH),
              "gsc": fvec(L0_ffn_gate_scale, I_FFN // P),
              "usc": fvec(L0_ffn_up_scale, I_FFN // P)}
    for nm, arr in fields.items():
        o, w = VOFF[nm]
        vecs[:, o:o + w] = arr
    o, w = VOFF["mixw"]
    vecs[0:1, o:o + 5] = np.asarray(L0_attn_score_mix_w, np.float32)
    shared = {
        "wq": np.ascontiguousarray(L0_attn_wq, np.float32),
        "wk": np.ascontiguousarray(L0_attn_wk, np.float32),
        "wv": np.ascontiguousarray(L0_attn_wv, np.float32),
        "wo": np.ascontiguousarray(L0_attn_wo, np.float32),
        "gate_w": np.ascontiguousarray(L0_ffn_gate_w, np.int32),
        "up_w": np.ascontiguousarray(L0_ffn_up_w, np.int32),
        "down_w": np.ascontiguousarray(L0_ffn_down_w, np.int32),
        "vecs": vecs,
    }
    in_maps = []
    for c in range(NCORES):
        t0 = c * T
        idx = (np.arange(-HALO_L, T + HALO_R) + t0) % L
        m = dict(shared)
        m["xT"] = np.ascontiguousarray(x[idx].T)
        in_maps.append(m)

    nc = build_program()
    res = run_bass_kernel_spmd(nc, in_maps, core_ids=list(range(NCORES)))
    out = np.empty((1, L, D), dtype=np.float32)
    for c in range(NCORES):
        out[0, c * T:(c + 1) * T, :] = np.asarray(res.results[c]["outT"]).T
    return out
